# version 10
# baseline (speedup 1.0000x reference)
import contextlib
import math
import numpy as np
import ml_dtypes
import concourse.bass as bass
import concourse.mybir as mybir
from concourse.bass_utils import run_bass_kernel_spmd

F32 = mybir.dt.float32
BF16 = mybir.dt.bfloat16
AF = mybir.ActivationFunctionType
ALU = mybir.AluOpType
AX = mybir.AxisListType
bfd = ml_dtypes.bfloat16

SEQ, D, TILE, NT = 4096, 1024, 512, 8
NCH = 8
HID, NJ = 2730, 22
JW = [128] * 21 + [42]
HL, HD = 8, 64
HQ, HKV = 16, 4
WIN = 512
DEV_NT = None
_STAGES = ["init", "ln", "shift", "qkvv", "qkv", "qkproc", "norm", "attn", "wo", "ffn",
           "A", "B", "full"]
STAGE_LIMIT = "full"
ATTN_HEADS = None


def _flatten_weights(W):
    parts, offs = [], {}
    pos = 0

    def add(key, arr):
        nonlocal pos
        a = np.ascontiguousarray(arr, dtype=np.float32)
        offs[key] = (pos, a.shape)
        parts.append(a.reshape(-1))
        pos += a.size

    def lhsT_groups(Wm, kk):
        return Wm.reshape(kk, 128, Wm.shape[1]).transpose(1, 0, 2).reshape(128, -1)

    marks = {}
    for l in range(2):
        for m in range(8):
            add(("qkv", l, m), lhsT_groups(W["L0_ffn_wqkv"][l][:, m * 128:(m + 1) * 128], 8))
        for k in range(8):
            add(("qkvv", l, k), W["L0_ffn_wqkv"][l][k * 128:(k + 1) * 128, 1024:1536])
        marks[("conv", "qkv", l)] = pos
        for m in range(8):
            add(("wo", l, m), lhsT_groups(W["L0_ffn_wo"][l][:, m * 128:(m + 1) * 128], 4))
        marks[("conv", "wo", l)] = pos
        for j in range(NJ):
            wj = JW[j]
            add(("w1a", l, j), lhsT_groups(W["L0_ffn_w1"][l][:, j * 128:j * 128 + wj], 8))
            add(("w1g", l, j), lhsT_groups(W["L0_ffn_w1"][l][:, 2730 + j * 128:2730 + j * 128 + wj], 8))
        marks[("conv", "w1", l)] = pos
        for j in range(NJ):
            add(("w2", l, j), W["L0_ffn_w2"][l][j * 128:j * 128 + JW[j], :])
        marks[("conv", "w2", l)] = pos
    for m in range(8):
        add(("Bq", m), lhsT_groups(W["L0_attn_W_qkv"][:, m * 128:(m + 1) * 128], 8))
    for m in range(2):
        add(("Bk", m), lhsT_groups(W["L0_attn_W_qkv"][:, 1024 + m * 128:1024 + (m + 1) * 128], 8))
    for k in range(8):
        add(("Bv", k), W["L0_attn_W_qkv"][k * 128:(k + 1) * 128, 1280:1536])
    marks[("conv", "Bqkv", 0)] = pos
    for m in range(8):
        add(("Bo", m), lhsT_groups(W["L0_attn_W_o"][:, m * 128:(m + 1) * 128], 8))
    marks[("conv", "Bo", 0)] = pos
    return np.concatenate(parts), offs, marks


def _host_tables():
    t = np.arange(SEQ, dtype=np.float64)
    i = np.arange(32, dtype=np.float64)
    inv_freq = 1.0 / 10000.0 ** (2 * i / 64.0)
    freqs = t[:, None] * inv_freq[None, :]
    base = (2 * i + 0.4 * 64) / (1.4 * 64)
    power = (t - SEQ // 2) / float(WIN // 2)
    scale = base[None, :] ** power[:, None]
    cos64 = np.cos(np.concatenate([freqs, freqs], 1))
    sin64 = np.sin(np.concatenate([freqs, freqs], 1))
    sc64 = np.concatenate([scale, scale], 1)
    def tab(x):
        return np.tile(x.T.astype(np.float32), (2, 1))
    T1q = tab(8.0 * cos64 * sc64)
    T2q = tab(8.0 * sin64 * sc64)
    T1k = tab(cos64 / sc64)
    T2k = tab(sin64 / sc64)

    R = np.zeros((128, 128), np.float32)
    for m in range(128):
        d = m % 64
        h64 = m - d
        if d < 32:
            R[h64 + d + 32, m] = -1.0
        else:
            R[h64 + d - 32, m] = 1.0

    kk, qq = np.meshgrid(np.arange(128), np.arange(128), indexing="ij")
    triC = (kk <= qq).astype(np.float32)
    triP = (kk >= qq).astype(np.float32)

    selq = np.zeros((128, 32), np.float32)
    selbc = np.zeros((8, 4 * 128), np.float32)
    for c in range(4):
        for p in range(128):
            h = 2 * c + p // 64
            selq[p, 8 * c + h] = 1.0
            selbc[h, 128 * c + p] = 1.0
    return dict(t1q=T1q, t2q=T2q, t1k=T1k, t2k=T2k,
                rmat=R.astype(bfd), tric=triC.astype(bfd), trip=triP.astype(bfd),
                selq=selq.astype(bfd), selbc=selbc.astype(bfd))


def _host_smalls(W):
    qs = np.zeros((128, 8), np.float32)
    for l in range(2):
        q_ = W["L0_ffn_q_scale"][l].astype(np.float32)
        k_ = W["L0_ffn_k_scale"][l].astype(np.float32)
        qp = np.concatenate([q_[32:], q_[:32]])
        kp = np.concatenate([k_[32:], k_[:32]])
        qs[:, l * 4 + 0] = np.tile(q_, 2)
        qs[:, l * 4 + 1] = np.tile(qp, 2)
        qs[:, l * 4 + 2] = np.tile(k_, 2)
        qs[:, l * 4 + 3] = np.tile(kp, 2)
    b1a = np.zeros((128, 44), np.float32)
    b1g = np.zeros((128, 44), np.float32)
    for l in range(2):
        for j in range(NJ):
            wj = JW[j]
            b1a[:wj, l * 22 + j] = W["L0_ffn_b1"][l][j * 128:j * 128 + wj]
            b1g[:wj, l * 22 + j] = W["L0_ffn_b1"][l][2730 + j * 128:2730 + j * 128 + wj]
    b2c = np.zeros((128, 16), np.float32)
    for l in range(2):
        b2c[:, l * 8:(l + 1) * 8] = W["L0_ffn_b2"][l].reshape(8, 128).T
    gb = np.zeros((2, 2048), np.float32)
    for l in range(2):
        gb[0, l * 1024:(l + 1) * 1024] = W["L0_ffn_ln_g"][l]
        gb[1, l * 1024:(l + 1) * 1024] = W["L0_ffn_ln_b"][l]
    misc = np.zeros((2, 512), np.float32)
    misc[0, :] = 1.0
    return dict(qs4=qs, b1a=b1a, b1g=b1g, b2c=b2c, gbrows=gb, onesrow=misc[0:1, :])



class _Sem:
    __slots__ = ("h", "v")

    def __init__(self, nc, ctx, name):
        self.h = ctx.enter_context(nc.semaphore(name))
        self.v = 0

    def inc(self, instr, n=1):
        instr.then_inc(self.h, n)
        self.v += n
        return self.v


class _Bank:
    __slots__ = ("ap", "pe_tag", "cons")

    def __init__(self, ap):
        self.ap = ap
        self.pe_tag = 0
        self.cons = []


TAPS_DEF = {
    "A_S0": (128, 4096), "A_h0": (128, 4096), "A_qaug": (65, 4096),
    "A_kaug": (65, 4096), "A_oT": (128, 2048), "A_S1": (128, 4096),
    "A_h2n": (128, 4096), "A_S2": (128, 4096), "A_X": (128, 4096),
    "B_ybf": (128, 4096), "B_qB": (128, 4096), "B_oTB": (128, 4096),
    "A_Pf": (128, 1536), "A_Pc": (128, 1536),
    "A_dbg1": (128, 512), "A_dbg2": (1, 2048),
    "B_X": (128, 4096), "A_AG": (128, 11264), "A_vaug": (128, 2080),
}
_TAP_AT_TILE = 0


class _MK:
    def __init__(self, nc, ctx, offs, marks, n_tiles, taps):
        self.nc = nc
        self.n_tiles = n_tiles
        self.taps = dict(taps or {})
        self.TE, self.VE, self.SE = nc.tensor, nc.vector, nc.scalar
        self.GE, self.SY = nc.gpsimd, nc.sync
        self.offs = offs
        self.units = []
        prev = 0
        for k, end in marks.items():
            self.units.append((k, prev, end))
            prev = end
        self.conv_tag = {}

        sem = lambda n: _Sem(nc, ctx, n)
        self.pe, self.ve, self.se, self.gp = sem("pe"), sem("ve"), sem("se"), sem("gp")
        self.wconv, self.wload, self.xload, self.osem = sem("wconv"), sem("wload"), sem("xload"), sem("osem")

        sb = lambda n, s, d: ctx.enter_context(nc.sbuf_tensor("s_" + n, s, d))
        ps = lambda n, s: ctx.enter_context(nc.psum_tensor(n, s, F32))
        self.PBIG = ps("PBIG", [128, 4096])
        self.pb = [_Bank(self.PBIG[:, 512 * i:512 * (i + 1)]) for i in range(8)]
        self.X = sb("X", [128, 4096], F32)
        self.S = sb("S", [128, 4096], F32)
        self.t1s = sb("t1s", [128, 512], F32)
        self.t2s = sb("t2s", [128, 512], F32)
        self.scr1 = sb("scr1", [128, 512], F32)
        self.scr2 = sb("scr2", [128, 512], F32)
        self.statf = sb("statf", [1, 2048], F32)
        self.CARR = sb("CARR", [128, 32], F32)
        self.qs4 = sb("qs4", [128, 8], F32)
        self.b1a = sb("b1a", [128, 44], F32)
        self.b1g = sb("b1g", [128, 44], F32)
        self.b2c = sb("b2c", [128, 16], F32)
        self.kn2 = sb("kn2", [1, 64], F32)
        self.Mprev = sb("Mprev", [64, 32], BF16)
        self.Mu = sb("Mu", [65, 8], BF16)
        self.absq = sb("absq", [65, 1024], BF16)
        self.one64f = sb("one64f", [1, 64], F32)
        self.KB = sb("KB", [128, 8192], BF16)
        self.VB = sb("VB", [128, 8320], BF16)
        self.KLp = sb("KLp", [128, 8192], BF16)
        self.VLp = sb("VLp", [128, 8320], BF16)
        self.ktmp = sb("ktmp", [65, 1024], BF16)
        self.Pfull = sb("Pfull", [128, 2048], BF16)
        self.Pprev = sb("Pprev", [128, 1536], BF16)
        self.Pcur = sb("Pcur", [128, 1536], BF16)
        self.grow = sb("grow", [1, 2048], BF16)
        self.gnb = sb("gnb", [2, 2048], BF16)
        self.statbf = sb("statbf", [2, 1024], BF16)
        self.onesbcol = sb("onesbcol", [128, 1], BF16)
        self.ones64col = sb("ones64col", [64, 1], BF16)
        self.onesb128 = sb("onesb128", [1, 128], BF16)
        self.negb128 = sb("negb128", [1, 128], BF16)
        self.selq = sb("selq", [128, 32], BF16)
        self.selbc = sb("selbc", [8, 512], BF16)
        self.rmat = sb("rmat", [128, 128], BF16)
        self.tric = sb("tric", [128, 128], BF16)
        self.trip = sb("trip", [128, 128], BF16)
        self.invbf = sb("invbf", [8, 1024], BF16)
        self.gelsb = sb("gelsb", [128, 1024], BF16)
        self.sqb = sb("sqb", [128, 1024], BF16)
        self.ring = sb("ring", [128, 10240], BF16)
        self.AR = sb("AR", [128, 20512], BF16)
        self.HBF, self.QKBF, self.QAUG, self.KAUG = 0, 4096, 8192, 12288
        self.VAUGC, self.OTL, self.AGOF = 16384, 18464, 4096
        self.rslot = [{"wtag": 0, "free": 0} for _ in range(10)]
        self.rr = 0
        self.free = {}
        self.ptag = {}
        self.sq_cons = [0, 0]
        self.gel_cons = [0, 0]
        self.ktmp_cons = [0, 0]
        self.pfull_rr = 0
        self.tab_cons_ve = 0
        self.tab_cons_gp = 0
        self.kaug_row64 = 0
        self.inv_cons = 0
        self.scr_cons = [0, 0]
        self.absq_cons = [0, 0]

    def unit_of(self, key):
        off = self.offs[key][0]
        for k, s, e in self.units:
            if s <= off < e:
                return k
        raise KeyError(key)

    def bank_acq(self, b):
        for (sem, tag) in b.cons:
            self.TE.wait_ge(sem.h, tag)
        b.cons = []

    def bank_rd(self, b, eng):
        eng.wait_ge(self.pe.h, b.pe_tag)

    def ring_use(self, key, wbf):
        i = self.rr
        self.rr = (self.rr + 1) % len(self.rslot)
        sl = self.rslot[i]
        if sl["free"] > 0:
            self.SY.wait_ge(self.pe.h, sl["free"])
        ct = self.conv_tag.get(self.unit_of(key))
        if ct:
            self.SY.wait_ge(self.wconv.h, ct)
        off, (P, C) = self.offs[key]
        src = wbf[off:off + P * C].rearrange("(p c) -> p c", p=P)
        dst = self.ring[0:P, 1024 * i:1024 * i + C]
        self.wload.inc(self.SY.dma_start(dst, src), 16)
        sl["wtag"] = self.wload.v
        return i, self.ring[0:P, 1024 * i:1024 * i + 1024], sl["wtag"]

    def ring_done(self, i):
        self.rslot[i]["free"] = self.pe.v

    def tap(self, name, ap):
        if name not in self.taps:
            return
        for s in (self.pe, self.ve, self.se, self.gp):
            if s.v:
                self.GE.wait_ge(s.h, s.v)
        self.osem.inc(self.GE.dma_start(self.taps[name][:, :], ap), 16)
        del self.taps[name]

    def emit_init(self, io):
        TE, VE, SE, GE, SY = self.TE, self.VE, self.SE, self.GE, self.SY
        for (dst, srcname) in (
            (self.onesbcol[:, :], "col_ones128"),
            (self.ones64col[:, :], "col_ones64"),
            (self.onesb128[:, :], "row_ones128"),
            (self.negb128[:, :], "row_neg128"),
            (self.grow[:, :], "gbrow_g"),
            (self.gnb[1:2, :], "gbrow_b"),
            (self.statbf[1:2, 0:512], "onesrow"),
        ):
            self.wconv.inc(GE.dma_start(dst, io[srcname][:, :]), 16)
        cvt = self.wconv.v
        for (k, s, e) in self.units:
            src = io["wflat"][s:e]
            dst = io["wbf"][s:e]
            self.conv_tag[k] = self.wconv.inc(GE.dma_start(dst, src), 16)
        for name, dstap in (
            ("selq", self.selq[:, :]), ("selbc", self.selbc[:, :]),
            ("rmat", self.rmat[:, :]), ("tric", self.tric[:, :]), ("trip", self.trip[:, :]),
        ):
            self.xload.inc(SE.dma_start(dstap, io[name][:, :]), 16)
        for name, dstap in (
            ("qs4", self.qs4[:, :]), ("b1a", self.b1a[:, :]), ("b1g", self.b1g[:, :]),
            ("b2c", self.b2c[:, :]), ("row_ones64", self.one64f[:, :]),
        ):
            self.xload.inc(SE.dma_start(dstap, io[name][:, :]), 16)
        GE.memset(self.CARR[:, :], 0.0)
        GE.memset(self.ktmp[64:65, :], -1.0)
        GE.memset(self.Pfull[:, :], 0.0)
        GE.memset(self.Pprev[:, :], 0.0)
        GE.memset(self.Pcur[:, :], 0.0)
        GE.memset(self.kn2[:, :], 0.0)
        GE.memset(self.Mu[64:65, :], -35.0)
        GE.memset(self.absq[64:65, :], 1.0)
        vb4 = self.VB[:, :].rearrange("p (c d) -> p c d", d=65)
        GE.memset(vb4[:, :, 64:65], 1.0)
        vl4 = self.VLp[:, :].rearrange("p (c d) -> p c d", d=65)
        GE.memset(vl4[:, :, 64:65], 1.0)
        va4 = self.AR[:, self.VAUGC:self.VAUGC + 2080].rearrange("p (c d) -> p c d", d=65)
        self.gp.inc(GE.memset(va4[:, :, 64:65], 1.0))
        VE.wait_ge(self.wconv.h, cvt)
        VE.wait_ge(self.xload.h, self.xload.v)
        self.ve.inc(VE.tensor_scalar(self.gnb[0:1, :], self.grow[0:1, :], -1.0, None, ALU.mult))
        TE.wait_ge(self.xload.h, self.xload.v)
        TE.wait_ge(self.wconv.h, cvt)
        TE.wait_ge(self.gp.h, self.gp.v)
        TE.wait_ge(self.ve.h, self.ve.v)
        SE.wait_ge(self.xload.h, self.xload.v)
        SE.wait_ge(self.wconv.h, cvt)
        SE.wait_ge(self.gp.h, self.gp.v)
        GE.wait_ge(self.xload.h, self.xload.v)

    def layernorm(self, src_getter, out_getter, l, out_f32, xbf_off=None,
                  src_tags=None):
        TE, VE, SE, GE = self.TE, self.VE, self.SE, self.GE
        bsum, bsq = self.pb[6], self.pb[7]
        self.bank_acq(bsum)
        self.bank_acq(bsq)
        sq_rot = self.sqb
        sv0 = self.ve.v
        if src_tags is None:
            SE.wait_ge(self.ve.h, sv0)
        for c in range(NCH):
            src = src_getter(c)
            if src_tags is not None:
                SE.wait_ge(self.ve.h, src_tags[c])
            if xbf_off is not None:
                xb = self.AR[:, xbf_off + 512 * c: xbf_off + 512 * (c + 1)]
                tve = self.ve.inc(VE.tensor_copy(xb, src))
            else:
                xb = src
                tve = None
            sl = c % 2
            if self.sq_cons[sl]:
                SE.wait_ge(self.pe.h, self.sq_cons[sl])
            tse = self.se.inc(SE.activation(sq_rot[:, 512 * sl:512 * (sl + 1)], src, AF.Square))
            if tve is not None:
                TE.wait_ge(self.ve.h, tve)
            TE.wait_ge(self.se.h, tse)
            mm1 = TE.matmul(bsum.ap[0:1, :], self.onesbcol[:, :], xb,
                            start=(c == 0), stop=(c == NCH - 1))
            if c == NCH - 1:
                bsum.pe_tag = self.pe.inc(mm1)
            mm2 = TE.matmul(bsq.ap[0:1, :], self.onesbcol[:, :],
                            sq_rot[:, 512 * sl:512 * (sl + 1)],
                            start=(c == 0), stop=(c == NCH - 1))
            self.sq_cons[sl] = self.pe.inc(mm2)
            if c == NCH - 1:
                bsq.pe_tag = self.pe.v
        return self.ln_finish(src_getter, out_getter, l)

    def ln_finish(self, src_getter, out_getter, l, src_tags=None):
        TE, VE, SE, GE = self.TE, self.VE, self.SE, self.GE
        bsum, bsq = self.pb[6], self.pb[7]
        sf, sbf = self.statf, self.statbf
        self.bank_rd(bsum, VE)
        VE.tensor_scalar(sf[0:1, 0:512], bsum.ap[0:1, :], 1.0 / 1024.0, None, ALU.mult)
        VE.tensor_tensor(sf[0:1, 1536:2048], sf[0:1, 0:512], sf[0:1, 0:512], ALU.mult)
        VE.scalar_tensor_tensor(sf[0:1, 512:1024], bsq.ap[0:1, :], 1.0 / 1024.0,
                                sf[0:1, 1536:2048], ALU.mult, ALU.subtract)
        t1 = self.ve.inc(VE.tensor_scalar(sf[0:1, 512:1024], sf[0:1, 512:1024],
                                          1.0, 1e-5, ALU.mult, ALU.add))
        bsum.cons.append((self.ve, t1))
        bsq.cons.append((self.ve, t1))
        SE.wait_ge(self.ve.h, t1)
        SE.activation(sf[0:1, 1024:1536], sf[0:1, 512:1024], AF.Ln)
        t2 = self.se.inc(SE.activation(sbf[0:1, 512:1024], sf[0:1, 1024:1536],
                                       AF.Exp, scale=-0.5))
        VE.wait_ge(self.se.h, t2)
        t3 = self.ve.inc(VE.tensor_tensor(sbf[0:1, 0:512], sf[0:1, 0:512],
                                          sbf[0:1, 512:1024], ALU.mult))
        out_tags = []
        for c in range(NCH):
            bA = self.pb[4 + 2 * (c % 2)]
            bB = self.pb[5 + 2 * (c % 2)]
            self.bank_acq(bA)
            self.bank_acq(bB)
            if l is None:
                lA = self.onesb128[0:1, :]
                lB = self.negb128[0:1, :]
            else:
                lA = self.grow[0:1, 1024 * l + 128 * c:1024 * l + 128 * (c + 1)]
                lB = self.gnb[0:2, 1024 * l + 128 * c:1024 * l + 128 * (c + 1)]
            if c == 0:
                TE.wait_ge(self.ve.h, t3)
                TE.wait_ge(self.se.h, t2)
            mmA = TE.matmul(bA.ap[:, :], lA, sbf[0:1, 512:1024], start=True, stop=True)
            bA.pe_tag = self.pe.inc(mmA)
            if l is None:
                mmB = TE.matmul(bB.ap[:, :], lB, sbf[0:1, 0:512], start=True, stop=True)
            else:
                mmB = TE.matmul(bB.ap[:, :], lB, sbf[0:2, 0:512], start=True, stop=True)
            bB.pe_tag = self.pe.inc(mmB)
            src = src_getter(c)
            if src_tags is not None:
                VE.wait_ge(self.ve.h, src_tags[c])
            out = out_getter(c)
            self.bank_rd(bA, VE)
            self.bank_rd(bB, VE)
            VE.tensor_tensor(self.scr2[:, :], src, bA.ap[:, :], ALU.mult)
            tv = self.ve.inc(VE.tensor_tensor(out, self.scr2[:, :], bB.ap[:, :], ALU.add))
            bA.cons.append((self.ve, tv))
            bB.cons.append((self.ve, tv))
            out_tags.append(tv)
        return out_tags


    def shift_cast(self, inst, site, src_getter, dst_off):
        VE, GE = self.VE, self.GE
        base = 4 * (2 * inst + site)
        tags = []
        for c in range(NCH):
            src = src_getter(c)
            dst = self.AR[:, dst_off + 512 * c: dst_off + 512 * (c + 1)]
            if c < 4:
                tags.append(self.ve.inc(VE.tensor_copy(dst, src)))
            else:
                cc = base + (c - 4)
                self.ve.inc(VE.tensor_copy(dst[:, 1:512], src[:, 0:511]))
                tags.append(self.ve.inc(VE.tensor_copy(dst[:, 0:1], self.CARR[:, cc:cc + 1])))
                self.ve.inc(VE.tensor_copy(self.CARR[:, cc:cc + 1], src[:, 511:512]))
        return tags

    def local_qkv(self, t, inst, wbf, v_only=False, h_tags=None):
        TE, VE, SE = self.TE, self.VE, self.SE
        l = inst % 2
        hb = lambda k: self.AR[:, self.HBF + 512 * k:self.HBF + 512 * (k + 1)]
        bq_sum, bk_sum = self.pb[6], self.pb[7]
        self.bank_acq(bq_sum)
        self.bank_acq(bk_sum)
        for m in range(8):
            b = self.pb[m % 4]
            self.bank_acq(b)
            si, slot, wt = self.ring_use(("qkv", l, m), wbf)
            TE.wait_ge(self.wload.h, wt)
            for k in range(NCH):
                if h_tags is not None and m == 0:
                    TE.wait_ge(self.ve.h, h_tags[k])
                mm = TE.matmul(b.ap[:, :], slot[0:128, 128 * k:128 * (k + 1)], hb(k),
                               start=(k == 0), stop=(k == NCH - 1))
            b.pe_tag = self.pe.inc(mm)
            self.ring_done(si)
            self.bank_rd(b, VE)
            dst = self.AR[:, self.QKBF + 512 * m:self.QKBF + 512 * (m + 1)]
            tv = self.ve.inc(VE.tensor_copy(dst, b.ap[:, :]))
            b.cons.append((self.ve, tv))
            sl = m % 2
            if self.sq_cons[sl]:
                SE.wait_ge(self.pe.h, self.sq_cons[sl])
            SE.wait_ge(self.ve.h, tv)
            ts = self.se.inc(SE.activation(self.sqb[:, 512 * sl:512 * (sl + 1)],
                                           dst, AF.Square))
            TE.wait_ge(self.se.h, ts)
            c4 = m % 4
            tgt = bq_sum if m < 4 else bk_sum
            mm2 = TE.matmul(tgt.ap[0:8, :], self.selq[:, 8 * c4:8 * (c4 + 1)],
                            self.sqb[:, 512 * sl:512 * (sl + 1)],
                            start=(c4 == 0), stop=(c4 == 3))
            self.sq_cons[sl] = self.pe.inc(mm2)
            if c4 == 3:
                tgt.pe_tag = self.pe.v
        self.qkbf_ve = self.ve.v
        self._invnorm_extract()
        for sub in range(4):
            self.bank_acq(self.pb[0 + sub])
        for k in range(NCH):
            si, slot, wt = self.ring_use(("qkvv", l, k), wbf)
            TE.wait_ge(self.wload.h, wt)
            if h_tags is not None:
                TE.wait_ge(self.ve.h, h_tags[k])
            for sub in range(4):
                mm = TE.matmul(self.pb[0 + sub].ap[:, :],
                               hb(k)[:, 128 * sub:128 * (sub + 1)], slot[0:128, 0:512],
                               start=(k == 0), stop=(k == NCH - 1))
                if k == NCH - 1:
                    self.pb[0 + sub].pe_tag = self.pe.inc(mm)
            if k < NCH - 1:
                self.pe.inc(mm)
            self.ring_done(si)
        for sub in range(4):
            b = self.pb[0 + sub]
            self.bank_rd(b, VE)
            dst = self.AR[:, self.VAUGC + 520 * sub:self.VAUGC + 520 * (sub + 1)]
            dst4 = dst.rearrange("p (g d) -> p g d", d=65)[:, :, 0:64]
            src4 = b.ap[:, :].rearrange("p (g d) -> p g d", d=64)
            tv = self.ve.inc(VE.tensor_copy(dst4, src4))
            b.cons.append((self.ve, tv))
        self.vaug_ve = self.ve.v
        if v_only:
            return
        self.free["HBF"] = self.pe.v


    def _invnorm_extract(self):
        VE, SE = self.VE, self.SE
        for bs, io_, rb in ((self.pb[7], 512, 0), (self.pb[6], 0, 32)):
            self.bank_rd(bs, VE)
            tv = self.ve.inc(VE.tensor_scalar(self.scr1[rb:rb + 8, :], bs.ap[0:8, :],
                                              1e-24, None, ALU.max))
            bs.cons.append((self.ve, tv))
            SE.wait_ge(self.ve.h, tv)
            SE.activation(self.scr2[rb:rb + 8, :], self.scr1[rb:rb + 8, :], AF.Ln)
            self.invbf_tse = self.se.inc(
                SE.activation(self.invbf[0:8, io_:io_ + 512],
                              self.scr2[rb:rb + 8, :], AF.Exp, scale=-0.5))

    def local_qk_process(self, t, inst, side, io):
        TE, VE, SE, GE, SY = self.TE, self.VE, self.SE, self.GE, self.SY
        l = inst % 2
        bsum = self.pb[6] if side == 0 else self.pb[7]
        coff = 0 if side == 0 else 2048
        aoff = self.QAUG if side == 0 else self.KAUG
        ioff = 0 if side == 0 else 512
        tq1 = io["t1q"] if side == 0 else io["t1k"]
        tq2 = io["t2q"] if side == 0 else io["t2k"]
        if self.tab_cons_ve:
            SY.wait_ge(self.ve.h, self.tab_cons_ve)
        if self.tab_cons_gp:
            SY.wait_ge(self.gp.h, self.tab_cons_gp)
        w1 = self.xload.inc(SY.dma_start(self.t1s[:, :], tq1[:, 512 * t:512 * (t + 1)]), 16)
        w2 = self.xload.inc(SY.dma_start(self.t2s[:, :], tq2[:, 512 * t:512 * (t + 1)]), 16)
        VE.wait_ge(self.xload.h, w2)
        qcol = self.qs4[:, 4 * l + 2 * side:4 * l + 2 * side + 1]
        qpcol = self.qs4[:, 4 * l + 2 * side + 1:4 * l + 2 * side + 2]
        VE.tensor_scalar(self.t1s[:, :], self.t1s[:, :], qcol, None, ALU.mult)
        tab_fold = self.ve.inc(VE.tensor_scalar(self.t2s[:, :], self.t2s[:, :],
                                                qpcol, None, ALU.mult))
        tse = self.invbf_tse
        if side == 1:
            GE.wait_ge(self.pe.h, max(self.free.values(), default=0))
            self.kaug_row64 = self.gp.inc(
                GE.memset(self.AR[64:65, self.KAUG:self.KAUG + 4096], -1.0))
        bBs, bRs = [], []
        for c in range(4):
            bB = self.pb[4 + c]
            self.bank_acq(bB)
            TE.wait_ge(self.se.h, tse)
            mmB = TE.matmul(bB.ap[:, :], self.selbc[:, 128 * c:128 * (c + 1)],
                            self.invbf[0:8, ioff:ioff + 512], start=True, stop=True)
            bB.pe_tag = self.pe.inc(mmB)
            bBs.append(bB)
        for c in range(4):
            bR = self.pb[c]
            self.bank_acq(bR)
            TE.wait_ge(self.ve.h, self.qkbf_ve)
            src_c = self.AR[:, self.QKBF + coff + 512 * c:self.QKBF + coff + 512 * (c + 1)]
            mmR = TE.matmul(bR.ap[:, :], self.rmat[:, :], src_c, start=True, stop=True)
            bR.pe_tag = self.pe.inc(mmR)
            bRs.append(bR)
        last_dst = 0
        aug_tags = []
        for c in range(4):
            bR, bB = bRs[c], bBs[c]
            src_c = self.AR[:, self.QKBF + coff + 512 * c:self.QKBF + coff + 512 * (c + 1)]
            VE.tensor_tensor(self.scr1[:, :], src_c, self.t1s[:, :], ALU.mult)
            self.bank_rd(bR, VE)
            tv3 = self.ve.inc(VE.tensor_tensor(self.scr2[:, :], bR.ap[:, :],
                                               self.t2s[:, :], ALU.mult))
            bR.cons.append((self.ve, tv3))
            VE.tensor_tensor(self.scr1[:, :], self.scr1[:, :], self.scr2[:, :], ALU.add)
            self.bank_rd(bB, VE)
            for hh in range(2):
                h = 2 * c + hh
                dst = self.AR[0:64, aoff + 512 * h:aoff + 512 * (h + 1)]
                last_dst = self.ve.inc(
                    VE.tensor_tensor(dst, self.scr1[64 * hh:64 * hh + 64, :],
                                     bB.ap[64 * hh:64 * hh + 64, :], ALU.mult))
                aug_tags.append(last_dst)
            bB.cons.append((self.ve, last_dst))
        self.free["QKBF" if side == 0 else "QKBF_k"] = self.pe.v
        self.tab_cons_ve = self.ve.v
        self.tab_cons_gp = self.gp.v
        return aug_tags

    def local_norm_rows(self, t, inst):
        VE = self.VE
        for h in range(HL):
            src = self.AR[0:64, self.KAUG + 512 * h:self.KAUG + 512 * (h + 1)]
            VE.reduce_max(self.scr2[0:64, h:h + 1], src, axis=AX.X,
                          apply_absolute_value=True)
        VE.drain()
        for h in range(HL):
            mp = self.Mprev[0:64, 8 * inst + h:8 * inst + h + 1]
            if t == 0:
                VE.tensor_copy(self.Mu[0:64, h:h + 1], self.scr2[0:64, h:h + 1])
            else:
                VE.tensor_tensor(self.Mu[0:64, h:h + 1], self.scr2[0:64, h:h + 1],
                                 mp, ALU.max)
            self.ve.inc(VE.tensor_copy(mp, self.scr2[0:64, h:h + 1]))
        self.mu_ve = self.ve.v
        return self.ve.v

    def _c_pipe(self, h, qv0):
        TE, SE = self.TE, self.SE
        sl = h % 2
        if self.absq_cons[sl]:
            SE.wait_ge(self.pe.h, self.absq_cons[sl])
        SE.wait_ge(self.ve.h, qv0)
        ts = self.se.inc(SE.activation(self.absq[0:64, 512 * sl:512 * (sl + 1)],
                                       self.AR[0:64, self.QAUG + 512 * h:
                                               self.QAUG + 512 * (h + 1)],
                                       AF.Abs))
        br = self.pb[6 + (h % 2)]
        self.bank_acq(br)
        TE.wait_ge(self.se.h, ts)
        TE.wait_ge(self.ve.h, self.mu_ve)
        mm = TE.matmul(br.ap[0:1, :], self.Mu[0:65, h:h + 1],
                       self.absq[0:65, 512 * sl:512 * (sl + 1)], start=True, stop=True)
        br.pe_tag = self.pe.inc(mm)
        self.absq_cons[sl] = self.pe.v
        self.bank_rd(br, SE)
        t2 = self.se.inc(SE.activation(self.AR[64:65, self.QAUG + 512 * h:
                                               self.QAUG + 512 * (h + 1)],
                                       br.ap[0:1, :], AF.Copy))
        br.cons.append((self.se, t2))
        return t2

    def _attn_normalize(self, bO, h, dst, after_ve, bb_base=4):
        TE, VE, SE = self.TE, self.VE, self.SE
        self.bank_rd(bO, VE)
        VE.tensor_copy(self.statf[0:1, 0:512], bO.ap[64:65, :])
        tvs = self.ve.inc(VE.tensor_scalar(self.statf[0:1, 0:512],
                                           self.statf[0:1, 0:512],
                                           1e-18, None, ALU.max))
        bO.cons.append((self.ve, tvs))
        if self.inv_cons:
            SE.wait_ge(self.pe.h, self.inv_cons)
        SE.wait_ge(self.ve.h, tvs)
        SE.activation(self.statf[0:1, 1024:1536], self.statf[0:1, 0:512], AF.Ln)
        ti = self.se.inc(SE.activation(self.invbf[0:1, 0:512],
                                       self.statf[0:1, 1024:1536], AF.Exp,
                                       scale=-1.0))
        bB = self.pb[bb_base + (h % 2)]
        self.bank_acq(bB)
        TE.wait_ge(self.se.h, ti)
        mmb = TE.matmul(bB.ap[0:64, :], self.onesb128[0:1, 0:64],
                        self.invbf[0:1, 0:512], start=True, stop=True)
        bB.pe_tag = self.pe.inc(mmb)
        self.inv_cons = bB.pe_tag
        scr = self.scr1 if h % 2 == 0 else self.scr2
        SE.wait_ge(self.pe.h, self.inv_cons)
        SE.wait_ge(self.ve.h, max(self.scr_cons[h % 2], after_ve))
        SE.wait_ge(self.gp.h, self.norm_gp)
        ts = self.se.inc(SE.activation(scr[0:64, :], bB.ap[0:64, :], AF.Copy))
        bB.cons.append((self.se, ts))
        VE.wait_ge(self.se.h, ts)
        tv2 = self.ve.inc(VE.tensor_tensor(dst, bO.ap[0:64, :], scr[0:64, :], ALU.mult))
        bO.cons.append((self.ve, tv2))
        bO.cons.append((self.se, ts))
        self.scr_cons[h % 2] = tv2

    def local_attn(self, t, inst, q_tags, k_tags):
        TE, VE, SE, GE = self.TE, self.VE, self.SE, self.GE
        qv = self.ve.v
        sv = self.se.v
        gv0 = self.gp.v
        GE.wait_ge(self.ve.h, qv)
        TE.wait_ge(self.gp.h, self.kaug_row64)
        self.norm_gp = gv0
        pend = None
        nh = ATTN_HEADS or HL
        ctags = {0: self._c_pipe(0, q_tags[0])}
        nprev = 4 if t > 0 else 0
        for h in range(nh):
            if h + 1 < nh:
                ctags[h + 1] = self._c_pipe(h + 1, q_tags[h + 1])
            tb = h % 2
            if t > 0:
                if self.ktmp_cons[tb]:
                    GE.wait_ge(self.pe.h, self.ktmp_cons[tb])
                src = self.KLp[64 * (h % 2):64 * (h % 2) + 64,
                               2048 * inst + 512 * (h // 2):2048 * inst + 512 * (h // 2) + 512]
                tg = self.gp.inc(GE.tensor_copy(self.ktmp[0:64, 512 * tb:512 * (tb + 1)], src))
            bO = self.pb[2 + (h % 2)]
            self.bank_acq(bO)
            TE.wait_ge(self.ve.h, max(q_tags[h], k_tags[h]))
            TE.wait_ge(self.se.h, ctags[h])
            pv_started = False
            plist = []
            for kc in range(nprev + 4):
                cur = kc >= nprev
                kci = kc - nprev if cur else kc
                bS = self.pb[kc % 2]
                self.bank_acq(bS)
                if cur:
                    lhs = self.AR[0:65, self.KAUG + 512 * h + 128 * kci:
                                  self.KAUG + 512 * h + 128 * (kci + 1)]
                    lo, hi = 128 * kci, 512
                else:
                    TE.wait_ge(self.gp.h, tg)
                    lhs = self.ktmp[0:65, 512 * tb + 128 * kci:512 * tb + 128 * (kci + 1)]
                    lo, hi = 0, 128 * (kci + 1)
                rhs = self.AR[0:65, self.QAUG + 512 * h + lo:self.QAUG + 512 * h + hi]
                mm = TE.matmul(bS.ap[:, lo:hi], lhs, rhs, start=True, stop=True)
                bS.pe_tag = self.pe.inc(mm)
                if t > 0 and not cur:
                    self.ktmp_cons[tb] = self.pe.v
                if cur and kci == 0:
                    pbuf, pz = self._pfull(), None
                elif cur:
                    pbuf, pz = self.Pcur[:, 512 * (kci - 1):512 * kci], None
                elif kci == 3:
                    pbuf, pz = self._pfull(), None
                else:
                    pbuf, pz = self.Pprev[:, 512 * kci:512 * (kci + 1)], None
                key = pbuf.offset
                if key in self.ptag and self.ptag[key]:
                    SE.wait_ge(self.pe.h, self.ptag[key])
                self.bank_rd(bS, SE)
                te = self.se.inc(SE.activation(pbuf[:, lo:hi], bS.ap[:, lo:hi], AF.Exp))
                bS.cons.append((self.se, te))
                tri = self.tric if cur else self.trip
                blk = 128 * kci
                GE.wait_ge(self.se.h, te)
                tg2 = self.gp.inc(GE.tensor_tensor(pbuf[:, blk:blk + 128],
                                                   pbuf[:, blk:blk + 128], tri[:, :], ALU.mult))
                plist.append((pbuf, key, te, tg2, kc))
            for (pbuf, key, te, tg2, kc) in plist:
                cur = kc >= nprev
                kci = kc - nprev if cur else kc
                if cur:
                    vsl = self.AR[:, self.VAUGC + 520 * kci + 65 * h:
                                  self.VAUGC + 520 * kci + 65 * h + 65]
                else:
                    vsl = self.VLp[:, 2080 * inst + 520 * kci + 65 * h:
                                   2080 * inst + 520 * kci + 65 * h + 65]
                TE.wait_ge(self.gp.h, tg2)
                mm = TE.matmul(bO.ap[0:65, :], vsl, pbuf[:, :],
                               start=(not pv_started), stop=(kc == nprev + 3))
                pv_started = True
                if kc == nprev + 3:
                    bO.pe_tag = self.pe.inc(mm)
            for (pbuf, key, te, tg2, kc) in plist:
                self.ptag[key] = bO.pe_tag
            if pend is not None:
                pbO, ph = pend
                pdst = self.AR[64 * (ph % 2):64 * (ph % 2) + 64,
                               self.OTL + 512 * (ph // 2):self.OTL + 512 * (ph // 2) + 512]
                self._attn_normalize(pbO, ph, pdst, qv)
            pend = (bO, h)
        if pend is not None:
            pbO, ph = pend
            pdst = self.AR[64 * (ph % 2):64 * (ph % 2) + 64,
                           self.OTL + 512 * (ph // 2):self.OTL + 512 * (ph // 2) + 512]
            self._attn_normalize(pbO, ph, pdst, qv)
        for h in range(HL):
            src = self.AR[0:64, self.KAUG + 512 * h:self.KAUG + 512 * (h + 1)]
            dst = self.KLp[64 * (h % 2):64 * (h % 2) + 64,
                           2048 * inst + 512 * (h // 2):2048 * inst + 512 * (h // 2) + 512]
            self.ve.inc(VE.tensor_copy(dst, src))
        tvv = self.ve.inc(VE.tensor_copy(self.VLp[:, 2080 * inst:2080 * (inst + 1)],
                                         self.AR[:, self.VAUGC:self.VAUGC + 2080]))
        self.klp_ve = self.ve.v
        return self.ve.v

    def _pfull(self):
        i = self.pfull_rr % 4
        self.pfull_rr += 1
        return self.Pfull[:, 512 * i:512 * (i + 1)]

    def local_wo(self, t, inst, wbf):
        TE, VE, SE = self.TE, self.VE, self.SE
        l = inst % 2
        otv = self.ve.v
        bsum, bsq = self.pb[6], self.pb[7]
        self.bank_acq(bsum)
        self.bank_acq(bsq)
        TE.wait_ge(self.ve.h, otv)
        VE.wait_ge(self.pe.h, max(self.free.values(), default=0))
        base = 4 * (2 * inst + 1)
        for m in range(NCH):
            b = self.pb[m % 4]
            self.bank_acq(b)
            si, slot, wt = self.ring_use(("wo", l, m), wbf)
            TE.wait_ge(self.wload.h, wt)
            for k in range(4):
                mm = TE.matmul(b.ap[:, :], slot[0:128, 128 * k:128 * (k + 1)],
                               self.AR[:, self.OTL + 512 * k:self.OTL + 512 * (k + 1)],
                               start=(k == 0), stop=(k == 3))
            b.pe_tag = self.pe.inc(mm)
            self.ring_done(si)
            self.bank_rd(b, VE)
            sc = self.S[:, 512 * m:512 * (m + 1)]
            tv = self.ve.inc(VE.tensor_tensor(sc, sc, b.ap[:, :], ALU.add))
            b.cons.append((self.ve, tv))
            dst = self.AR[:, self.HBF + 512 * m:self.HBF + 512 * (m + 1)]
            if m < 4:
                tagm = self.ve.inc(VE.tensor_copy(dst, sc))
            else:
                cc = base + (m - 4)
                VE.tensor_copy(dst[:, 1:512], sc[:, 0:511])
                tagm = self.ve.inc(VE.tensor_copy(dst[:, 0:1], self.CARR[:, cc:cc + 1]))
                self.ve.inc(VE.tensor_copy(self.CARR[:, cc:cc + 1], sc[:, 511:512]))
            sl = m % 2
            if self.sq_cons[sl]:
                SE.wait_ge(self.pe.h, self.sq_cons[sl])
            SE.wait_ge(self.ve.h, tagm)
            tse = self.se.inc(SE.activation(self.sqb[:, 512 * sl:512 * (sl + 1)],
                                            dst, AF.Square))
            TE.wait_ge(self.ve.h, tagm)
            mm1 = TE.matmul(bsum.ap[0:1, :], self.onesbcol[:, :], dst,
                            start=(m == 0), stop=(m == NCH - 1))
            if m == NCH - 1:
                bsum.pe_tag = self.pe.inc(mm1)
            TE.wait_ge(self.se.h, tse)
            mm2 = TE.matmul(bsq.ap[0:1, :], self.onesbcol[:, :],
                            self.sqb[:, 512 * sl:512 * (sl + 1)],
                            start=(m == 0), stop=(m == NCH - 1))
            self.sq_cons[sl] = self.pe.inc(mm2)
            if m == NCH - 1:
                bsq.pe_tag = self.pe.v
        self.free["OTL"] = self.pe.v
        return self.ve.v

    def local_ffn(self, t, inst, wbf, next_shift=False):
        TE, VE, SE, GE = self.TE, self.VE, self.SE, self.GE
        l = inst % 2
        hbg = lambda c: self.AR[:, self.HBF + 512 * c:self.HBF + 512 * (c + 1)]
        h2_tags = self.ln_finish(hbg, hbg, l)
        hv = self.ve.v
        for j in range(NJ):
            wj = JW[j]
            bA = self.pb[2 * (j % 2)]
            bG = self.pb[1 + 2 * (j % 2)]
            self.bank_acq(bA)
            self.bank_acq(bG)
            sa, slota, wta = self.ring_use(("w1a", l, j), wbf)
            sg, slotg, wtg = self.ring_use(("w1g", l, j), wbf)
            TE.wait_ge(self.wload.h, wtg)
            for k in range(NCH):
                if j == 0:
                    TE.wait_ge(self.ve.h, h2_tags[k])
                mma = TE.matmul(bA.ap[0:wj, :], slota[0:128, wj * k:wj * (k + 1)],
                                hbg(k), start=(k == 0), stop=(k == NCH - 1))
            bA.pe_tag = self.pe.inc(mma)
            for k in range(NCH):
                mmg = TE.matmul(bG.ap[0:wj, :], slotg[0:128, wj * k:wj * (k + 1)],
                                hbg(k), start=(k == 0), stop=(k == NCH - 1))
            bG.pe_tag = self.pe.inc(mmg)
            self.ring_done(sa)
            self.ring_done(sg)
            gsl = j % 2
            if self.gel_cons[gsl]:
                SE.wait_ge(self.ve.h, self.gel_cons[gsl])
            self.bank_rd(bG, SE)
            ts = self.se.inc(SE.activation(self.gelsb[0:wj, 512 * gsl:512 * gsl + 512],
                                           bG.ap[0:wj, :], AF.Gelu,
                                           bias=self.b1g[0:wj, 22 * l + j:22 * l + j + 1]))
            bG.cons.append((self.se, ts))
            self.bank_rd(bA, VE)
            VE.wait_ge(self.se.h, ts)
            tv = self.ve.inc(VE.scalar_tensor_tensor(
                self.AR[0:wj, self.AGOF + 512 * j:self.AGOF + 512 * (j + 1)],
                bA.ap[0:wj, :], self.b1a[0:wj, 22 * l + j:22 * l + j + 1],
                self.gelsb[0:wj, 512 * gsl:512 * gsl + 512], ALU.add, ALU.mult))
            bA.cons.append((self.ve, tv))
            self.gel_cons[gsl] = tv
        agv = self.ve.v
        for m in range(NCH):
            self.bank_acq(self.pb[m])
        TE.wait_ge(self.ve.h, agv)
        for j in range(NJ):
            kr = JW[j]
            si, slot, wt = self.ring_use(("w2", l, j), wbf)
            TE.wait_ge(self.wload.h, wt)
            for m in range(NCH):
                mm = TE.matmul(self.pb[m].ap[:, :], slot[0:kr, 128 * m:128 * (m + 1)],
                               self.AR[0:kr, self.AGOF + 512 * j:self.AGOF + 512 * (j + 1)],
                               start=(j == 0), stop=(j == NJ - 1))
                if j == NJ - 1:
                    self.pb[m].pe_tag = self.pe.inc(mm)
            if j < NJ - 1:
                self.pe.inc(mm)
            self.ring_done(si)
        self.free["AG"] = self.pe.v
        out_tags = []
        g0 = max(self.free.values(), default=0)
        VE.wait_ge(self.pe.h, g0)
        base = 4 * (2 * (inst + 1) + 0)
        for m in range(NCH):
            b = self.pb[m]
            self.bank_rd(b, VE)
            sc = self.S[:, 512 * m:512 * (m + 1)]
            tv = self.ve.inc(VE.scalar_tensor_tensor(sc, b.ap[:, :],
                                                     self.b2c[:, 8 * l + m:8 * l + m + 1],
                                                     sc, ALU.add, ALU.add))
            b.cons.append((self.ve, tv))
            if next_shift:
                dst = self.AR[:, self.HBF + 512 * m:self.HBF + 512 * (m + 1)]
                if m < 4:
                    out_tags.append(self.ve.inc(VE.tensor_copy(dst, sc)))
                else:
                    cc = base + (m - 4)
                    VE.tensor_copy(dst[:, 1:512], sc[:, 0:511])
                    out_tags.append(self.ve.inc(
                        VE.tensor_copy(dst[:, 0:1], self.CARR[:, cc:cc + 1])))
                    self.ve.inc(VE.tensor_copy(self.CARR[:, cc:cc + 1], sc[:, 511:512]))
            else:
                xc = self.X[:, 512 * m:512 * (m + 1)]
                out_tags.append(self.ve.inc(
                    VE.scalar_tensor_tensor(xc, sc, 0.5, xc, ALU.mult, ALU.add)))
        return out_tags

    @staticmethod
    def _bq_pos(h):
        return 64 * ((h // 4) % 2), (h % 4) + 4 * (h // 8)

    def phaseB(self, t, wbf):
        TE, VE, SE, GE = self.TE, self.VE, self.SE, self.GE
        g0 = max(self.free.values(), default=0)
        VE.wait_ge(self.pe.h, g0)
        xg = lambda c: self.X[:, 512 * c:512 * (c + 1)]
        yg = lambda c: self.AR[:, self.HBF + 512 * c:self.HBF + 512 * (c + 1)]
        y_tags = self.layernorm(xg, yg, None, out_f32=False, xbf_off=self.QKBF,
                                src_tags=self.x_tags)
        yv = self.ve.v
        hb = lambda k: self.AR[:, self.HBF + 512 * k:self.HBF + 512 * (k + 1)]
        for m in range(NCH):
            b = self.pb[m % 4]
            self.bank_acq(b)
            si, slot, wt = self.ring_use(("Bq", m), wbf)
            TE.wait_ge(self.wload.h, wt)
            for k in range(NCH):
                if m == 0:
                    TE.wait_ge(self.ve.h, y_tags[k])
                mm = TE.matmul(b.ap[:, :], slot[0:128, 128 * k:128 * (k + 1)], hb(k),
                               start=(k == 0), stop=(k == NCH - 1))
            b.pe_tag = self.pe.inc(mm)
            self.bank_rd(b, VE)
            for hh in range(2):
                h = 2 * m + hh
                rh, cs = self._bq_pos(h)
                dst = self.AR[rh:rh + 64, self.QKBF + 512 * cs:self.QKBF + 512 * (cs + 1)]
                tv = self.ve.inc(VE.tensor_scalar(dst, b.ap[64 * hh:64 * hh + 64, :],
                                                  0.125, None, ALU.mult))
            b.cons.append((self.ve, tv))
            self.ring_done(si)
        for m in range(2):
            b = self.pb[4 + m]
            self.bank_acq(b)
            si, slot, wt = self.ring_use(("Bk", m), wbf)
            TE.wait_ge(self.wload.h, wt)
            TE.wait_ge(self.ve.h, yv)
            for k in range(NCH):
                mm = TE.matmul(b.ap[:, :], slot[0:128, 128 * k:128 * (k + 1)], hb(k),
                               start=(k == 0), stop=(k == NCH - 1))
            b.pe_tag = self.pe.inc(mm)
            self.bank_rd(b, VE)
            tv = self.ve.inc(VE.tensor_copy(self.KB[:, 4096 * m + 512 * t:4096 * m + 512 * (t + 1)],
                                            b.ap[:, :]))
            b.cons.append((self.ve, tv))
            self.ring_done(si)
        for sub in range(4):
            self.bank_acq(self.pb[4 + sub])
        for k in range(NCH):
            si, slot, wt = self.ring_use(("Bv", k), wbf)
            TE.wait_ge(self.wload.h, wt)
            TE.wait_ge(self.ve.h, yv)
            for sub in range(4):
                mm = TE.matmul(self.pb[4 + sub].ap[:, 0:256],
                               hb(k)[:, 128 * sub:128 * (sub + 1)], slot[0:128, 0:256],
                               start=(k == 0), stop=(k == NCH - 1))
                if k == NCH - 1:
                    self.pb[4 + sub].pe_tag = self.pe.inc(mm)
            self.ring_done(si)
        for sub in range(4):
            b = self.pb[4 + sub]
            self.bank_rd(b, VE)
            ci = 4 * t + sub
            dst4 = self.VB[:, 260 * ci:260 * (ci + 1)].rearrange("p (g d) -> p g d", d=65)[:, :, 0:64]
            src4 = b.ap[:, 0:256].rearrange("p (g d) -> p g d", d=64)
            tv = self.ve.inc(VE.tensor_copy(dst4, src4))
            b.cons.append((self.ve, tv))
        qkv_ve = self.ve.v
        TEw = TE
        nK = 4 * (t + 1)
        self.norm_gp = self.gp.v
        pend = None
        for h in range(HQ):
            g = h // 4
            rh, cs = self._bq_pos(h)
            qs = self.AR[rh:rh + 64, self.QKBF + 512 * cs:self.QKBF + 512 * (cs + 1)]
            bO = self.pb[4 + (h % 2)]
            self.bank_acq(bO)
            plist = []
            TE.wait_ge(self.ve.h, qkv_ve)
            K = 0
            while K < 4 * t:
                pi = (K // 2) % 2
                bSa, bSb = self.pb[2 * pi], self.pb[2 * pi + 1]
                self.bank_acq(bSa)
                self.bank_acq(bSb)
                for j, bS in ((0, bSa), (1, bSb)):
                    lhs = self.KB[rh:rh + 64,
                                  4096 * (g // 2) + 128 * (K + j):4096 * (g // 2) + 128 * (K + j + 1)]
                    mm = TE.matmul(bS.ap[:, :], lhs, qs, start=True, stop=True)
                bSa.pe_tag = bSb.pe_tag = self.pe.inc(mm)
                pairap = self.PBIG[:, 1024 * pi:1024 * (pi + 1)]
                pslot = self.pfull_rr % 2
                self.pfull_rr += 1
                pbuf = self.Pfull[:, 1024 * pslot:1024 * (pslot + 1)]
                k0 = pbuf[:, 0:512].offset
                k1 = pbuf[:, 512:1024].offset
                mx = max(self.ptag.get(k0, 0), self.ptag.get(k1, 0))
                if mx:
                    SE.wait_ge(self.pe.h, mx)
                SE.wait_ge(self.pe.h, bSa.pe_tag)
                te = self.se.inc(SE.activation(pbuf, pairap, AF.Exp))
                bSa.cons.append((self.se, te))
                bSb.cons.append((self.se, te))
                TE.wait_ge(self.se.h, te)
                for j in range(2):
                    vsl = self.VB[:, 260 * (K + j) + 65 * g:260 * (K + j) + 65 * g + 65]
                    mmv = TE.matmul(bO.ap[0:65, :], vsl, pbuf[:, 512 * j:512 * (j + 1)],
                                    start=(K + j == 0), stop=(K + j == nK - 1))
                plist.append((k0, k1))
                K += 2
            for kci in range(4):
                K = 4 * t + kci
                bS = self.pb[K % 2]
                self.bank_acq(bS)
                lhs = self.KB[rh:rh + 64, 4096 * (g // 2) + 128 * K:4096 * (g // 2) + 128 * (K + 1)]
                lo = 128 * kci
                mm = TE.matmul(bS.ap[:, lo:512], lhs, qs[:, lo:512], start=True, stop=True)
                bS.pe_tag = self.pe.inc(mm)
                if kci > 0:
                    pbuf = self.Pcur[:, 512 * (kci - 1):512 * kci]
                else:
                    pslot = self.pfull_rr % 2
                    self.pfull_rr += 1
                    pbuf = self.Pfull[:, 1024 * pslot:1024 * pslot + 512]
                key = pbuf.offset
                mx = self.ptag.get(key, 0)
                if mx:
                    SE.wait_ge(self.pe.h, mx)
                self.bank_rd(bS, SE)
                te = self.se.inc(SE.activation(pbuf[:, lo:512], bS.ap[:, lo:512], AF.Exp))
                bS.cons.append((self.se, te))
                tg2 = 0
                if kci > 0:
                    blk = 128 * kci
                    GE.wait_ge(self.se.h, te)
                    tg2 = self.gp.inc(GE.tensor_tensor(pbuf[:, blk:blk + 128],
                                                       pbuf[:, blk:blk + 128],
                                                       self.tric[:, :], ALU.mult))
                else:
                    blk = 0
                    GE.wait_ge(self.se.h, te)
                    tg2 = self.gp.inc(GE.tensor_tensor(pbuf[:, 0:128],
                                                       pbuf[:, 0:128],
                                                       self.tric[:, :], ALU.mult))
                TE.wait_ge(self.gp.h, tg2)
                vsl = self.VB[:, 260 * K + 65 * g:260 * K + 65 * g + 65]
                mmv = TE.matmul(bO.ap[0:65, :], vsl, pbuf[:, :],
                                start=(K == 0), stop=(K == nK - 1))
                if K == nK - 1:
                    bO.pe_tag = self.pe.inc(mmv)
                plist.append((key, None))
            for (k0, k1) in plist:
                self.ptag[k0] = bO.pe_tag
                if k1 is not None:
                    self.ptag[k1] = bO.pe_tag
            if pend is not None:
                pbO, ph = pend
                pdst = self.AR[64 * (ph % 2):64 * (ph % 2) + 64,
                               self.QAUG + 512 * (ph // 2):self.QAUG + 512 * (ph // 2) + 512]
                self._attn_normalize(pbO, ph, pdst, qkv_ve, bb_base=6)
            pend = (bO, h)
        if pend is not None:
            pbO, ph = pend
            pdst = self.AR[64 * (ph % 2):64 * (ph % 2) + 64,
                           self.QAUG + 512 * (ph // 2):self.QAUG + 512 * (ph // 2) + 512]
            self._attn_normalize(pbO, ph, pdst, qkv_ve, bb_base=6)
        otv = self.ve.v
        self.x_tags = []
        for m in range(NCH):
            b = self.pb[4 + (m % 4)]
            self.bank_acq(b)
            si, slot, wt = self.ring_use(("Bo", m), wbf)
            TE.wait_ge(self.wload.h, wt)
            TE.wait_ge(self.ve.h, otv)
            for k in range(NCH):
                mm = TE.matmul(b.ap[:, :], slot[0:128, 128 * k:128 * (k + 1)],
                               self.AR[:, self.QAUG + 512 * k:self.QAUG + 512 * (k + 1)],
                               start=(k == 0), stop=(k == NCH - 1))
            b.pe_tag = self.pe.inc(mm)
            self.ring_done(si)
            self.bank_rd(b, VE)
            xc = self.X[:, 512 * m:512 * (m + 1)]
            tv = self.ve.inc(VE.tensor_tensor(xc, xc, b.ap[:, :], ALU.add))
            b.cons.append((self.ve, tv))
            self.x_tags.append(tv)
        self.free["OTB"] = self.pe.v
        return self.ve.v

    def ffn_phase(self, t, phase, wbf):
        VE = self.VE
        lim = _STAGES.index(STAGE_LIMIT)
        g0 = max(self.free.values(), default=0)
        VE.wait_ge(self.pe.h, g0)
        xg = lambda c: self.X[:, 512 * c:512 * (c + 1)]
        sg = lambda c: self.S[:, 512 * c:512 * (c + 1)]
        s_tags = self.layernorm(xg, sg, None, out_f32=True, xbf_off=self.QKBF,
                                src_tags=self.x_tags)
        if phase == 0:
            self.tap("A_S0", self.S[:, :])
        if lim < _STAGES.index("shift"):
            return self.ve.v
        h_tags = None
        for li in range(2):
            inst = 2 * phase + li
            if h_tags is None:
                g0 = max(self.free.values(), default=0)
                VE.wait_ge(self.pe.h, g0)
                h_tags = self.shift_cast(inst, 0, sg, self.HBF)
            if phase == 0 and li == 0:
                self.tap("A_h0", self.AR[:, self.HBF:self.HBF + 4096])
            if lim < _STAGES.index("qkvv"):
                return self.ve.v
            self.local_qkv(t, inst, wbf, v_only=(lim == _STAGES.index("qkvv")),
                           h_tags=h_tags)
            if lim < _STAGES.index("qkproc"):
                return self.ve.v
            k_tags = self.local_qk_process(t, inst, 1, self.io)
            self.local_norm_rows(t, inst)
            q_tags = self.local_qk_process(t, inst, 0, self.io)
            if lim < _STAGES.index("norm"):
                return self.ve.v
            if phase == 0 and li == 0:
                self.tap("A_qaug", self.AR[0:65, self.QAUG:self.QAUG + 4096])
                self.tap("A_kaug", self.AR[0:65, self.KAUG:self.KAUG + 4096])
                self.tap("A_vaug", self.AR[:, self.VAUGC:self.VAUGC + 2080])
            if lim < _STAGES.index("attn"):
                return self.ve.v
            self.local_attn(t, inst, q_tags, k_tags)
            if phase == 0 and li == 0:
                self.tap("A_oT", self.AR[:, self.OTL:self.OTL + 2048])
                self.tap("A_Pf", self.Pfull[:, :])
                self.tap("A_Pc", self.Pcur[:, :])
                self.tap("A_dbg1", self.scr1[:, :])
                self.tap("A_dbg2", self.statf[:, :])
            if lim < _STAGES.index("wo"):
                return self.ve.v
            self.local_wo(t, inst, wbf)
            if phase == 0 and li == 0:
                self.tap("A_S1", self.S[:, :])
            if lim < _STAGES.index("ffn"):
                return self.ve.v
            h_tags = self.local_ffn(t, inst, wbf, next_shift=(li == 0))
            if phase == 0 and li == 0:
                self.tap("A_AG", self.AR[:, self.AGOF:self.AGOF + 11264])
                self.tap("A_S2", self.S[:, :])
        self.x_tags = h_tags
        if phase == 0:
            self.tap("A_X", self.X[:, :])
        return self.ve.v

    def emit_tile(self, t, io):
        self.io = io
        lim = _STAGES.index(STAGE_LIMIT)
        VE, SE, GE, SY = self.VE, self.SE, self.GE, self.SY
        if t > 0:
            SE.wait_ge(self.osem.h, self.osem.v)
        for c in range(NCH):
            self.xload.inc(SE.dma_start(self.X[:, 512 * c:512 * (c + 1)],
                                        io["x_in"][128 * c:128 * (c + 1), 512 * t:512 * (t + 1)]), 16)
        VE.wait_ge(self.xload.h, self.xload.v)
        self.TE.wait_ge(self.xload.h, self.xload.v)
        SE.wait_ge(self.xload.h, self.xload.v)
        self.x_tags = None
        if lim >= _STAGES.index("ln"):
            self.ffn_phase(t, 0, io["wbf"])
        if lim >= _STAGES.index("B"):
            self.phaseB(t, io["wbf"])
            if t == _TAP_AT_TILE:
                self.tap("B_X", self.X[:, :])
        if lim >= _STAGES.index("full"):
            self.ffn_phase(t, 1, io["wbf"])
        vfin = self.ve.v
        GE.wait_ge(self.ve.h, vfin)
        for c in range(NCH):
            self.osem.inc(GE.dma_start(io["out"][128 * c:128 * (c + 1), 512 * t:512 * (t + 1)],
                                       self.X[:, 512 * c:512 * (c + 1)]), 16)

    def finish(self):
        self.SY.wait_ge(self.osem.h, self.osem.v)
        self.GE.wait_ge(self.osem.h, self.osem.v)



def build_nc(n_tiles=None, taps=()):
    n_tiles = n_tiles or (DEV_NT or NT)
    nc = bass.Bass("TRN2", target_bir_lowering=False)
    io = {}
    io["x_in"] = nc.dram_tensor("x_in", [D, SEQ], F32, kind="ExternalInput")
    io["out"] = nc.dram_tensor("out", [D, SEQ], F32, kind="ExternalOutput")
    _wf_len = _WFLAT_LEN[0]
    io["wflat"] = nc.dram_tensor("wflat", [_wf_len], F32, kind="ExternalInput")
    io["wbf"] = nc.dram_tensor("wbf", [_wf_len], BF16)
    for n, s in (("t1q", [128, SEQ]), ("t2q", [128, SEQ]), ("t1k", [128, SEQ]),
                 ("t2k", [128, SEQ])):
        io[n] = nc.dram_tensor(n, s, F32, kind="ExternalInput")
    for n, s in (("rmat", [128, 128]), ("tric", [128, 128]), ("trip", [128, 128]),
                 ("selq", [128, 32]), ("selbc", [8, 512])):
        io[n] = nc.dram_tensor(n, s, BF16, kind="ExternalInput")
    for n, s in (("qs4", [128, 8]), ("b1a", [128, 44]), ("b1g", [128, 44]),
                 ("b2c", [128, 16]), ("onesrow", [1, 512]),
                 ("row_ones128", [1, 128]), ("row_neg128", [1, 128]),
                 ("row_ones64", [1, 64]), ("col_ones128", [128, 1]),
                 ("col_ones64", [64, 1]), ("gbrow_g", [1, 2048]), ("gbrow_b", [1, 2048])):
        io[n] = nc.dram_tensor(n, s, F32, kind="ExternalInput")
    tap_handles = {}
    for name in taps:
        shp = TAPS_DEF[name]
        tap_handles[name] = nc.dram_tensor("tap_" + name, list(shp), F32, kind="ExternalOutput")

    offs, marks = _OFFS[0], _MARKS[0]
    ctx = contextlib.ExitStack()
    mk = _MK(nc, ctx, offs, marks, n_tiles, tap_handles)
    with ctx, nc.Block() as block:
        @block.sync
        def _(sy):
            mk.emit_init(io)
            for t in range(n_tiles):
                mk.emit_tile(t, io)
            mk.finish()
    return nc


_WFLAT_LEN = [None]
_OFFS = [None]
_MARKS = [None]
_BUILD = {}


def _get_host_arrays(inputs):
    wflat, offs, marks = _flatten_weights(inputs)
    _WFLAT_LEN[0] = wflat.size
    _OFFS[0] = offs
    _MARKS[0] = marks
    tabs = _host_tables()
    sm = _host_smalls(inputs)
    gb = sm.pop("gbrows")
    consts = dict(
        wflat=wflat,
        t1q=tabs["t1q"], t2q=tabs["t2q"], t1k=tabs["t1k"], t2k=tabs["t2k"],
        rmat=np.asarray(tabs["rmat"]), tric=np.asarray(tabs["tric"]),
        trip=np.asarray(tabs["trip"]), selq=np.asarray(tabs["selq"]),
        selbc=np.asarray(tabs["selbc"]),
        qs4=sm["qs4"], b1a=sm["b1a"], b1g=sm["b1g"], b2c=sm["b2c"],
        onesrow=sm["onesrow"],
        row_ones128=np.ones((1, 128), np.float32),
        row_neg128=-np.ones((1, 128), np.float32),
        row_ones64=np.ones((1, 64), np.float32),
        col_ones128=np.ones((128, 1), np.float32),
        col_ones64=np.ones((64, 1), np.float32),
        gbrow_g=gb[0:1, :], gbrow_b=gb[1:2, :],
    )
    return consts


def kernel(**inputs):
    inputs = {k: np.asarray(v) for k, v in inputs.items()}
    consts = _get_host_arrays(inputs)
    key = ("full", DEV_NT)
    if key not in _BUILD:
        _BUILD[key] = build_nc()
    nc = _BUILD[key]
    x = inputs["L0_attn_hidden_states"]
    in_maps = []
    for b in range(8):
        m = dict(consts)
        m["x_in"] = np.ascontiguousarray(x[b].T.astype(np.float32))
        in_maps.append(m)
    res = run_bass_kernel_spmd(nc, in_maps, core_ids=list(range(8)))
    outs = [np.asarray(r["out"]).T for r in res.results]
    return np.stack(outs, axis=0).astype(np.float32)


# version 11
# speedup vs baseline: 1.0068x; 1.0068x over previous
import contextlib
import math
import numpy as np
import ml_dtypes
import concourse.bass as bass
import concourse.mybir as mybir
from concourse.bass_utils import run_bass_kernel_spmd

F32 = mybir.dt.float32
BF16 = mybir.dt.bfloat16
AF = mybir.ActivationFunctionType
ALU = mybir.AluOpType
AX = mybir.AxisListType
bfd = ml_dtypes.bfloat16

SEQ, D, TILE, NT = 4096, 1024, 512, 8
NCH = 8
HID, NJ = 2730, 22
JW = [128] * 21 + [42]
HL, HD = 8, 64
HQ, HKV = 16, 4
WIN = 512
DEV_NT = None
_STAGES = ["init", "ln", "shift", "qkvv", "qkv", "qkproc", "norm", "attn", "wo", "ffn",
           "A", "B", "full"]
STAGE_LIMIT = "full"
ATTN_HEADS = None


def _flatten_weights(W):
    parts, offs = [], {}
    pos = 0

    def add(key, arr):
        nonlocal pos
        a = np.ascontiguousarray(arr, dtype=np.float32)
        offs[key] = (pos, a.shape)
        parts.append(a.reshape(-1))
        pos += a.size

    def lhsT_groups(Wm, kk):
        return Wm.reshape(kk, 128, Wm.shape[1]).transpose(1, 0, 2).reshape(128, -1)

    marks = {}
    for l in range(2):
        for m in range(8):
            add(("qkv", l, m), lhsT_groups(W["L0_ffn_wqkv"][l][:, m * 128:(m + 1) * 128], 8))
        for k in range(8):
            add(("qkvv", l, k), W["L0_ffn_wqkv"][l][k * 128:(k + 1) * 128, 1024:1536])
        marks[("conv", "qkv", l)] = pos
        for m in range(8):
            add(("wo", l, m), lhsT_groups(W["L0_ffn_wo"][l][:, m * 128:(m + 1) * 128], 4))
        marks[("conv", "wo", l)] = pos
        for j in range(NJ):
            wj = JW[j]
            add(("w1a", l, j), lhsT_groups(W["L0_ffn_w1"][l][:, j * 128:j * 128 + wj], 8))
            add(("w1g", l, j), lhsT_groups(W["L0_ffn_w1"][l][:, 2730 + j * 128:2730 + j * 128 + wj], 8))
        marks[("conv", "w1", l)] = pos
        for j in range(NJ):
            add(("w2", l, j), W["L0_ffn_w2"][l][j * 128:j * 128 + JW[j], :])
        marks[("conv", "w2", l)] = pos
    for m in range(8):
        add(("Bq", m), lhsT_groups(W["L0_attn_W_qkv"][:, m * 128:(m + 1) * 128], 8))
    for m in range(2):
        add(("Bk", m), lhsT_groups(W["L0_attn_W_qkv"][:, 1024 + m * 128:1024 + (m + 1) * 128], 8))
    for k in range(8):
        add(("Bv", k), W["L0_attn_W_qkv"][k * 128:(k + 1) * 128, 1280:1536])
    marks[("conv", "Bqkv", 0)] = pos
    for m in range(8):
        add(("Bo", m), lhsT_groups(W["L0_attn_W_o"][:, m * 128:(m + 1) * 128], 8))
    marks[("conv", "Bo", 0)] = pos
    return np.concatenate(parts), offs, marks


def _host_tables():
    t = np.arange(SEQ, dtype=np.float64)
    i = np.arange(32, dtype=np.float64)
    inv_freq = 1.0 / 10000.0 ** (2 * i / 64.0)
    freqs = t[:, None] * inv_freq[None, :]
    base = (2 * i + 0.4 * 64) / (1.4 * 64)
    power = (t - SEQ // 2) / float(WIN // 2)
    scale = base[None, :] ** power[:, None]
    cos64 = np.cos(np.concatenate([freqs, freqs], 1))
    sin64 = np.sin(np.concatenate([freqs, freqs], 1))
    sc64 = np.concatenate([scale, scale], 1)
    def tab(x):
        return np.tile(x.T.astype(np.float32), (2, 1))
    T1q = tab(8.0 * cos64 * sc64)
    T2q = tab(8.0 * sin64 * sc64)
    T1k = tab(cos64 / sc64)
    T2k = tab(sin64 / sc64)

    R = np.zeros((128, 128), np.float32)
    for m in range(128):
        d = m % 64
        h64 = m - d
        if d < 32:
            R[h64 + d + 32, m] = -1.0
        else:
            R[h64 + d - 32, m] = 1.0

    kk, qq = np.meshgrid(np.arange(128), np.arange(128), indexing="ij")
    triC = (kk <= qq).astype(np.float32)
    triP = (kk >= qq).astype(np.float32)

    selq = np.zeros((128, 32), np.float32)
    selbc = np.zeros((8, 4 * 128), np.float32)
    for c in range(4):
        for p in range(128):
            h = 2 * c + p // 64
            selq[p, 8 * c + h] = 1.0
            selbc[h, 128 * c + p] = 1.0
    return dict(t1q=T1q, t2q=T2q, t1k=T1k, t2k=T2k,
                rmat=R.astype(bfd), tric=triC.astype(bfd), trip=triP.astype(bfd),
                selq=selq.astype(bfd), selbc=selbc.astype(bfd))


def _host_smalls(W):
    qs = np.zeros((128, 8), np.float32)
    for l in range(2):
        q_ = W["L0_ffn_q_scale"][l].astype(np.float32)
        k_ = W["L0_ffn_k_scale"][l].astype(np.float32)
        qp = np.concatenate([q_[32:], q_[:32]])
        kp = np.concatenate([k_[32:], k_[:32]])
        qs[:, l * 4 + 0] = np.tile(q_, 2)
        qs[:, l * 4 + 1] = np.tile(qp, 2)
        qs[:, l * 4 + 2] = np.tile(k_, 2)
        qs[:, l * 4 + 3] = np.tile(kp, 2)
    b1a = np.zeros((128, 44), np.float32)
    b1g = np.zeros((128, 44), np.float32)
    for l in range(2):
        for j in range(NJ):
            wj = JW[j]
            b1a[:wj, l * 22 + j] = W["L0_ffn_b1"][l][j * 128:j * 128 + wj]
            b1g[:wj, l * 22 + j] = W["L0_ffn_b1"][l][2730 + j * 128:2730 + j * 128 + wj]
    b2c = np.zeros((128, 16), np.float32)
    for l in range(2):
        b2c[:, l * 8:(l + 1) * 8] = W["L0_ffn_b2"][l].reshape(8, 128).T
    gb = np.zeros((2, 2048), np.float32)
    for l in range(2):
        gb[0, l * 1024:(l + 1) * 1024] = W["L0_ffn_ln_g"][l]
        gb[1, l * 1024:(l + 1) * 1024] = W["L0_ffn_ln_b"][l]
    misc = np.zeros((2, 512), np.float32)
    misc[0, :] = 1.0
    return dict(qs4=qs, b1a=b1a, b1g=b1g, b2c=b2c, gbrows=gb, onesrow=misc[0:1, :])



class _Sem:
    __slots__ = ("h", "v")

    def __init__(self, nc, ctx, name):
        self.h = ctx.enter_context(nc.semaphore(name))
        self.v = 0

    def inc(self, instr, n=1):
        instr.then_inc(self.h, n)
        self.v += n
        return self.v


class _Bank:
    __slots__ = ("ap", "pe_tag", "cons")

    def __init__(self, ap):
        self.ap = ap
        self.pe_tag = 0
        self.cons = []


TAPS_DEF = {
    "A_S0": (128, 4096), "A_h0": (128, 4096), "A_qaug": (65, 4096),
    "A_kaug": (65, 4096), "A_oT": (128, 2048), "A_S1": (128, 4096),
    "A_h2n": (128, 4096), "A_S2": (128, 4096), "A_X": (128, 4096),
    "B_ybf": (128, 4096), "B_qB": (128, 4096), "B_oTB": (128, 4096),
    "A_Pf": (128, 1536), "A_Pc": (128, 1536),
    "A_dbg1": (128, 512), "A_dbg2": (1, 2048),
    "B_X": (128, 4096), "A_AG": (128, 11264), "A_vaug": (128, 2080),
}
_TAP_AT_TILE = 0


class _MK:
    def __init__(self, nc, ctx, offs, marks, n_tiles, taps):
        self.nc = nc
        self.n_tiles = n_tiles
        self.taps = dict(taps or {})
        self.TE, self.VE, self.SE = nc.tensor, nc.vector, nc.scalar
        self.GE, self.SY = nc.gpsimd, nc.sync
        self.offs = offs
        self.units = []
        prev = 0
        for k, end in marks.items():
            self.units.append((k, prev, end))
            prev = end
        self.conv_tag = {}

        sem = lambda n: _Sem(nc, ctx, n)
        self.pe, self.ve, self.se, self.gp = sem("pe"), sem("ve"), sem("se"), sem("gp")
        self.wconv, self.wload, self.xload, self.osem = sem("wconv"), sem("wload"), sem("xload"), sem("osem")

        sb = lambda n, s, d: ctx.enter_context(nc.sbuf_tensor("s_" + n, s, d))
        ps = lambda n, s: ctx.enter_context(nc.psum_tensor(n, s, F32))
        self.PBIG = ps("PBIG", [128, 4096])
        self.pb = [_Bank(self.PBIG[:, 512 * i:512 * (i + 1)]) for i in range(8)]
        self.X = sb("X", [128, 4096], F32)
        self.S = sb("S", [128, 4096], F32)
        self.t1s = sb("t1s", [128, 512], F32)
        self.t2s = sb("t2s", [128, 512], F32)
        self.scr1 = sb("scr1", [128, 512], F32)
        self.scr2 = sb("scr2", [128, 512], F32)
        self.statf = sb("statf", [1, 2048], F32)
        self.CARR = sb("CARR", [128, 32], F32)
        self.qs4 = sb("qs4", [128, 8], F32)
        self.b1a = sb("b1a", [128, 44], F32)
        self.b1g = sb("b1g", [128, 44], F32)
        self.b2c = sb("b2c", [128, 16], F32)
        self.kn2 = sb("kn2", [1, 64], F32)
        self.Mprev = sb("Mprev", [64, 32], BF16)
        self.Mu = sb("Mu", [65, 8], BF16)
        self.absq = sb("absq", [65, 1024], BF16)
        self.one64f = sb("one64f", [1, 64], F32)
        self.KB = sb("KB", [128, 8192], BF16)
        self.VB = sb("VB", [128, 8320], BF16)
        self.KLp = sb("KLp", [128, 8192], BF16)
        self.VLp = sb("VLp", [128, 8320], BF16)
        self.ktmp = sb("ktmp", [65, 1024], BF16)
        self.Pfull = sb("Pfull", [128, 2048], BF16)
        self.Pprev = sb("Pprev", [128, 1536], BF16)
        self.Pcur = sb("Pcur", [128, 1536], BF16)
        self.grow = sb("grow", [1, 2048], BF16)
        self.gnb = sb("gnb", [2, 2048], BF16)
        self.statbf = sb("statbf", [2, 1024], BF16)
        self.onesbcol = sb("onesbcol", [128, 1], BF16)
        self.ones64col = sb("ones64col", [64, 1], BF16)
        self.onesb128 = sb("onesb128", [1, 128], BF16)
        self.negb128 = sb("negb128", [1, 128], BF16)
        self.selq = sb("selq", [128, 32], BF16)
        self.selbc = sb("selbc", [8, 512], BF16)
        self.rmat = sb("rmat", [128, 128], BF16)
        self.tric = sb("tric", [128, 128], BF16)
        self.trip = sb("trip", [128, 128], BF16)
        self.invbf = sb("invbf", [8, 1024], BF16)
        self.gelsb = sb("gelsb", [128, 1024], BF16)
        self.sqb = sb("sqb", [128, 1024], BF16)
        self.ring = sb("ring", [128, 10240], BF16)
        self.AR = sb("AR", [128, 20512], BF16)
        self.HBF, self.QKBF, self.QAUG, self.KAUG = 0, 4096, 8192, 12288
        self.VAUGC, self.OTL, self.AGOF = 16384, 18464, 4096
        self.rslot = [{"wtag": 0, "free": 0} for _ in range(10)]
        self.rr = 0
        self.free = {}
        self.ptag = {}
        self.sq_cons = [0, 0]
        self.gel_cons = [0, 0]
        self.ktmp_cons = [0, 0]
        self.pfull_rr = 0
        self.tab_cons_ve = 0
        self.tab_cons_gp = 0
        self.kaug_row64 = 0
        self.inv_cons = 0
        self.scr_cons = [0, 0]
        self.absq_cons = [0, 0]
        self.xbf_ready = False

    def unit_of(self, key):
        off = self.offs[key][0]
        for k, s, e in self.units:
            if s <= off < e:
                return k
        raise KeyError(key)

    def bank_acq(self, b):
        for (sem, tag) in b.cons:
            self.TE.wait_ge(sem.h, tag)
        b.cons = []

    def bank_rd(self, b, eng):
        eng.wait_ge(self.pe.h, b.pe_tag)

    def ring_use(self, key, wbf):
        i = self.rr
        self.rr = (self.rr + 1) % len(self.rslot)
        sl = self.rslot[i]
        if sl["free"] > 0:
            self.SY.wait_ge(self.pe.h, sl["free"])
        ct = self.conv_tag.get(self.unit_of(key))
        if ct:
            self.SY.wait_ge(self.wconv.h, ct)
        off, (P, C) = self.offs[key]
        src = wbf[off:off + P * C].rearrange("(p c) -> p c", p=P)
        dst = self.ring[0:P, 1024 * i:1024 * i + C]
        self.wload.inc(self.SY.dma_start(dst, src), 16)
        sl["wtag"] = self.wload.v
        return i, self.ring[0:P, 1024 * i:1024 * i + 1024], sl["wtag"]

    def ring_done(self, i):
        self.rslot[i]["free"] = self.pe.v

    def tap(self, name, ap):
        if name not in self.taps:
            return
        for s in (self.pe, self.ve, self.se, self.gp):
            if s.v:
                self.GE.wait_ge(s.h, s.v)
        self.osem.inc(self.GE.dma_start(self.taps[name][:, :], ap), 16)
        del self.taps[name]

    def emit_init(self, io):
        TE, VE, SE, GE, SY = self.TE, self.VE, self.SE, self.GE, self.SY
        for (dst, srcname) in (
            (self.onesbcol[:, :], "col_ones128"),
            (self.ones64col[:, :], "col_ones64"),
            (self.onesb128[:, :], "row_ones128"),
            (self.negb128[:, :], "row_neg128"),
            (self.grow[:, :], "gbrow_g"),
            (self.gnb[1:2, :], "gbrow_b"),
            (self.statbf[1:2, 0:512], "onesrow"),
        ):
            self.wconv.inc(GE.dma_start(dst, io[srcname][:, :]), 16)
        cvt = self.wconv.v
        for (k, s, e) in self.units:
            src = io["wflat"][s:e]
            dst = io["wbf"][s:e]
            self.conv_tag[k] = self.wconv.inc(GE.dma_start(dst, src), 16)
        for name, dstap in (
            ("selq", self.selq[:, :]), ("selbc", self.selbc[:, :]),
            ("rmat", self.rmat[:, :]), ("tric", self.tric[:, :]), ("trip", self.trip[:, :]),
        ):
            self.xload.inc(SE.dma_start(dstap, io[name][:, :]), 16)
        for name, dstap in (
            ("qs4", self.qs4[:, :]), ("b1a", self.b1a[:, :]), ("b1g", self.b1g[:, :]),
            ("b2c", self.b2c[:, :]), ("row_ones64", self.one64f[:, :]),
        ):
            self.xload.inc(SE.dma_start(dstap, io[name][:, :]), 16)
        GE.memset(self.CARR[:, :], 0.0)
        GE.memset(self.ktmp[64:65, :], -1.0)
        GE.memset(self.Pfull[:, :], 0.0)
        GE.memset(self.Pprev[:, :], 0.0)
        GE.memset(self.Pcur[:, :], 0.0)
        GE.memset(self.kn2[:, :], 0.0)
        GE.memset(self.Mu[64:65, :], -35.0)
        GE.memset(self.absq[64:65, :], 1.0)
        vb4 = self.VB[:, :].rearrange("p (c d) -> p c d", d=65)
        GE.memset(vb4[:, :, 64:65], 1.0)
        vl4 = self.VLp[:, :].rearrange("p (c d) -> p c d", d=65)
        GE.memset(vl4[:, :, 64:65], 1.0)
        va4 = self.AR[:, self.VAUGC:self.VAUGC + 2080].rearrange("p (c d) -> p c d", d=65)
        self.gp.inc(GE.memset(va4[:, :, 64:65], 1.0))
        VE.wait_ge(self.wconv.h, cvt)
        VE.wait_ge(self.xload.h, self.xload.v)
        self.ve.inc(VE.tensor_scalar(self.gnb[0:1, :], self.grow[0:1, :], -1.0, None, ALU.mult))
        TE.wait_ge(self.xload.h, self.xload.v)
        TE.wait_ge(self.wconv.h, cvt)
        TE.wait_ge(self.gp.h, self.gp.v)
        TE.wait_ge(self.ve.h, self.ve.v)
        SE.wait_ge(self.xload.h, self.xload.v)
        SE.wait_ge(self.wconv.h, cvt)
        SE.wait_ge(self.gp.h, self.gp.v)
        GE.wait_ge(self.xload.h, self.xload.v)

    def layernorm(self, src_getter, out_getter, l, out_f32, xbf_off=None,
                  src_tags=None):
        TE, VE, SE, GE = self.TE, self.VE, self.SE, self.GE
        bsum, bsq = self.pb[6], self.pb[7]
        self.bank_acq(bsum)
        self.bank_acq(bsq)
        sq_rot = self.sqb
        sv0 = self.ve.v
        if src_tags is None:
            SE.wait_ge(self.ve.h, sv0)
        for c in range(NCH):
            src = src_getter(c)
            if src_tags is not None:
                SE.wait_ge(self.ve.h, src_tags[c])
            if xbf_off is not None and self.xbf_ready:
                xb = self.AR[:, xbf_off + 512 * c: xbf_off + 512 * (c + 1)]
                tve = src_tags[c] if src_tags is not None else self.ve.v
            elif xbf_off is not None:
                xb = self.AR[:, xbf_off + 512 * c: xbf_off + 512 * (c + 1)]
                tve = self.ve.inc(VE.tensor_copy(xb, src))
            else:
                xb = src
                tve = None
            sl = c % 2
            if self.sq_cons[sl]:
                SE.wait_ge(self.pe.h, self.sq_cons[sl])
            tse = self.se.inc(SE.activation(sq_rot[:, 512 * sl:512 * (sl + 1)], src, AF.Square))
            if tve is not None:
                TE.wait_ge(self.ve.h, tve)
            TE.wait_ge(self.se.h, tse)
            mm1 = TE.matmul(bsum.ap[0:1, :], self.onesbcol[:, :], xb,
                            start=(c == 0), stop=(c == NCH - 1))
            if c == NCH - 1:
                bsum.pe_tag = self.pe.inc(mm1)
            mm2 = TE.matmul(bsq.ap[0:1, :], self.onesbcol[:, :],
                            sq_rot[:, 512 * sl:512 * (sl + 1)],
                            start=(c == 0), stop=(c == NCH - 1))
            self.sq_cons[sl] = self.pe.inc(mm2)
            if c == NCH - 1:
                bsq.pe_tag = self.pe.v
                self.xbf_ready = False
        return self.ln_finish(src_getter, out_getter, l)

    def ln_finish(self, src_getter, out_getter, l, src_tags=None):
        TE, VE, SE, GE = self.TE, self.VE, self.SE, self.GE
        bsum, bsq = self.pb[6], self.pb[7]
        sf, sbf = self.statf, self.statbf
        self.bank_rd(bsum, VE)
        VE.tensor_scalar(sf[0:1, 0:512], bsum.ap[0:1, :], 1.0 / 1024.0, None, ALU.mult)
        VE.tensor_tensor(sf[0:1, 1536:2048], sf[0:1, 0:512], sf[0:1, 0:512], ALU.mult)
        VE.scalar_tensor_tensor(sf[0:1, 512:1024], bsq.ap[0:1, :], 1.0 / 1024.0,
                                sf[0:1, 1536:2048], ALU.mult, ALU.subtract)
        t1 = self.ve.inc(VE.tensor_scalar(sf[0:1, 512:1024], sf[0:1, 512:1024],
                                          1.0, 1e-5, ALU.mult, ALU.add))
        bsum.cons.append((self.ve, t1))
        bsq.cons.append((self.ve, t1))
        SE.wait_ge(self.ve.h, t1)
        SE.activation(sf[0:1, 1024:1536], sf[0:1, 512:1024], AF.Ln)
        t2 = self.se.inc(SE.activation(sbf[0:1, 512:1024], sf[0:1, 1024:1536],
                                       AF.Exp, scale=-0.5))
        VE.wait_ge(self.se.h, t2)
        t3 = self.ve.inc(VE.tensor_tensor(sbf[0:1, 0:512], sf[0:1, 0:512],
                                          sbf[0:1, 512:1024], ALU.mult))
        out_tags = []
        for c in range(NCH):
            bA = self.pb[4 + 2 * (c % 2)]
            bB = self.pb[5 + 2 * (c % 2)]
            self.bank_acq(bA)
            self.bank_acq(bB)
            if l is None:
                lA = self.onesb128[0:1, :]
                lB = self.negb128[0:1, :]
            else:
                lA = self.grow[0:1, 1024 * l + 128 * c:1024 * l + 128 * (c + 1)]
                lB = self.gnb[0:2, 1024 * l + 128 * c:1024 * l + 128 * (c + 1)]
            if c == 0:
                TE.wait_ge(self.ve.h, t3)
                TE.wait_ge(self.se.h, t2)
            mmA = TE.matmul(bA.ap[:, :], lA, sbf[0:1, 512:1024], start=True, stop=True)
            bA.pe_tag = self.pe.inc(mmA)
            if l is None:
                mmB = TE.matmul(bB.ap[:, :], lB, sbf[0:1, 0:512], start=True, stop=True)
            else:
                mmB = TE.matmul(bB.ap[:, :], lB, sbf[0:2, 0:512], start=True, stop=True)
            bB.pe_tag = self.pe.inc(mmB)
            src = src_getter(c)
            if src_tags is not None:
                VE.wait_ge(self.ve.h, src_tags[c])
            out = out_getter(c)
            self.bank_rd(bA, VE)
            self.bank_rd(bB, VE)
            VE.tensor_tensor(self.scr2[:, :], src, bA.ap[:, :], ALU.mult)
            tv = self.ve.inc(VE.tensor_tensor(out, self.scr2[:, :], bB.ap[:, :], ALU.add))
            bA.cons.append((self.ve, tv))
            bB.cons.append((self.ve, tv))
            out_tags.append(tv)
        return out_tags


    def shift_cast(self, inst, site, src_getter, dst_off):
        VE, GE = self.VE, self.GE
        base = 4 * (2 * inst + site)
        tags = []
        for c in range(NCH):
            src = src_getter(c)
            dst = self.AR[:, dst_off + 512 * c: dst_off + 512 * (c + 1)]
            if c < 4:
                tags.append(self.ve.inc(VE.tensor_copy(dst, src)))
            else:
                cc = base + (c - 4)
                self.ve.inc(VE.tensor_copy(dst[:, 1:512], src[:, 0:511]))
                tags.append(self.ve.inc(VE.tensor_copy(dst[:, 0:1], self.CARR[:, cc:cc + 1])))
                self.ve.inc(VE.tensor_copy(self.CARR[:, cc:cc + 1], src[:, 511:512]))
        return tags

    def local_qkv(self, t, inst, wbf, v_only=False, h_tags=None):
        TE, VE, SE = self.TE, self.VE, self.SE
        l = inst % 2
        hb = lambda k: self.AR[:, self.HBF + 512 * k:self.HBF + 512 * (k + 1)]
        bq_sum, bk_sum = self.pb[6], self.pb[7]
        self.bank_acq(bq_sum)
        self.bank_acq(bk_sum)
        for m in range(8):
            b = self.pb[m % 4]
            self.bank_acq(b)
            si, slot, wt = self.ring_use(("qkv", l, m), wbf)
            TE.wait_ge(self.wload.h, wt)
            for k in range(NCH):
                if h_tags is not None and m == 0:
                    TE.wait_ge(self.ve.h, h_tags[k])
                mm = TE.matmul(b.ap[:, :], slot[0:128, 128 * k:128 * (k + 1)], hb(k),
                               start=(k == 0), stop=(k == NCH - 1))
            b.pe_tag = self.pe.inc(mm)
            self.ring_done(si)
            self.bank_rd(b, VE)
            dst = self.AR[:, self.QKBF + 512 * m:self.QKBF + 512 * (m + 1)]
            tv = self.ve.inc(VE.tensor_copy(dst, b.ap[:, :]))
            b.cons.append((self.ve, tv))
            sl = m % 2
            if self.sq_cons[sl]:
                SE.wait_ge(self.pe.h, self.sq_cons[sl])
            SE.wait_ge(self.ve.h, tv)
            ts = self.se.inc(SE.activation(self.sqb[:, 512 * sl:512 * (sl + 1)],
                                           dst, AF.Square))
            TE.wait_ge(self.se.h, ts)
            c4 = m % 4
            tgt = bq_sum if m < 4 else bk_sum
            mm2 = TE.matmul(tgt.ap[0:8, :], self.selq[:, 8 * c4:8 * (c4 + 1)],
                            self.sqb[:, 512 * sl:512 * (sl + 1)],
                            start=(c4 == 0), stop=(c4 == 3))
            self.sq_cons[sl] = self.pe.inc(mm2)
            if c4 == 3:
                tgt.pe_tag = self.pe.v
        self.qkbf_ve = self.ve.v
        self._invnorm_extract()
        for sub in range(4):
            self.bank_acq(self.pb[0 + sub])
        for k in range(NCH):
            si, slot, wt = self.ring_use(("qkvv", l, k), wbf)
            TE.wait_ge(self.wload.h, wt)
            if h_tags is not None:
                TE.wait_ge(self.ve.h, h_tags[k])
            for sub in range(4):
                mm = TE.matmul(self.pb[0 + sub].ap[:, :],
                               hb(k)[:, 128 * sub:128 * (sub + 1)], slot[0:128, 0:512],
                               start=(k == 0), stop=(k == NCH - 1))
                if k == NCH - 1:
                    self.pb[0 + sub].pe_tag = self.pe.inc(mm)
            if k < NCH - 1:
                self.pe.inc(mm)
            self.ring_done(si)
        for sub in range(4):
            b = self.pb[0 + sub]
            self.bank_rd(b, VE)
            dst = self.AR[:, self.VAUGC + 520 * sub:self.VAUGC + 520 * (sub + 1)]
            dst4 = dst.rearrange("p (g d) -> p g d", d=65)[:, :, 0:64]
            src4 = b.ap[:, :].rearrange("p (g d) -> p g d", d=64)
            tv = self.ve.inc(VE.tensor_copy(dst4, src4))
            b.cons.append((self.ve, tv))
        self.vaug_ve = self.ve.v
        if v_only:
            return
        self.free["HBF"] = self.pe.v


    def _invnorm_extract(self):
        VE, SE = self.VE, self.SE
        for bs, io_, rb in ((self.pb[7], 512, 0), (self.pb[6], 0, 32)):
            self.bank_rd(bs, VE)
            tv = self.ve.inc(VE.tensor_scalar(self.scr1[rb:rb + 8, :], bs.ap[0:8, :],
                                              1e-24, None, ALU.max))
            bs.cons.append((self.ve, tv))
            SE.wait_ge(self.ve.h, tv)
            SE.activation(self.scr2[rb:rb + 8, :], self.scr1[rb:rb + 8, :], AF.Ln)
            self.invbf_tse = self.se.inc(
                SE.activation(self.invbf[0:8, io_:io_ + 512],
                              self.scr2[rb:rb + 8, :], AF.Exp, scale=-0.5))

    def local_qk_process(self, t, inst, side, io):
        TE, VE, SE, GE, SY = self.TE, self.VE, self.SE, self.GE, self.SY
        l = inst % 2
        bsum = self.pb[6] if side == 0 else self.pb[7]
        coff = 0 if side == 0 else 2048
        aoff = self.QAUG if side == 0 else self.KAUG
        ioff = 0 if side == 0 else 512
        tq1 = io["t1q"] if side == 0 else io["t1k"]
        tq2 = io["t2q"] if side == 0 else io["t2k"]
        if self.tab_cons_ve:
            SY.wait_ge(self.ve.h, self.tab_cons_ve)
        if self.tab_cons_gp:
            SY.wait_ge(self.gp.h, self.tab_cons_gp)
        w1 = self.xload.inc(SY.dma_start(self.t1s[:, :], tq1[:, 512 * t:512 * (t + 1)]), 16)
        w2 = self.xload.inc(SY.dma_start(self.t2s[:, :], tq2[:, 512 * t:512 * (t + 1)]), 16)
        VE.wait_ge(self.xload.h, w2)
        qcol = self.qs4[:, 4 * l + 2 * side:4 * l + 2 * side + 1]
        qpcol = self.qs4[:, 4 * l + 2 * side + 1:4 * l + 2 * side + 2]
        VE.tensor_scalar(self.t1s[:, :], self.t1s[:, :], qcol, None, ALU.mult)
        tab_fold = self.ve.inc(VE.tensor_scalar(self.t2s[:, :], self.t2s[:, :],
                                                qpcol, None, ALU.mult))
        tse = self.invbf_tse
        if side == 1:
            GE.wait_ge(self.pe.h, max(self.free.values(), default=0))
            self.kaug_row64 = self.gp.inc(
                GE.memset(self.AR[64:65, self.KAUG:self.KAUG + 4096], -1.0))
        bBs, bRs = [], []
        for c in range(4):
            bB = self.pb[4 + c]
            self.bank_acq(bB)
            TE.wait_ge(self.se.h, tse)
            mmB = TE.matmul(bB.ap[:, :], self.selbc[:, 128 * c:128 * (c + 1)],
                            self.invbf[0:8, ioff:ioff + 512], start=True, stop=True)
            bB.pe_tag = self.pe.inc(mmB)
            bBs.append(bB)
        for c in range(4):
            bR = self.pb[c]
            self.bank_acq(bR)
            TE.wait_ge(self.ve.h, self.qkbf_ve)
            src_c = self.AR[:, self.QKBF + coff + 512 * c:self.QKBF + coff + 512 * (c + 1)]
            mmR = TE.matmul(bR.ap[:, :], self.rmat[:, :], src_c, start=True, stop=True)
            bR.pe_tag = self.pe.inc(mmR)
            bRs.append(bR)
        last_dst = 0
        aug_tags = []
        for c in range(4):
            bR, bB = bRs[c], bBs[c]
            src_c = self.AR[:, self.QKBF + coff + 512 * c:self.QKBF + coff + 512 * (c + 1)]
            VE.tensor_tensor(self.scr1[:, :], src_c, self.t1s[:, :], ALU.mult)
            self.bank_rd(bR, VE)
            tv3 = self.ve.inc(VE.tensor_tensor(self.scr2[:, :], bR.ap[:, :],
                                               self.t2s[:, :], ALU.mult))
            bR.cons.append((self.ve, tv3))
            VE.tensor_tensor(self.scr1[:, :], self.scr1[:, :], self.scr2[:, :], ALU.add)
            self.bank_rd(bB, VE)
            for hh in range(2):
                h = 2 * c + hh
                dst = self.AR[0:64, aoff + 512 * h:aoff + 512 * (h + 1)]
                last_dst = self.ve.inc(
                    VE.tensor_tensor(dst, self.scr1[64 * hh:64 * hh + 64, :],
                                     bB.ap[64 * hh:64 * hh + 64, :], ALU.mult))
                aug_tags.append(last_dst)
            bB.cons.append((self.ve, last_dst))
        self.free["QKBF" if side == 0 else "QKBF_k"] = self.pe.v
        self.tab_cons_ve = self.ve.v
        self.tab_cons_gp = self.gp.v
        return aug_tags

    def local_norm_rows(self, t, inst):
        VE = self.VE
        for h in range(HL):
            src = self.AR[0:64, self.KAUG + 512 * h:self.KAUG + 512 * (h + 1)]
            VE.reduce_max(self.scr2[0:64, h:h + 1], src, axis=AX.X,
                          apply_absolute_value=True)
        VE.drain()
        for h in range(HL):
            mp = self.Mprev[0:64, 8 * inst + h:8 * inst + h + 1]
            if t == 0:
                VE.tensor_copy(self.Mu[0:64, h:h + 1], self.scr2[0:64, h:h + 1])
            else:
                VE.tensor_tensor(self.Mu[0:64, h:h + 1], self.scr2[0:64, h:h + 1],
                                 mp, ALU.max)
            self.ve.inc(VE.tensor_copy(mp, self.scr2[0:64, h:h + 1]))
        self.mu_ve = self.ve.v
        return self.ve.v

    def _c_pipe(self, h, qv0):
        TE, SE = self.TE, self.SE
        sl = h % 2
        if self.absq_cons[sl]:
            SE.wait_ge(self.pe.h, self.absq_cons[sl])
        SE.wait_ge(self.ve.h, qv0)
        ts = self.se.inc(SE.activation(self.absq[0:64, 512 * sl:512 * (sl + 1)],
                                       self.AR[0:64, self.QAUG + 512 * h:
                                               self.QAUG + 512 * (h + 1)],
                                       AF.Abs))
        br = self.pb[6 + (h % 2)]
        self.bank_acq(br)
        TE.wait_ge(self.se.h, ts)
        TE.wait_ge(self.ve.h, self.mu_ve)
        mm = TE.matmul(br.ap[0:1, :], self.Mu[0:65, h:h + 1],
                       self.absq[0:65, 512 * sl:512 * (sl + 1)], start=True, stop=True)
        br.pe_tag = self.pe.inc(mm)
        self.absq_cons[sl] = self.pe.v
        self.bank_rd(br, SE)
        t2 = self.se.inc(SE.activation(self.AR[64:65, self.QAUG + 512 * h:
                                               self.QAUG + 512 * (h + 1)],
                                       br.ap[0:1, :], AF.Copy))
        br.cons.append((self.se, t2))
        return t2

    def _attn_normalize(self, bO, h, dst, after_ve, bb_base=4):
        TE, VE, SE = self.TE, self.VE, self.SE
        self.bank_rd(bO, VE)
        VE.tensor_copy(self.statf[0:1, 0:512], bO.ap[64:65, :])
        tvs = self.ve.inc(VE.tensor_scalar(self.statf[0:1, 0:512],
                                           self.statf[0:1, 0:512],
                                           1e-18, None, ALU.max))
        bO.cons.append((self.ve, tvs))
        if self.inv_cons:
            SE.wait_ge(self.pe.h, self.inv_cons)
        SE.wait_ge(self.ve.h, tvs)
        SE.activation(self.statf[0:1, 1024:1536], self.statf[0:1, 0:512], AF.Ln)
        ti = self.se.inc(SE.activation(self.invbf[0:1, 0:512],
                                       self.statf[0:1, 1024:1536], AF.Exp,
                                       scale=-1.0))
        bB = self.pb[bb_base + (h % 2)]
        self.bank_acq(bB)
        TE.wait_ge(self.se.h, ti)
        mmb = TE.matmul(bB.ap[0:64, :], self.onesb128[0:1, 0:64],
                        self.invbf[0:1, 0:512], start=True, stop=True)
        bB.pe_tag = self.pe.inc(mmb)
        self.inv_cons = bB.pe_tag
        scr = self.scr1 if h % 2 == 0 else self.scr2
        SE.wait_ge(self.pe.h, self.inv_cons)
        SE.wait_ge(self.ve.h, max(self.scr_cons[h % 2], after_ve))
        SE.wait_ge(self.gp.h, self.norm_gp)
        ts = self.se.inc(SE.activation(scr[0:64, :], bB.ap[0:64, :], AF.Copy))
        bB.cons.append((self.se, ts))
        VE.wait_ge(self.se.h, ts)
        tv2 = self.ve.inc(VE.tensor_tensor(dst, bO.ap[0:64, :], scr[0:64, :], ALU.mult))
        bO.cons.append((self.ve, tv2))
        bO.cons.append((self.se, ts))
        self.scr_cons[h % 2] = tv2

    def local_attn(self, t, inst, q_tags, k_tags):
        TE, VE, SE, GE = self.TE, self.VE, self.SE, self.GE
        qv = self.ve.v
        sv = self.se.v
        gv0 = self.gp.v
        GE.wait_ge(self.ve.h, qv)
        TE.wait_ge(self.gp.h, self.kaug_row64)
        self.norm_gp = gv0
        pend = None
        nh = ATTN_HEADS or HL
        ctags = {0: self._c_pipe(0, q_tags[0])}
        nprev = 4 if t > 0 else 0
        for h in range(nh):
            if h + 1 < nh:
                ctags[h + 1] = self._c_pipe(h + 1, q_tags[h + 1])
            tb = h % 2
            if t > 0:
                if self.ktmp_cons[tb]:
                    GE.wait_ge(self.pe.h, self.ktmp_cons[tb])
                src = self.KLp[64 * (h % 2):64 * (h % 2) + 64,
                               2048 * inst + 512 * (h // 2):2048 * inst + 512 * (h // 2) + 512]
                tg = self.gp.inc(GE.tensor_copy(self.ktmp[0:64, 512 * tb:512 * (tb + 1)], src))
            bO = self.pb[2 + (h % 2)]
            self.bank_acq(bO)
            TE.wait_ge(self.ve.h, max(q_tags[h], k_tags[h]))
            TE.wait_ge(self.se.h, ctags[h])
            pv_started = False
            plist = []
            for kc in range(nprev + 4):
                cur = kc >= nprev
                kci = kc - nprev if cur else kc
                bS = self.pb[kc % 2]
                self.bank_acq(bS)
                if cur:
                    lhs = self.AR[0:65, self.KAUG + 512 * h + 128 * kci:
                                  self.KAUG + 512 * h + 128 * (kci + 1)]
                    lo, hi = 128 * kci, 512
                else:
                    TE.wait_ge(self.gp.h, tg)
                    lhs = self.ktmp[0:65, 512 * tb + 128 * kci:512 * tb + 128 * (kci + 1)]
                    lo, hi = 0, 128 * (kci + 1)
                rhs = self.AR[0:65, self.QAUG + 512 * h + lo:self.QAUG + 512 * h + hi]
                mm = TE.matmul(bS.ap[:, lo:hi], lhs, rhs, start=True, stop=True)
                bS.pe_tag = self.pe.inc(mm)
                if t > 0 and not cur:
                    self.ktmp_cons[tb] = self.pe.v
                if cur and kci == 0:
                    pbuf, pz = self._pfull(), None
                elif cur:
                    pbuf, pz = self.Pcur[:, 512 * (kci - 1):512 * kci], None
                elif kci == 3:
                    pbuf, pz = self._pfull(), None
                else:
                    pbuf, pz = self.Pprev[:, 512 * kci:512 * (kci + 1)], None
                key = pbuf.offset
                if key in self.ptag and self.ptag[key]:
                    SE.wait_ge(self.pe.h, self.ptag[key])
                self.bank_rd(bS, SE)
                te = self.se.inc(SE.activation(pbuf[:, lo:hi], bS.ap[:, lo:hi], AF.Exp))
                bS.cons.append((self.se, te))
                tri = self.tric if cur else self.trip
                blk = 128 * kci
                GE.wait_ge(self.se.h, te)
                tg2 = self.gp.inc(GE.tensor_tensor(pbuf[:, blk:blk + 128],
                                                   pbuf[:, blk:blk + 128], tri[:, :], ALU.mult))
                plist.append((pbuf, key, te, tg2, kc))
            for (pbuf, key, te, tg2, kc) in plist:
                cur = kc >= nprev
                kci = kc - nprev if cur else kc
                if cur:
                    vsl = self.AR[:, self.VAUGC + 520 * kci + 65 * h:
                                  self.VAUGC + 520 * kci + 65 * h + 65]
                else:
                    vsl = self.VLp[:, 2080 * inst + 520 * kci + 65 * h:
                                   2080 * inst + 520 * kci + 65 * h + 65]
                TE.wait_ge(self.gp.h, tg2)
                mm = TE.matmul(bO.ap[0:65, :], vsl, pbuf[:, :],
                               start=(not pv_started), stop=(kc == nprev + 3))
                pv_started = True
                if kc == nprev + 3:
                    bO.pe_tag = self.pe.inc(mm)
            for (pbuf, key, te, tg2, kc) in plist:
                self.ptag[key] = bO.pe_tag
            if pend is not None:
                pbO, ph = pend
                pdst = self.AR[64 * (ph % 2):64 * (ph % 2) + 64,
                               self.OTL + 512 * (ph // 2):self.OTL + 512 * (ph // 2) + 512]
                self._attn_normalize(pbO, ph, pdst, qv)
            pend = (bO, h)
        if pend is not None:
            pbO, ph = pend
            pdst = self.AR[64 * (ph % 2):64 * (ph % 2) + 64,
                           self.OTL + 512 * (ph // 2):self.OTL + 512 * (ph // 2) + 512]
            self._attn_normalize(pbO, ph, pdst, qv)
        for h in range(HL):
            src = self.AR[0:64, self.KAUG + 512 * h:self.KAUG + 512 * (h + 1)]
            dst = self.KLp[64 * (h % 2):64 * (h % 2) + 64,
                           2048 * inst + 512 * (h // 2):2048 * inst + 512 * (h // 2) + 512]
            self.ve.inc(VE.tensor_copy(dst, src))
        tvv = self.ve.inc(VE.tensor_copy(self.VLp[:, 2080 * inst:2080 * (inst + 1)],
                                         self.AR[:, self.VAUGC:self.VAUGC + 2080]))
        self.klp_ve = self.ve.v
        return self.ve.v

    def _pfull(self):
        i = self.pfull_rr % 4
        self.pfull_rr += 1
        return self.Pfull[:, 512 * i:512 * (i + 1)]

    def local_wo(self, t, inst, wbf):
        TE, VE, SE = self.TE, self.VE, self.SE
        l = inst % 2
        otv = self.ve.v
        bsum, bsq = self.pb[6], self.pb[7]
        self.bank_acq(bsum)
        self.bank_acq(bsq)
        TE.wait_ge(self.ve.h, otv)
        VE.wait_ge(self.pe.h, max(self.free.values(), default=0))
        base = 4 * (2 * inst + 1)
        for m in range(NCH):
            b = self.pb[m % 4]
            self.bank_acq(b)
            si, slot, wt = self.ring_use(("wo", l, m), wbf)
            TE.wait_ge(self.wload.h, wt)
            for k in range(4):
                mm = TE.matmul(b.ap[:, :], slot[0:128, 128 * k:128 * (k + 1)],
                               self.AR[:, self.OTL + 512 * k:self.OTL + 512 * (k + 1)],
                               start=(k == 0), stop=(k == 3))
            b.pe_tag = self.pe.inc(mm)
            self.ring_done(si)
            self.bank_rd(b, VE)
            sc = self.S[:, 512 * m:512 * (m + 1)]
            tv = self.ve.inc(VE.tensor_tensor(sc, sc, b.ap[:, :], ALU.add))
            b.cons.append((self.ve, tv))
            dst = self.AR[:, self.HBF + 512 * m:self.HBF + 512 * (m + 1)]
            if m < 4:
                tagm = self.ve.inc(VE.tensor_copy(dst, sc))
            else:
                cc = base + (m - 4)
                VE.tensor_copy(dst[:, 1:512], sc[:, 0:511])
                tagm = self.ve.inc(VE.tensor_copy(dst[:, 0:1], self.CARR[:, cc:cc + 1]))
                self.ve.inc(VE.tensor_copy(self.CARR[:, cc:cc + 1], sc[:, 511:512]))
            sl = m % 2
            if self.sq_cons[sl]:
                SE.wait_ge(self.pe.h, self.sq_cons[sl])
            SE.wait_ge(self.ve.h, tagm)
            tse = self.se.inc(SE.activation(self.sqb[:, 512 * sl:512 * (sl + 1)],
                                            dst, AF.Square))
            TE.wait_ge(self.ve.h, tagm)
            mm1 = TE.matmul(bsum.ap[0:1, :], self.onesbcol[:, :], dst,
                            start=(m == 0), stop=(m == NCH - 1))
            if m == NCH - 1:
                bsum.pe_tag = self.pe.inc(mm1)
            TE.wait_ge(self.se.h, tse)
            mm2 = TE.matmul(bsq.ap[0:1, :], self.onesbcol[:, :],
                            self.sqb[:, 512 * sl:512 * (sl + 1)],
                            start=(m == 0), stop=(m == NCH - 1))
            self.sq_cons[sl] = self.pe.inc(mm2)
            if m == NCH - 1:
                bsq.pe_tag = self.pe.v
        self.free["OTL"] = self.pe.v
        return self.ve.v

    def local_ffn(self, t, inst, wbf, next_shift=False):
        TE, VE, SE, GE = self.TE, self.VE, self.SE, self.GE
        l = inst % 2
        hbg = lambda c: self.AR[:, self.HBF + 512 * c:self.HBF + 512 * (c + 1)]
        h2_tags = self.ln_finish(hbg, hbg, l)
        hv = self.ve.v
        for j in range(NJ):
            wj = JW[j]
            bA = self.pb[2 * (j % 2)]
            bG = self.pb[1 + 2 * (j % 2)]
            self.bank_acq(bA)
            self.bank_acq(bG)
            sa, slota, wta = self.ring_use(("w1a", l, j), wbf)
            sg, slotg, wtg = self.ring_use(("w1g", l, j), wbf)
            TE.wait_ge(self.wload.h, wtg)
            for k in range(NCH):
                if j == 0:
                    TE.wait_ge(self.ve.h, h2_tags[k])
                mma = TE.matmul(bA.ap[0:wj, :], slota[0:128, wj * k:wj * (k + 1)],
                                hbg(k), start=(k == 0), stop=(k == NCH - 1))
            bA.pe_tag = self.pe.inc(mma)
            for k in range(NCH):
                mmg = TE.matmul(bG.ap[0:wj, :], slotg[0:128, wj * k:wj * (k + 1)],
                                hbg(k), start=(k == 0), stop=(k == NCH - 1))
            bG.pe_tag = self.pe.inc(mmg)
            self.ring_done(sa)
            self.ring_done(sg)
            gsl = j % 2
            if self.gel_cons[gsl]:
                SE.wait_ge(self.ve.h, self.gel_cons[gsl])
            self.bank_rd(bG, SE)
            ts = self.se.inc(SE.activation(self.gelsb[0:wj, 512 * gsl:512 * gsl + 512],
                                           bG.ap[0:wj, :], AF.Gelu,
                                           bias=self.b1g[0:wj, 22 * l + j:22 * l + j + 1]))
            bG.cons.append((self.se, ts))
            self.bank_rd(bA, VE)
            VE.wait_ge(self.se.h, ts)
            tv = self.ve.inc(VE.scalar_tensor_tensor(
                self.AR[0:wj, self.AGOF + 512 * j:self.AGOF + 512 * (j + 1)],
                bA.ap[0:wj, :], self.b1a[0:wj, 22 * l + j:22 * l + j + 1],
                self.gelsb[0:wj, 512 * gsl:512 * gsl + 512], ALU.add, ALU.mult))
            bA.cons.append((self.ve, tv))
            self.gel_cons[gsl] = tv
        agv = self.ve.v
        for m in range(NCH):
            self.bank_acq(self.pb[m])
        TE.wait_ge(self.ve.h, agv)
        for j in range(NJ):
            kr = JW[j]
            si, slot, wt = self.ring_use(("w2", l, j), wbf)
            TE.wait_ge(self.wload.h, wt)
            for m in range(NCH):
                mm = TE.matmul(self.pb[m].ap[:, :], slot[0:kr, 128 * m:128 * (m + 1)],
                               self.AR[0:kr, self.AGOF + 512 * j:self.AGOF + 512 * (j + 1)],
                               start=(j == 0), stop=(j == NJ - 1))
                if j == NJ - 1:
                    self.pb[m].pe_tag = self.pe.inc(mm)
            if j < NJ - 1:
                self.pe.inc(mm)
            self.ring_done(si)
        self.free["AG"] = self.pe.v
        out_tags = []
        g0 = max(self.free.values(), default=0)
        VE.wait_ge(self.pe.h, g0)
        base = 4 * (2 * (inst + 1) + 0)
        for m in range(NCH):
            b = self.pb[m]
            self.bank_rd(b, VE)
            sc = self.S[:, 512 * m:512 * (m + 1)]
            tv = self.ve.inc(VE.scalar_tensor_tensor(sc, b.ap[:, :],
                                                     self.b2c[:, 8 * l + m:8 * l + m + 1],
                                                     sc, ALU.add, ALU.add))
            b.cons.append((self.ve, tv))
            if next_shift:
                dst = self.AR[:, self.HBF + 512 * m:self.HBF + 512 * (m + 1)]
                if m < 4:
                    out_tags.append(self.ve.inc(VE.tensor_copy(dst, sc)))
                else:
                    cc = base + (m - 4)
                    VE.tensor_copy(dst[:, 1:512], sc[:, 0:511])
                    out_tags.append(self.ve.inc(
                        VE.tensor_copy(dst[:, 0:1], self.CARR[:, cc:cc + 1])))
                    self.ve.inc(VE.tensor_copy(self.CARR[:, cc:cc + 1], sc[:, 511:512]))
            else:
                xc = self.X[:, 512 * m:512 * (m + 1)]
                VE.scalar_tensor_tensor(xc, sc, 0.5, xc, ALU.mult, ALU.add)
                xb = self.AR[:, self.QKBF + 512 * m:self.QKBF + 512 * (m + 1)]
                out_tags.append(self.ve.inc(VE.tensor_copy(xb, xc)))
                self.xbf_ready = True
        return out_tags

    @staticmethod
    def _bq_pos(h):
        return 64 * ((h // 4) % 2), (h % 4) + 4 * (h // 8)

    def phaseB(self, t, wbf):
        TE, VE, SE, GE = self.TE, self.VE, self.SE, self.GE
        g0 = max(self.free.values(), default=0)
        VE.wait_ge(self.pe.h, g0)
        xg = lambda c: self.X[:, 512 * c:512 * (c + 1)]
        yg = lambda c: self.AR[:, self.HBF + 512 * c:self.HBF + 512 * (c + 1)]
        y_tags = self.layernorm(xg, yg, None, out_f32=False, xbf_off=self.QKBF,
                                src_tags=self.x_tags)
        yv = self.ve.v
        hb = lambda k: self.AR[:, self.HBF + 512 * k:self.HBF + 512 * (k + 1)]
        for m in range(NCH):
            b = self.pb[m % 4]
            self.bank_acq(b)
            si, slot, wt = self.ring_use(("Bq", m), wbf)
            TE.wait_ge(self.wload.h, wt)
            for k in range(NCH):
                if m == 0:
                    TE.wait_ge(self.ve.h, y_tags[k])
                mm = TE.matmul(b.ap[:, :], slot[0:128, 128 * k:128 * (k + 1)], hb(k),
                               start=(k == 0), stop=(k == NCH - 1))
            b.pe_tag = self.pe.inc(mm)
            self.bank_rd(b, VE)
            for hh in range(2):
                h = 2 * m + hh
                rh, cs = self._bq_pos(h)
                dst = self.AR[rh:rh + 64, self.QKBF + 512 * cs:self.QKBF + 512 * (cs + 1)]
                tv = self.ve.inc(VE.tensor_scalar(dst, b.ap[64 * hh:64 * hh + 64, :],
                                                  0.125, None, ALU.mult))
            b.cons.append((self.ve, tv))
            self.ring_done(si)
        for m in range(2):
            b = self.pb[4 + m]
            self.bank_acq(b)
            si, slot, wt = self.ring_use(("Bk", m), wbf)
            TE.wait_ge(self.wload.h, wt)
            TE.wait_ge(self.ve.h, yv)
            for k in range(NCH):
                mm = TE.matmul(b.ap[:, :], slot[0:128, 128 * k:128 * (k + 1)], hb(k),
                               start=(k == 0), stop=(k == NCH - 1))
            b.pe_tag = self.pe.inc(mm)
            self.bank_rd(b, VE)
            tv = self.ve.inc(VE.tensor_copy(self.KB[:, 4096 * m + 512 * t:4096 * m + 512 * (t + 1)],
                                            b.ap[:, :]))
            b.cons.append((self.ve, tv))
            self.ring_done(si)
        for sub in range(4):
            self.bank_acq(self.pb[4 + sub])
        for k in range(NCH):
            si, slot, wt = self.ring_use(("Bv", k), wbf)
            TE.wait_ge(self.wload.h, wt)
            TE.wait_ge(self.ve.h, yv)
            for sub in range(4):
                mm = TE.matmul(self.pb[4 + sub].ap[:, 0:256],
                               hb(k)[:, 128 * sub:128 * (sub + 1)], slot[0:128, 0:256],
                               start=(k == 0), stop=(k == NCH - 1))
                if k == NCH - 1:
                    self.pb[4 + sub].pe_tag = self.pe.inc(mm)
            self.ring_done(si)
        for sub in range(4):
            b = self.pb[4 + sub]
            self.bank_rd(b, VE)
            ci = 4 * t + sub
            dst4 = self.VB[:, 260 * ci:260 * (ci + 1)].rearrange("p (g d) -> p g d", d=65)[:, :, 0:64]
            src4 = b.ap[:, 0:256].rearrange("p (g d) -> p g d", d=64)
            tv = self.ve.inc(VE.tensor_copy(dst4, src4))
            b.cons.append((self.ve, tv))
        qkv_ve = self.ve.v
        TEw = TE
        nK = 4 * (t + 1)
        self.norm_gp = self.gp.v
        pend = None
        for h in range(HQ):
            g = h // 4
            rh, cs = self._bq_pos(h)
            qs = self.AR[rh:rh + 64, self.QKBF + 512 * cs:self.QKBF + 512 * (cs + 1)]
            bO = self.pb[4 + (h % 2)]
            self.bank_acq(bO)
            plist = []
            TE.wait_ge(self.ve.h, qkv_ve)
            K = 0
            while K < 4 * t:
                pi = (K // 2) % 2
                bSa, bSb = self.pb[2 * pi], self.pb[2 * pi + 1]
                self.bank_acq(bSa)
                self.bank_acq(bSb)
                for j, bS in ((0, bSa), (1, bSb)):
                    lhs = self.KB[rh:rh + 64,
                                  4096 * (g // 2) + 128 * (K + j):4096 * (g // 2) + 128 * (K + j + 1)]
                    mm = TE.matmul(bS.ap[:, :], lhs, qs, start=True, stop=True)
                bSa.pe_tag = bSb.pe_tag = self.pe.inc(mm)
                pairap = self.PBIG[:, 1024 * pi:1024 * (pi + 1)]
                pslot = self.pfull_rr % 2
                self.pfull_rr += 1
                pbuf = self.Pfull[:, 1024 * pslot:1024 * (pslot + 1)]
                k0 = pbuf[:, 0:512].offset
                k1 = pbuf[:, 512:1024].offset
                mx = max(self.ptag.get(k0, 0), self.ptag.get(k1, 0))
                if mx:
                    SE.wait_ge(self.pe.h, mx)
                SE.wait_ge(self.pe.h, bSa.pe_tag)
                te = self.se.inc(SE.activation(pbuf, pairap, AF.Exp))
                bSa.cons.append((self.se, te))
                bSb.cons.append((self.se, te))
                TE.wait_ge(self.se.h, te)
                for j in range(2):
                    vsl = self.VB[:, 260 * (K + j) + 65 * g:260 * (K + j) + 65 * g + 65]
                    mmv = TE.matmul(bO.ap[0:65, :], vsl, pbuf[:, 512 * j:512 * (j + 1)],
                                    start=(K + j == 0), stop=(K + j == nK - 1))
                plist.append((k0, k1))
                K += 2
            for kci in range(4):
                K = 4 * t + kci
                bS = self.pb[K % 2]
                self.bank_acq(bS)
                lhs = self.KB[rh:rh + 64, 4096 * (g // 2) + 128 * K:4096 * (g // 2) + 128 * (K + 1)]
                lo = 128 * kci
                mm = TE.matmul(bS.ap[:, lo:512], lhs, qs[:, lo:512], start=True, stop=True)
                bS.pe_tag = self.pe.inc(mm)
                if kci > 0:
                    pbuf = self.Pcur[:, 512 * (kci - 1):512 * kci]
                else:
                    pslot = self.pfull_rr % 2
                    self.pfull_rr += 1
                    pbuf = self.Pfull[:, 1024 * pslot:1024 * pslot + 512]
                key = pbuf.offset
                mx = self.ptag.get(key, 0)
                if mx:
                    SE.wait_ge(self.pe.h, mx)
                self.bank_rd(bS, SE)
                te = self.se.inc(SE.activation(pbuf[:, lo:512], bS.ap[:, lo:512], AF.Exp))
                bS.cons.append((self.se, te))
                tg2 = 0
                if kci > 0:
                    blk = 128 * kci
                    GE.wait_ge(self.se.h, te)
                    tg2 = self.gp.inc(GE.tensor_tensor(pbuf[:, blk:blk + 128],
                                                       pbuf[:, blk:blk + 128],
                                                       self.tric[:, :], ALU.mult))
                else:
                    blk = 0
                    GE.wait_ge(self.se.h, te)
                    tg2 = self.gp.inc(GE.tensor_tensor(pbuf[:, 0:128],
                                                       pbuf[:, 0:128],
                                                       self.tric[:, :], ALU.mult))
                TE.wait_ge(self.gp.h, tg2)
                vsl = self.VB[:, 260 * K + 65 * g:260 * K + 65 * g + 65]
                mmv = TE.matmul(bO.ap[0:65, :], vsl, pbuf[:, :],
                                start=(K == 0), stop=(K == nK - 1))
                if K == nK - 1:
                    bO.pe_tag = self.pe.inc(mmv)
                plist.append((key, None))
            for (k0, k1) in plist:
                self.ptag[k0] = bO.pe_tag
                if k1 is not None:
                    self.ptag[k1] = bO.pe_tag
            if pend is not None:
                pbO, ph = pend
                pdst = self.AR[64 * (ph % 2):64 * (ph % 2) + 64,
                               self.QAUG + 512 * (ph // 2):self.QAUG + 512 * (ph // 2) + 512]
                self._attn_normalize(pbO, ph, pdst, qkv_ve, bb_base=6)
            pend = (bO, h)
        if pend is not None:
            pbO, ph = pend
            pdst = self.AR[64 * (ph % 2):64 * (ph % 2) + 64,
                           self.QAUG + 512 * (ph // 2):self.QAUG + 512 * (ph // 2) + 512]
            self._attn_normalize(pbO, ph, pdst, qkv_ve, bb_base=6)
        otv = self.ve.v
        self.x_tags = []
        for m in range(NCH):
            b = self.pb[4 + (m % 4)]
            self.bank_acq(b)
            si, slot, wt = self.ring_use(("Bo", m), wbf)
            TE.wait_ge(self.wload.h, wt)
            TE.wait_ge(self.ve.h, otv)
            for k in range(NCH):
                mm = TE.matmul(b.ap[:, :], slot[0:128, 128 * k:128 * (k + 1)],
                               self.AR[:, self.QAUG + 512 * k:self.QAUG + 512 * (k + 1)],
                               start=(k == 0), stop=(k == NCH - 1))
            b.pe_tag = self.pe.inc(mm)
            self.ring_done(si)
            self.bank_rd(b, VE)
            xc = self.X[:, 512 * m:512 * (m + 1)]
            tv = self.ve.inc(VE.tensor_tensor(xc, xc, b.ap[:, :], ALU.add))
            b.cons.append((self.ve, tv))
            self.x_tags.append(tv)
        self.free["OTB"] = self.pe.v
        return self.ve.v

    def ffn_phase(self, t, phase, wbf):
        VE = self.VE
        lim = _STAGES.index(STAGE_LIMIT)
        g0 = max(self.free.values(), default=0)
        VE.wait_ge(self.pe.h, g0)
        xg = lambda c: self.X[:, 512 * c:512 * (c + 1)]
        sg = lambda c: self.S[:, 512 * c:512 * (c + 1)]
        s_tags = self.layernorm(xg, sg, None, out_f32=True, xbf_off=self.QKBF,
                                src_tags=self.x_tags)
        if phase == 0:
            self.tap("A_S0", self.S[:, :])
        if lim < _STAGES.index("shift"):
            return self.ve.v
        h_tags = None
        for li in range(2):
            inst = 2 * phase + li
            if h_tags is None:
                g0 = max(self.free.values(), default=0)
                VE.wait_ge(self.pe.h, g0)
                h_tags = self.shift_cast(inst, 0, sg, self.HBF)
            if phase == 0 and li == 0:
                self.tap("A_h0", self.AR[:, self.HBF:self.HBF + 4096])
            if lim < _STAGES.index("qkvv"):
                return self.ve.v
            self.local_qkv(t, inst, wbf, v_only=(lim == _STAGES.index("qkvv")),
                           h_tags=h_tags)
            if lim < _STAGES.index("qkproc"):
                return self.ve.v
            k_tags = self.local_qk_process(t, inst, 1, self.io)
            self.local_norm_rows(t, inst)
            q_tags = self.local_qk_process(t, inst, 0, self.io)
            if lim < _STAGES.index("norm"):
                return self.ve.v
            if phase == 0 and li == 0:
                self.tap("A_qaug", self.AR[0:65, self.QAUG:self.QAUG + 4096])
                self.tap("A_kaug", self.AR[0:65, self.KAUG:self.KAUG + 4096])
                self.tap("A_vaug", self.AR[:, self.VAUGC:self.VAUGC + 2080])
            if lim < _STAGES.index("attn"):
                return self.ve.v
            self.local_attn(t, inst, q_tags, k_tags)
            if phase == 0 and li == 0:
                self.tap("A_oT", self.AR[:, self.OTL:self.OTL + 2048])
                self.tap("A_Pf", self.Pfull[:, :])
                self.tap("A_Pc", self.Pcur[:, :])
                self.tap("A_dbg1", self.scr1[:, :])
                self.tap("A_dbg2", self.statf[:, :])
            if lim < _STAGES.index("wo"):
                return self.ve.v
            self.local_wo(t, inst, wbf)
            if phase == 0 and li == 0:
                self.tap("A_S1", self.S[:, :])
            if lim < _STAGES.index("ffn"):
                return self.ve.v
            h_tags = self.local_ffn(t, inst, wbf, next_shift=(li == 0))
            if phase == 0 and li == 0:
                self.tap("A_AG", self.AR[:, self.AGOF:self.AGOF + 11264])
                self.tap("A_S2", self.S[:, :])
        self.x_tags = h_tags
        if phase == 0:
            self.tap("A_X", self.X[:, :])
        return self.ve.v

    def emit_tile(self, t, io):
        self.io = io
        lim = _STAGES.index(STAGE_LIMIT)
        VE, SE, GE, SY = self.VE, self.SE, self.GE, self.SY
        if t > 0:
            SE.wait_ge(self.osem.h, self.osem.v)
        for c in range(NCH):
            self.xload.inc(SE.dma_start(self.X[:, 512 * c:512 * (c + 1)],
                                        io["x_in"][128 * c:128 * (c + 1), 512 * t:512 * (t + 1)]), 16)
        VE.wait_ge(self.xload.h, self.xload.v)
        self.TE.wait_ge(self.xload.h, self.xload.v)
        SE.wait_ge(self.xload.h, self.xload.v)
        self.x_tags = None
        self.xbf_ready = False
        if lim >= _STAGES.index("ln"):
            self.ffn_phase(t, 0, io["wbf"])
        if lim >= _STAGES.index("B"):
            self.phaseB(t, io["wbf"])
            if t == _TAP_AT_TILE:
                self.tap("B_X", self.X[:, :])
        if lim >= _STAGES.index("full"):
            self.ffn_phase(t, 1, io["wbf"])
        vfin = self.ve.v
        GE.wait_ge(self.ve.h, vfin)
        for c in range(NCH):
            self.osem.inc(GE.dma_start(io["out"][128 * c:128 * (c + 1), 512 * t:512 * (t + 1)],
                                       self.X[:, 512 * c:512 * (c + 1)]), 16)

    def finish(self):
        self.SY.wait_ge(self.osem.h, self.osem.v)
        self.GE.wait_ge(self.osem.h, self.osem.v)



def build_nc(n_tiles=None, taps=()):
    n_tiles = n_tiles or (DEV_NT or NT)
    nc = bass.Bass("TRN2", target_bir_lowering=False)
    io = {}
    io["x_in"] = nc.dram_tensor("x_in", [D, SEQ], F32, kind="ExternalInput")
    io["out"] = nc.dram_tensor("out", [D, SEQ], F32, kind="ExternalOutput")
    _wf_len = _WFLAT_LEN[0]
    io["wflat"] = nc.dram_tensor("wflat", [_wf_len], F32, kind="ExternalInput")
    io["wbf"] = nc.dram_tensor("wbf", [_wf_len], BF16)
    for n, s in (("t1q", [128, SEQ]), ("t2q", [128, SEQ]), ("t1k", [128, SEQ]),
                 ("t2k", [128, SEQ])):
        io[n] = nc.dram_tensor(n, s, F32, kind="ExternalInput")
    for n, s in (("rmat", [128, 128]), ("tric", [128, 128]), ("trip", [128, 128]),
                 ("selq", [128, 32]), ("selbc", [8, 512])):
        io[n] = nc.dram_tensor(n, s, BF16, kind="ExternalInput")
    for n, s in (("qs4", [128, 8]), ("b1a", [128, 44]), ("b1g", [128, 44]),
                 ("b2c", [128, 16]), ("onesrow", [1, 512]),
                 ("row_ones128", [1, 128]), ("row_neg128", [1, 128]),
                 ("row_ones64", [1, 64]), ("col_ones128", [128, 1]),
                 ("col_ones64", [64, 1]), ("gbrow_g", [1, 2048]), ("gbrow_b", [1, 2048])):
        io[n] = nc.dram_tensor(n, s, F32, kind="ExternalInput")
    tap_handles = {}
    for name in taps:
        shp = TAPS_DEF[name]
        tap_handles[name] = nc.dram_tensor("tap_" + name, list(shp), F32, kind="ExternalOutput")

    offs, marks = _OFFS[0], _MARKS[0]
    ctx = contextlib.ExitStack()
    mk = _MK(nc, ctx, offs, marks, n_tiles, tap_handles)
    with ctx, nc.Block() as block:
        @block.sync
        def _(sy):
            mk.emit_init(io)
            for t in range(n_tiles):
                mk.emit_tile(t, io)
            mk.finish()
    return nc


_WFLAT_LEN = [None]
_OFFS = [None]
_MARKS = [None]
_BUILD = {}


def _get_host_arrays(inputs):
    wflat, offs, marks = _flatten_weights(inputs)
    _WFLAT_LEN[0] = wflat.size
    _OFFS[0] = offs
    _MARKS[0] = marks
    tabs = _host_tables()
    sm = _host_smalls(inputs)
    gb = sm.pop("gbrows")
    consts = dict(
        wflat=wflat,
        t1q=tabs["t1q"], t2q=tabs["t2q"], t1k=tabs["t1k"], t2k=tabs["t2k"],
        rmat=np.asarray(tabs["rmat"]), tric=np.asarray(tabs["tric"]),
        trip=np.asarray(tabs["trip"]), selq=np.asarray(tabs["selq"]),
        selbc=np.asarray(tabs["selbc"]),
        qs4=sm["qs4"], b1a=sm["b1a"], b1g=sm["b1g"], b2c=sm["b2c"],
        onesrow=sm["onesrow"],
        row_ones128=np.ones((1, 128), np.float32),
        row_neg128=-np.ones((1, 128), np.float32),
        row_ones64=np.ones((1, 64), np.float32),
        col_ones128=np.ones((128, 1), np.float32),
        col_ones64=np.ones((64, 1), np.float32),
        gbrow_g=gb[0:1, :], gbrow_b=gb[1:2, :],
    )
    return consts


def kernel(**inputs):
    inputs = {k: np.asarray(v) for k, v in inputs.items()}
    consts = _get_host_arrays(inputs)
    key = ("full", DEV_NT)
    if key not in _BUILD:
        _BUILD[key] = build_nc()
    nc = _BUILD[key]
    x = inputs["L0_attn_hidden_states"]
    in_maps = []
    for b in range(8):
        m = dict(consts)
        m["x_in"] = np.ascontiguousarray(x[b].T.astype(np.float32))
        in_maps.append(m)
    res = run_bass_kernel_spmd(nc, in_maps, core_ids=list(range(8)))
    outs = [np.asarray(r["out"]).T for r in res.results]
    return np.stack(outs, axis=0).astype(np.float32)


# version 13
# speedup vs baseline: 1.0143x; 1.0074x over previous
import contextlib
import math
import numpy as np
import ml_dtypes
import concourse.bass as bass
import concourse.mybir as mybir
from concourse.bass_utils import run_bass_kernel_spmd

F32 = mybir.dt.float32
BF16 = mybir.dt.bfloat16
AF = mybir.ActivationFunctionType
ALU = mybir.AluOpType
AX = mybir.AxisListType
bfd = ml_dtypes.bfloat16

SEQ, D, TILE, NT = 4096, 1024, 512, 8
NCH = 8
HID, NJ = 2730, 22
JW = [128] * 21 + [42]
HL, HD = 8, 64
HQ, HKV = 16, 4
WIN = 512
DEV_NT = None
_STAGES = ["init", "ln", "shift", "qkvv", "qkv", "qkproc", "norm", "attn", "wo", "ffn",
           "A", "B", "full"]
STAGE_LIMIT = "full"
ATTN_HEADS = None


def _flatten_weights(W):
    parts, offs = [], {}
    pos = 0

    def add(key, arr):
        nonlocal pos
        a = np.ascontiguousarray(arr, dtype=np.float32)
        offs[key] = (pos, a.shape)
        parts.append(a.reshape(-1))
        pos += a.size

    def lhsT_groups(Wm, kk):
        return Wm.reshape(kk, 128, Wm.shape[1]).transpose(1, 0, 2).reshape(128, -1)

    marks = {}
    for l in range(2):
        for m in range(8):
            add(("qkv", l, m), lhsT_groups(W["L0_ffn_wqkv"][l][:, m * 128:(m + 1) * 128], 8))
        for k in range(8):
            add(("qkvv", l, k), W["L0_ffn_wqkv"][l][k * 128:(k + 1) * 128, 1024:1536])
        marks[("conv", "qkv", l)] = pos
        for m in range(8):
            add(("wo", l, m), lhsT_groups(W["L0_ffn_wo"][l][:, m * 128:(m + 1) * 128], 4))
        marks[("conv", "wo", l)] = pos
        for j in range(NJ):
            wj = JW[j]
            add(("w1a", l, j), lhsT_groups(W["L0_ffn_w1"][l][:, j * 128:j * 128 + wj], 8))
            add(("w1g", l, j), lhsT_groups(W["L0_ffn_w1"][l][:, 2730 + j * 128:2730 + j * 128 + wj], 8))
        marks[("conv", "w1", l)] = pos
        for j in range(NJ):
            add(("w2", l, j), W["L0_ffn_w2"][l][j * 128:j * 128 + JW[j], :])
        marks[("conv", "w2", l)] = pos
    for m in range(8):
        add(("Bq", m), lhsT_groups(W["L0_attn_W_qkv"][:, m * 128:(m + 1) * 128], 8))
    for m in range(2):
        add(("Bk", m), lhsT_groups(W["L0_attn_W_qkv"][:, 1024 + m * 128:1024 + (m + 1) * 128], 8))
    for k in range(8):
        add(("Bv", k), W["L0_attn_W_qkv"][k * 128:(k + 1) * 128, 1280:1536])
    marks[("conv", "Bqkv", 0)] = pos
    for m in range(8):
        add(("Bo", m), lhsT_groups(W["L0_attn_W_o"][:, m * 128:(m + 1) * 128], 8))
    marks[("conv", "Bo", 0)] = pos
    return np.concatenate(parts), offs, marks


def _host_tables():
    t = np.arange(SEQ, dtype=np.float64)
    i = np.arange(32, dtype=np.float64)
    inv_freq = 1.0 / 10000.0 ** (2 * i / 64.0)
    freqs = t[:, None] * inv_freq[None, :]
    base = (2 * i + 0.4 * 64) / (1.4 * 64)
    power = (t - SEQ // 2) / float(WIN // 2)
    scale = base[None, :] ** power[:, None]
    cos64 = np.cos(np.concatenate([freqs, freqs], 1))
    sin64 = np.sin(np.concatenate([freqs, freqs], 1))
    sc64 = np.concatenate([scale, scale], 1)
    def tab(x):
        return np.tile(x.T.astype(np.float32), (2, 1))
    T1q = tab(8.0 * cos64 * sc64)
    T2q = tab(8.0 * sin64 * sc64)
    T1k = tab(cos64 / sc64)
    T2k = tab(sin64 / sc64)

    R = np.zeros((128, 128), np.float32)
    for m in range(128):
        d = m % 64
        h64 = m - d
        if d < 32:
            R[h64 + d + 32, m] = -1.0
        else:
            R[h64 + d - 32, m] = 1.0

    kk, qq = np.meshgrid(np.arange(128), np.arange(128), indexing="ij")
    triC = (kk <= qq).astype(np.float32)
    triP = (kk >= qq).astype(np.float32)

    selq = np.zeros((128, 32), np.float32)
    selbc = np.zeros((8, 4 * 128), np.float32)
    for c in range(4):
        for p in range(128):
            h = 2 * c + p // 64
            selq[p, 8 * c + h] = 1.0
            selbc[h, 128 * c + p] = 1.0
    return dict(t1q=T1q, t2q=T2q, t1k=T1k, t2k=T2k,
                rmat=R.astype(bfd), tric=triC.astype(bfd), trip=triP.astype(bfd),
                selq=selq.astype(bfd), selbc=selbc.astype(bfd))


def _host_smalls(W):
    qs = np.zeros((128, 8), np.float32)
    for l in range(2):
        q_ = W["L0_ffn_q_scale"][l].astype(np.float32)
        k_ = W["L0_ffn_k_scale"][l].astype(np.float32)
        qp = np.concatenate([q_[32:], q_[:32]])
        kp = np.concatenate([k_[32:], k_[:32]])
        qs[:, l * 4 + 0] = np.tile(q_, 2)
        qs[:, l * 4 + 1] = np.tile(qp, 2)
        qs[:, l * 4 + 2] = np.tile(k_, 2)
        qs[:, l * 4 + 3] = np.tile(kp, 2)
    b1a = np.zeros((128, 44), np.float32)
    b1g = np.zeros((128, 44), np.float32)
    for l in range(2):
        for j in range(NJ):
            wj = JW[j]
            b1a[:wj, l * 22 + j] = W["L0_ffn_b1"][l][j * 128:j * 128 + wj]
            b1g[:wj, l * 22 + j] = W["L0_ffn_b1"][l][2730 + j * 128:2730 + j * 128 + wj]
    b2c = np.zeros((128, 16), np.float32)
    for l in range(2):
        b2c[:, l * 8:(l + 1) * 8] = W["L0_ffn_b2"][l].reshape(8, 128).T
    gb = np.zeros((2, 2048), np.float32)
    for l in range(2):
        gb[0, l * 1024:(l + 1) * 1024] = W["L0_ffn_ln_g"][l]
        gb[1, l * 1024:(l + 1) * 1024] = W["L0_ffn_ln_b"][l]
    misc = np.zeros((2, 512), np.float32)
    misc[0, :] = 1.0
    return dict(qs4=qs, b1a=b1a, b1g=b1g, b2c=b2c, gbrows=gb, onesrow=misc[0:1, :])



class _Sem:
    __slots__ = ("h", "v")

    def __init__(self, nc, ctx, name):
        self.h = ctx.enter_context(nc.semaphore(name))
        self.v = 0

    def inc(self, instr, n=1):
        instr.then_inc(self.h, n)
        self.v += n
        return self.v


class _Bank:
    __slots__ = ("ap", "pe_tag", "cons")

    def __init__(self, ap):
        self.ap = ap
        self.pe_tag = 0
        self.cons = []


TAPS_DEF = {
    "A_S0": (128, 4096), "A_h0": (128, 4096), "A_qaug": (65, 4096),
    "A_kaug": (65, 4096), "A_oT": (128, 2048), "A_S1": (128, 4096),
    "A_h2n": (128, 4096), "A_S2": (128, 4096), "A_X": (128, 4096),
    "B_ybf": (128, 4096), "B_qB": (128, 4096), "B_oTB": (128, 4096),
    "A_Pf": (128, 1536), "A_Pc": (128, 1536),
    "A_dbg1": (128, 512), "A_dbg2": (1, 2048),
    "B_X": (128, 4096), "A_AG": (128, 11264), "A_vaug": (128, 2080),
}
_TAP_AT_TILE = 0


class _MK:
    def __init__(self, nc, ctx, offs, marks, n_tiles, taps):
        self.nc = nc
        self.n_tiles = n_tiles
        self.taps = dict(taps or {})
        self.TE, self.VE, self.SE = nc.tensor, nc.vector, nc.scalar
        self.GE, self.SY = nc.gpsimd, nc.sync
        self.offs = offs
        self.units = []
        prev = 0
        for k, end in marks.items():
            self.units.append((k, prev, end))
            prev = end
        self.conv_tag = {}

        sem = lambda n: _Sem(nc, ctx, n)
        self.pe, self.ve, self.se, self.gp = sem("pe"), sem("ve"), sem("se"), sem("gp")
        self.wconv, self.wload, self.xload, self.osem = sem("wconv"), sem("wload"), sem("xload"), sem("osem")

        sb = lambda n, s, d: ctx.enter_context(nc.sbuf_tensor("s_" + n, s, d))
        ps = lambda n, s: ctx.enter_context(nc.psum_tensor(n, s, F32))
        self.PBIG = ps("PBIG", [128, 4096])
        self.pb = [_Bank(self.PBIG[:, 512 * i:512 * (i + 1)]) for i in range(8)]
        self.X = sb("X", [128, 4096], F32)
        self.S = sb("S", [128, 4096], F32)
        self.t1s = sb("t1s", [128, 512], F32)
        self.t2s = sb("t2s", [128, 512], F32)
        self.scr1 = sb("scr1", [128, 512], F32)
        self.scr2 = sb("scr2", [128, 512], F32)
        self.statf = sb("statf", [1, 2048], F32)
        self.CARR = sb("CARR", [128, 32], F32)
        self.qs4 = sb("qs4", [128, 8], F32)
        self.b1a = sb("b1a", [128, 44], F32)
        self.b1g = sb("b1g", [128, 44], F32)
        self.b2c = sb("b2c", [128, 16], F32)
        self.kn2 = sb("kn2", [1, 64], F32)
        self.Mprev = sb("Mprev", [64, 32], BF16)
        self.Mu = sb("Mu", [65, 8], BF16)
        self.absq = sb("absq", [65, 1024], BF16)
        self.one64f = sb("one64f", [1, 64], F32)
        self.KB = sb("KB", [128, 8192], BF16)
        self.VB = sb("VB", [128, 8320], BF16)
        self.KLp = sb("KLp", [128, 8192], BF16)
        self.VLp = sb("VLp", [128, 8320], BF16)
        self.ktmp = sb("ktmp", [65, 1024], BF16)
        self.Pfull = sb("Pfull", [128, 2048], BF16)
        self.Pprev = sb("Pprev", [128, 1536], BF16)
        self.Pcur = sb("Pcur", [128, 1536], BF16)
        self.grow = sb("grow", [1, 2048], BF16)
        self.gnb = sb("gnb", [2, 2048], BF16)
        self.statbf = sb("statbf", [2, 1024], BF16)
        self.onesbcol = sb("onesbcol", [128, 1], BF16)
        self.ones64col = sb("ones64col", [64, 1], BF16)
        self.onesb128 = sb("onesb128", [1, 128], BF16)
        self.negb128 = sb("negb128", [1, 128], BF16)
        self.selq = sb("selq", [128, 32], BF16)
        self.selbc = sb("selbc", [8, 512], BF16)
        self.rmat = sb("rmat", [128, 128], BF16)
        self.tric = sb("tric", [128, 128], BF16)
        self.trip = sb("trip", [128, 128], BF16)
        self.invbf = sb("invbf", [8, 1024], BF16)
        self.gelsb = sb("gelsb", [128, 1024], BF16)
        self.sqb = sb("sqb", [128, 1024], BF16)
        self.ring = sb("ring", [128, 10240], BF16)
        self.AR = sb("AR", [128, 20512], BF16)
        self.HBF, self.QKBF, self.QAUG, self.KAUG = 0, 4096, 8192, 12288
        self.VAUGC, self.OTL, self.AGOF = 16384, 18464, 4096
        self.rslot = [{"wtag": 0, "free": 0} for _ in range(10)]
        self.rr = 0
        self.free = {}
        self.ptag = {}
        self.sq_cons = [0, 0]
        self.gel_cons = [0, 0]
        self.ktmp_cons = [0, 0]
        self.pfull_rr = 0
        self.tab_cons_ve = 0
        self.tab_cons_gp = 0
        self.kaug_row64 = 0
        self.inv_cons = 0
        self.scr_cons = [0, 0]
        self.absq_cons = [0, 0]

    def unit_of(self, key):
        off = self.offs[key][0]
        for k, s, e in self.units:
            if s <= off < e:
                return k
        raise KeyError(key)

    def bank_acq(self, b):
        for (sem, tag) in b.cons:
            self.TE.wait_ge(sem.h, tag)
        b.cons = []

    def bank_rd(self, b, eng):
        eng.wait_ge(self.pe.h, b.pe_tag)

    def ring_use(self, key, wbf):
        i = self.rr
        self.rr = (self.rr + 1) % len(self.rslot)
        sl = self.rslot[i]
        if sl["free"] > 0:
            self.SY.wait_ge(self.pe.h, sl["free"])
        ct = self.conv_tag.get(self.unit_of(key))
        if ct:
            self.SY.wait_ge(self.wconv.h, ct)
        off, (P, C) = self.offs[key]
        src = wbf[off:off + P * C].rearrange("(p c) -> p c", p=P)
        dst = self.ring[0:P, 1024 * i:1024 * i + C]
        self.wload.inc(self.SY.dma_start(dst, src), 16)
        sl["wtag"] = self.wload.v
        return i, self.ring[0:P, 1024 * i:1024 * i + 1024], sl["wtag"]

    def ring_done(self, i):
        self.rslot[i]["free"] = self.pe.v

    def tap(self, name, ap):
        if name not in self.taps:
            return
        for s in (self.pe, self.ve, self.se, self.gp):
            if s.v:
                self.GE.wait_ge(s.h, s.v)
        self.osem.inc(self.GE.dma_start(self.taps[name][:, :], ap), 16)
        del self.taps[name]

    def emit_init(self, io):
        TE, VE, SE, GE, SY = self.TE, self.VE, self.SE, self.GE, self.SY
        for (dst, srcname) in (
            (self.onesbcol[:, :], "col_ones128"),
            (self.ones64col[:, :], "col_ones64"),
            (self.onesb128[:, :], "row_ones128"),
            (self.negb128[:, :], "row_neg128"),
            (self.grow[:, :], "gbrow_g"),
            (self.gnb[1:2, :], "gbrow_b"),
            (self.statbf[1:2, 0:512], "onesrow"),
        ):
            self.wconv.inc(GE.dma_start(dst, io[srcname][:, :]), 16)
        cvt = self.wconv.v
        for (k, s, e) in self.units:
            src = io["wflat"][s:e]
            dst = io["wbf"][s:e]
            self.conv_tag[k] = self.wconv.inc(GE.dma_start(dst, src), 16)
        for name, dstap in (
            ("selq", self.selq[:, :]), ("selbc", self.selbc[:, :]),
            ("rmat", self.rmat[:, :]), ("tric", self.tric[:, :]), ("trip", self.trip[:, :]),
        ):
            self.xload.inc(SE.dma_start(dstap, io[name][:, :]), 16)
        for name, dstap in (
            ("qs4", self.qs4[:, :]), ("b1a", self.b1a[:, :]), ("b1g", self.b1g[:, :]),
            ("b2c", self.b2c[:, :]), ("row_ones64", self.one64f[:, :]),
        ):
            self.xload.inc(SE.dma_start(dstap, io[name][:, :]), 16)
        GE.memset(self.CARR[:, :], 0.0)
        GE.memset(self.ktmp[64:65, :], -1.0)
        GE.memset(self.Pfull[:, :], 0.0)
        GE.memset(self.Pprev[:, :], 0.0)
        GE.memset(self.Pcur[:, :], 0.0)
        GE.memset(self.kn2[:, :], 0.0)
        GE.memset(self.Mu[64:65, :], -35.0)
        GE.memset(self.absq[64:65, :], 1.0)
        vb4 = self.VB[:, :].rearrange("p (c d) -> p c d", d=65)
        GE.memset(vb4[:, :, 64:65], 1.0)
        vl4 = self.VLp[:, :].rearrange("p (c d) -> p c d", d=65)
        GE.memset(vl4[:, :, 64:65], 1.0)
        va4 = self.AR[:, self.VAUGC:self.VAUGC + 2080].rearrange("p (c d) -> p c d", d=65)
        self.gp.inc(GE.memset(va4[:, :, 64:65], 1.0))
        VE.wait_ge(self.wconv.h, cvt)
        VE.wait_ge(self.xload.h, self.xload.v)
        self.ve.inc(VE.tensor_scalar(self.gnb[0:1, :], self.grow[0:1, :], -1.0, None, ALU.mult))
        TE.wait_ge(self.xload.h, self.xload.v)
        TE.wait_ge(self.wconv.h, cvt)
        TE.wait_ge(self.gp.h, self.gp.v)
        TE.wait_ge(self.ve.h, self.ve.v)
        SE.wait_ge(self.xload.h, self.xload.v)
        SE.wait_ge(self.wconv.h, cvt)
        SE.wait_ge(self.gp.h, self.gp.v)
        GE.wait_ge(self.xload.h, self.xload.v)

    def layernorm(self, src_getter, out_getter, l, out_f32, xbf_off=None,
                  src_tags=None):
        TE, VE, SE, GE = self.TE, self.VE, self.SE, self.GE
        bsum, bsq = self.pb[6], self.pb[7]
        self.bank_acq(bsum)
        self.bank_acq(bsq)
        sq_rot = self.sqb
        sv0 = self.ve.v
        if src_tags is None:
            SE.wait_ge(self.ve.h, sv0)
        for c in range(NCH):
            src = src_getter(c)
            if src_tags is not None:
                SE.wait_ge(self.ve.h, src_tags[c])
            if xbf_off is not None:
                xb = self.AR[:, xbf_off + 512 * c: xbf_off + 512 * (c + 1)]
                tve = self.ve.inc(VE.tensor_copy(xb, src))
            else:
                xb = src
                tve = None
            sl = c % 2
            if self.sq_cons[sl]:
                SE.wait_ge(self.pe.h, self.sq_cons[sl])
            tse = self.se.inc(SE.activation(sq_rot[:, 512 * sl:512 * (sl + 1)], src, AF.Square))
            if tve is not None:
                TE.wait_ge(self.ve.h, tve)
            TE.wait_ge(self.se.h, tse)
            mm1 = TE.matmul(bsum.ap[0:1, :], self.onesbcol[:, :], xb,
                            start=(c == 0), stop=(c == NCH - 1))
            if c == NCH - 1:
                bsum.pe_tag = self.pe.inc(mm1)
            mm2 = TE.matmul(bsq.ap[0:1, :], self.onesbcol[:, :],
                            sq_rot[:, 512 * sl:512 * (sl + 1)],
                            start=(c == 0), stop=(c == NCH - 1))
            self.sq_cons[sl] = self.pe.inc(mm2)
            if c == NCH - 1:
                bsq.pe_tag = self.pe.v
        return self.ln_finish(src_getter, out_getter, l)

    def ln_finish(self, src_getter, out_getter, l, src_tags=None):
        TE, VE, SE, GE = self.TE, self.VE, self.SE, self.GE
        bsum, bsq = self.pb[6], self.pb[7]
        sf, sbf = self.statf, self.statbf
        self.bank_rd(bsum, VE)
        VE.tensor_scalar(sf[0:1, 0:512], bsum.ap[0:1, :], 1.0 / 1024.0, None, ALU.mult)
        VE.tensor_tensor(sf[0:1, 1536:2048], sf[0:1, 0:512], sf[0:1, 0:512], ALU.mult)
        VE.scalar_tensor_tensor(sf[0:1, 512:1024], bsq.ap[0:1, :], 1.0 / 1024.0,
                                sf[0:1, 1536:2048], ALU.mult, ALU.subtract)
        t1 = self.ve.inc(VE.tensor_scalar(sf[0:1, 512:1024], sf[0:1, 512:1024],
                                          1.0, 1e-5, ALU.mult, ALU.add))
        bsum.cons.append((self.ve, t1))
        bsq.cons.append((self.ve, t1))
        SE.wait_ge(self.ve.h, t1)
        SE.activation(sf[0:1, 1024:1536], sf[0:1, 512:1024], AF.Ln)
        t2 = self.se.inc(SE.activation(sbf[0:1, 512:1024], sf[0:1, 1024:1536],
                                       AF.Exp, scale=-0.5))
        VE.wait_ge(self.se.h, t2)
        t3 = self.ve.inc(VE.tensor_tensor(sbf[0:1, 0:512], sf[0:1, 0:512],
                                          sbf[0:1, 512:1024], ALU.mult))
        out_tags = []
        for c in range(NCH):
            bA = self.pb[4 + 2 * (c % 2)]
            bB = self.pb[5 + 2 * (c % 2)]
            self.bank_acq(bA)
            self.bank_acq(bB)
            if l is None:
                lA = self.onesb128[0:1, :]
                lB = self.negb128[0:1, :]
            else:
                lA = self.grow[0:1, 1024 * l + 128 * c:1024 * l + 128 * (c + 1)]
                lB = self.gnb[0:2, 1024 * l + 128 * c:1024 * l + 128 * (c + 1)]
            if c == 0:
                TE.wait_ge(self.ve.h, t3)
                TE.wait_ge(self.se.h, t2)
            mmA = TE.matmul(bA.ap[:, :], lA, sbf[0:1, 512:1024], start=True, stop=True)
            bA.pe_tag = self.pe.inc(mmA)
            if l is None:
                mmB = TE.matmul(bB.ap[:, :], lB, sbf[0:1, 0:512], start=True, stop=True)
            else:
                mmB = TE.matmul(bB.ap[:, :], lB, sbf[0:2, 0:512], start=True, stop=True)
            bB.pe_tag = self.pe.inc(mmB)
            src = src_getter(c)
            if src_tags is not None:
                VE.wait_ge(self.ve.h, src_tags[c])
            out = out_getter(c)
            self.bank_rd(bA, VE)
            self.bank_rd(bB, VE)
            VE.tensor_tensor(self.scr2[:, :], src, bA.ap[:, :], ALU.mult)
            tv = self.ve.inc(VE.tensor_tensor(out, self.scr2[:, :], bB.ap[:, :], ALU.add))
            bA.cons.append((self.ve, tv))
            bB.cons.append((self.ve, tv))
            out_tags.append(tv)
        return out_tags


    def shift_cast(self, inst, site, src_getter, dst_off):
        VE, GE = self.VE, self.GE
        base = 4 * (2 * inst + site)
        tags = []
        for c in range(NCH):
            src = src_getter(c)
            dst = self.AR[:, dst_off + 512 * c: dst_off + 512 * (c + 1)]
            if c < 4:
                tags.append(self.ve.inc(VE.tensor_copy(dst, src)))
            else:
                cc = base + (c - 4)
                self.ve.inc(VE.tensor_copy(dst[:, 1:512], src[:, 0:511]))
                tags.append(self.ve.inc(VE.tensor_copy(dst[:, 0:1], self.CARR[:, cc:cc + 1])))
                self.ve.inc(VE.tensor_copy(self.CARR[:, cc:cc + 1], src[:, 511:512]))
        return tags

    def local_qkv(self, t, inst, wbf, v_only=False, h_tags=None):
        TE, VE, SE = self.TE, self.VE, self.SE
        l = inst % 2
        hb = lambda k: self.AR[:, self.HBF + 512 * k:self.HBF + 512 * (k + 1)]
        bq_sum, bk_sum = self.pb[6], self.pb[7]
        self.bank_acq(bq_sum)
        self.bank_acq(bk_sum)
        for m in range(8):
            b = self.pb[m % 4]
            self.bank_acq(b)
            si, slot, wt = self.ring_use(("qkv", l, m), wbf)
            TE.wait_ge(self.wload.h, wt)
            for k in range(NCH):
                if h_tags is not None and m == 0:
                    TE.wait_ge(self.ve.h, h_tags[k])
                mm = TE.matmul(b.ap[:, :], slot[0:128, 128 * k:128 * (k + 1)], hb(k),
                               start=(k == 0), stop=(k == NCH - 1))
            b.pe_tag = self.pe.inc(mm)
            self.ring_done(si)
            self.bank_rd(b, VE)
            dst = self.AR[:, self.QKBF + 512 * m:self.QKBF + 512 * (m + 1)]
            tv = self.ve.inc(VE.tensor_copy(dst, b.ap[:, :]))
            b.cons.append((self.ve, tv))
            sl = m % 2
            if self.sq_cons[sl]:
                SE.wait_ge(self.pe.h, self.sq_cons[sl])
            SE.wait_ge(self.ve.h, tv)
            ts = self.se.inc(SE.activation(self.sqb[:, 512 * sl:512 * (sl + 1)],
                                           dst, AF.Square))
            TE.wait_ge(self.se.h, ts)
            c4 = m % 4
            tgt = bq_sum if m < 4 else bk_sum
            mm2 = TE.matmul(tgt.ap[0:8, :], self.selq[:, 8 * c4:8 * (c4 + 1)],
                            self.sqb[:, 512 * sl:512 * (sl + 1)],
                            start=(c4 == 0), stop=(c4 == 3))
            self.sq_cons[sl] = self.pe.inc(mm2)
            if c4 == 3:
                tgt.pe_tag = self.pe.v
        self.qkbf_ve = self.ve.v
        self._invnorm_extract()
        for sub in range(4):
            self.bank_acq(self.pb[0 + sub])
        for k in range(NCH):
            si, slot, wt = self.ring_use(("qkvv", l, k), wbf)
            TE.wait_ge(self.wload.h, wt)
            if h_tags is not None:
                TE.wait_ge(self.ve.h, h_tags[k])
            for sub in range(4):
                mm = TE.matmul(self.pb[0 + sub].ap[:, :],
                               hb(k)[:, 128 * sub:128 * (sub + 1)], slot[0:128, 0:512],
                               start=(k == 0), stop=(k == NCH - 1))
                if k == NCH - 1:
                    self.pb[0 + sub].pe_tag = self.pe.inc(mm)
            if k < NCH - 1:
                self.pe.inc(mm)
            self.ring_done(si)
        for sub in range(4):
            b = self.pb[0 + sub]
            self.bank_rd(b, VE)
            dst = self.AR[:, self.VAUGC + 520 * sub:self.VAUGC + 520 * (sub + 1)]
            dst4 = dst.rearrange("p (g d) -> p g d", d=65)[:, :, 0:64]
            src4 = b.ap[:, :].rearrange("p (g d) -> p g d", d=64)
            tv = self.ve.inc(VE.tensor_copy(dst4, src4))
            b.cons.append((self.ve, tv))
        self.vaug_ve = self.ve.v
        if v_only:
            return
        self.free["HBF"] = self.pe.v


    def _invnorm_extract(self):
        VE, SE = self.VE, self.SE
        for bs, io_, rb in ((self.pb[7], 512, 0), (self.pb[6], 0, 32)):
            self.bank_rd(bs, VE)
            tv = self.ve.inc(VE.tensor_scalar(self.scr1[rb:rb + 8, :], bs.ap[0:8, :],
                                              1e-24, None, ALU.max))
            bs.cons.append((self.ve, tv))
            SE.wait_ge(self.ve.h, tv)
            SE.activation(self.scr2[rb:rb + 8, :], self.scr1[rb:rb + 8, :], AF.Ln)
            self.invbf_tse = self.se.inc(
                SE.activation(self.invbf[0:8, io_:io_ + 512],
                              self.scr2[rb:rb + 8, :], AF.Exp, scale=-0.5))

    def local_qk_process(self, t, inst, side, io):
        TE, VE, SE, GE, SY = self.TE, self.VE, self.SE, self.GE, self.SY
        l = inst % 2
        bsum = self.pb[6] if side == 0 else self.pb[7]
        coff = 0 if side == 0 else 2048
        aoff = self.QAUG if side == 0 else self.KAUG
        ioff = 0 if side == 0 else 512
        tq1 = io["t1q"] if side == 0 else io["t1k"]
        tq2 = io["t2q"] if side == 0 else io["t2k"]
        if self.tab_cons_ve:
            SY.wait_ge(self.ve.h, self.tab_cons_ve)
        if self.tab_cons_gp:
            SY.wait_ge(self.gp.h, self.tab_cons_gp)
        w1 = self.xload.inc(SY.dma_start(self.t1s[:, :], tq1[:, 512 * t:512 * (t + 1)]), 16)
        w2 = self.xload.inc(SY.dma_start(self.t2s[:, :], tq2[:, 512 * t:512 * (t + 1)]), 16)
        VE.wait_ge(self.xload.h, w2)
        qcol = self.qs4[:, 4 * l + 2 * side:4 * l + 2 * side + 1]
        qpcol = self.qs4[:, 4 * l + 2 * side + 1:4 * l + 2 * side + 2]
        VE.tensor_scalar(self.t1s[:, :], self.t1s[:, :], qcol, None, ALU.mult)
        tab_fold = self.ve.inc(VE.tensor_scalar(self.t2s[:, :], self.t2s[:, :],
                                                qpcol, None, ALU.mult))
        tse = self.invbf_tse
        if side == 1:
            GE.wait_ge(self.pe.h, max(self.free.values(), default=0))
            self.kaug_row64 = self.gp.inc(
                GE.memset(self.AR[64:65, self.KAUG:self.KAUG + 4096], -1.0))
        bBs, bRs = [], []
        for c in range(4):
            bB = self.pb[4 + c]
            self.bank_acq(bB)
            TE.wait_ge(self.se.h, tse)
            mmB = TE.matmul(bB.ap[:, :], self.selbc[:, 128 * c:128 * (c + 1)],
                            self.invbf[0:8, ioff:ioff + 512], start=True, stop=True)
            bB.pe_tag = self.pe.inc(mmB)
            bBs.append(bB)
        for c in range(4):
            bR = self.pb[c]
            self.bank_acq(bR)
            TE.wait_ge(self.ve.h, self.qkbf_ve)
            src_c = self.AR[:, self.QKBF + coff + 512 * c:self.QKBF + coff + 512 * (c + 1)]
            mmR = TE.matmul(bR.ap[:, :], self.rmat[:, :], src_c, start=True, stop=True)
            bR.pe_tag = self.pe.inc(mmR)
            bRs.append(bR)
        last_dst = 0
        aug_tags = []
        for c in range(4):
            bR, bB = bRs[c], bBs[c]
            src_c = self.AR[:, self.QKBF + coff + 512 * c:self.QKBF + coff + 512 * (c + 1)]
            VE.tensor_tensor(self.scr1[:, :], src_c, self.t1s[:, :], ALU.mult)
            self.bank_rd(bR, VE)
            tv3 = self.ve.inc(VE.tensor_tensor(self.scr2[:, :], bR.ap[:, :],
                                               self.t2s[:, :], ALU.mult))
            bR.cons.append((self.ve, tv3))
            VE.tensor_tensor(self.scr1[:, :], self.scr1[:, :], self.scr2[:, :], ALU.add)
            self.bank_rd(bB, VE)
            for hh in range(2):
                h = 2 * c + hh
                dst = self.AR[0:64, aoff + 512 * h:aoff + 512 * (h + 1)]
                last_dst = self.ve.inc(
                    VE.tensor_tensor(dst, self.scr1[64 * hh:64 * hh + 64, :],
                                     bB.ap[64 * hh:64 * hh + 64, :], ALU.mult))
                aug_tags.append(last_dst)
            bB.cons.append((self.ve, last_dst))
        self.free["QKBF" if side == 0 else "QKBF_k"] = self.pe.v
        self.tab_cons_ve = self.ve.v
        self.tab_cons_gp = self.gp.v
        return aug_tags

    def local_norm_rows(self, t, inst):
        VE = self.VE
        for h in range(HL):
            src = self.AR[0:64, self.KAUG + 512 * h:self.KAUG + 512 * (h + 1)]
            VE.reduce_max(self.scr2[0:64, h:h + 1], src, axis=AX.X,
                          apply_absolute_value=True)
        VE.drain()
        self.mu_tags = []
        for h in range(HL):
            mp = self.Mprev[0:64, 8 * inst + h:8 * inst + h + 1]
            if t == 0:
                tm = self.ve.inc(VE.tensor_copy(self.Mu[0:64, h:h + 1],
                                                self.scr2[0:64, h:h + 1]))
            else:
                tm = self.ve.inc(VE.tensor_tensor(self.Mu[0:64, h:h + 1],
                                                  self.scr2[0:64, h:h + 1],
                                                  mp, ALU.max))
            self.mu_tags.append(tm)
            self.ve.inc(VE.tensor_copy(mp, self.scr2[0:64, h:h + 1]))
        self.mu_ve = self.ve.v
        return self.ve.v

    def _c_pipe(self, h, qv0):
        TE, SE = self.TE, self.SE
        sl = h % 2
        if self.absq_cons[sl]:
            SE.wait_ge(self.pe.h, self.absq_cons[sl])
        SE.wait_ge(self.ve.h, qv0)
        ts = self.se.inc(SE.activation(self.absq[0:64, 512 * sl:512 * (sl + 1)],
                                       self.AR[0:64, self.QAUG + 512 * h:
                                               self.QAUG + 512 * (h + 1)],
                                       AF.Abs))
        br = self.pb[6 + (h % 2)]
        self.bank_acq(br)
        TE.wait_ge(self.se.h, ts)
        TE.wait_ge(self.ve.h, self.mu_tags[h])
        mm = TE.matmul(br.ap[0:1, :], self.Mu[0:65, h:h + 1],
                       self.absq[0:65, 512 * sl:512 * (sl + 1)], start=True, stop=True)
        br.pe_tag = self.pe.inc(mm)
        self.absq_cons[sl] = self.pe.v
        self.bank_rd(br, SE)
        t2 = self.se.inc(SE.activation(self.AR[64:65, self.QAUG + 512 * h:
                                               self.QAUG + 512 * (h + 1)],
                                       br.ap[0:1, :], AF.Copy))
        br.cons.append((self.se, t2))
        return t2

    def _attn_normalize(self, bO, h, dst, after_ve, bb_base=4):
        TE, VE, SE = self.TE, self.VE, self.SE
        self.bank_rd(bO, VE)
        VE.tensor_copy(self.statf[0:1, 0:512], bO.ap[64:65, :])
        tvs = self.ve.inc(VE.tensor_scalar(self.statf[0:1, 0:512],
                                           self.statf[0:1, 0:512],
                                           1e-18, None, ALU.max))
        bO.cons.append((self.ve, tvs))
        if self.inv_cons:
            SE.wait_ge(self.pe.h, self.inv_cons)
        SE.wait_ge(self.ve.h, tvs)
        SE.activation(self.statf[0:1, 1024:1536], self.statf[0:1, 0:512], AF.Ln)
        ti = self.se.inc(SE.activation(self.invbf[0:1, 0:512],
                                       self.statf[0:1, 1024:1536], AF.Exp,
                                       scale=-1.0))
        bB = self.pb[bb_base + (h % 2)]
        self.bank_acq(bB)
        TE.wait_ge(self.se.h, ti)
        mmb = TE.matmul(bB.ap[0:64, :], self.onesb128[0:1, 0:64],
                        self.invbf[0:1, 0:512], start=True, stop=True)
        bB.pe_tag = self.pe.inc(mmb)
        self.inv_cons = bB.pe_tag
        scr = self.scr1 if h % 2 == 0 else self.scr2
        SE.wait_ge(self.pe.h, self.inv_cons)
        SE.wait_ge(self.ve.h, max(self.scr_cons[h % 2], after_ve))
        SE.wait_ge(self.gp.h, self.norm_gp)
        ts = self.se.inc(SE.activation(scr[0:64, :], bB.ap[0:64, :], AF.Copy))
        bB.cons.append((self.se, ts))
        VE.wait_ge(self.se.h, ts)
        tv2 = self.ve.inc(VE.tensor_tensor(dst, bO.ap[0:64, :], scr[0:64, :], ALU.mult))
        bO.cons.append((self.ve, tv2))
        bO.cons.append((self.se, ts))
        self.scr_cons[h % 2] = tv2

    def local_attn(self, t, inst, q_tags, k_tags):
        TE, VE, SE, GE = self.TE, self.VE, self.SE, self.GE
        qv = self.ve.v
        sv = self.se.v
        gv0 = self.gp.v
        GE.wait_ge(self.ve.h, qv)
        TE.wait_ge(self.gp.h, self.kaug_row64)
        self.norm_gp = gv0
        pend = None
        nh = ATTN_HEADS or HL
        ctags = {0: self._c_pipe(0, q_tags[0])}
        nprev = 4 if t > 0 else 0
        for h in range(nh):
            if h + 1 < nh:
                ctags[h + 1] = self._c_pipe(h + 1, q_tags[h + 1])
            tb = h % 2
            if t > 0:
                if self.ktmp_cons[tb]:
                    GE.wait_ge(self.pe.h, self.ktmp_cons[tb])
                src = self.KLp[64 * (h % 2):64 * (h % 2) + 64,
                               2048 * inst + 512 * (h // 2):2048 * inst + 512 * (h // 2) + 512]
                tg = self.gp.inc(GE.tensor_copy(self.ktmp[0:64, 512 * tb:512 * (tb + 1)], src))
            bO = self.pb[2 + (h % 2)]
            self.bank_acq(bO)
            TE.wait_ge(self.ve.h, max(q_tags[h], k_tags[h]))
            TE.wait_ge(self.se.h, ctags[h])
            pv_started = False
            plist = []
            for kc in range(nprev + 4):
                cur = kc >= nprev
                kci = kc - nprev if cur else kc
                bS = self.pb[kc % 2]
                self.bank_acq(bS)
                if cur:
                    lhs = self.AR[0:65, self.KAUG + 512 * h + 128 * kci:
                                  self.KAUG + 512 * h + 128 * (kci + 1)]
                    lo, hi = 128 * kci, 512
                else:
                    TE.wait_ge(self.gp.h, tg)
                    lhs = self.ktmp[0:65, 512 * tb + 128 * kci:512 * tb + 128 * (kci + 1)]
                    lo, hi = 0, 128 * (kci + 1)
                rhs = self.AR[0:65, self.QAUG + 512 * h + lo:self.QAUG + 512 * h + hi]
                mm = TE.matmul(bS.ap[:, lo:hi], lhs, rhs, start=True, stop=True)
                bS.pe_tag = self.pe.inc(mm)
                if t > 0 and not cur:
                    self.ktmp_cons[tb] = self.pe.v
                if cur and kci == 0:
                    pbuf, pz = self._pfull(), None
                elif cur:
                    pbuf, pz = self.Pcur[:, 512 * (kci - 1):512 * kci], None
                elif kci == 3:
                    pbuf, pz = self._pfull(), None
                else:
                    pbuf, pz = self.Pprev[:, 512 * kci:512 * (kci + 1)], None
                key = pbuf.offset
                if key in self.ptag and self.ptag[key]:
                    SE.wait_ge(self.pe.h, self.ptag[key])
                self.bank_rd(bS, SE)
                te = self.se.inc(SE.activation(pbuf[:, lo:hi], bS.ap[:, lo:hi], AF.Exp))
                bS.cons.append((self.se, te))
                tri = self.tric if cur else self.trip
                blk = 128 * kci
                GE.wait_ge(self.se.h, te)
                tg2 = self.gp.inc(GE.tensor_tensor(pbuf[:, blk:blk + 128],
                                                   pbuf[:, blk:blk + 128], tri[:, :], ALU.mult))
                plist.append((pbuf, key, te, tg2, kc))
            for (pbuf, key, te, tg2, kc) in plist:
                cur = kc >= nprev
                kci = kc - nprev if cur else kc
                if cur:
                    vsl = self.AR[:, self.VAUGC + 520 * kci + 65 * h:
                                  self.VAUGC + 520 * kci + 65 * h + 65]
                else:
                    vsl = self.VLp[:, 2080 * inst + 520 * kci + 65 * h:
                                   2080 * inst + 520 * kci + 65 * h + 65]
                TE.wait_ge(self.gp.h, tg2)
                mm = TE.matmul(bO.ap[0:65, :], vsl, pbuf[:, :],
                               start=(not pv_started), stop=(kc == nprev + 3))
                pv_started = True
                if kc == nprev + 3:
                    bO.pe_tag = self.pe.inc(mm)
            for (pbuf, key, te, tg2, kc) in plist:
                self.ptag[key] = bO.pe_tag
            if pend is not None:
                pbO, ph = pend
                pdst = self.AR[64 * (ph % 2):64 * (ph % 2) + 64,
                               self.OTL + 512 * (ph // 2):self.OTL + 512 * (ph // 2) + 512]
                self._attn_normalize(pbO, ph, pdst, qv)
            pend = (bO, h)
        if pend is not None:
            pbO, ph = pend
            pdst = self.AR[64 * (ph % 2):64 * (ph % 2) + 64,
                           self.OTL + 512 * (ph // 2):self.OTL + 512 * (ph // 2) + 512]
            self._attn_normalize(pbO, ph, pdst, qv)
        for h in range(HL):
            src = self.AR[0:64, self.KAUG + 512 * h:self.KAUG + 512 * (h + 1)]
            dst = self.KLp[64 * (h % 2):64 * (h % 2) + 64,
                           2048 * inst + 512 * (h // 2):2048 * inst + 512 * (h // 2) + 512]
            self.ve.inc(VE.tensor_copy(dst, src))
        tvv = self.ve.inc(VE.tensor_copy(self.VLp[:, 2080 * inst:2080 * (inst + 1)],
                                         self.AR[:, self.VAUGC:self.VAUGC + 2080]))
        self.klp_ve = self.ve.v
        return self.ve.v

    def _pfull(self):
        i = self.pfull_rr % 4
        self.pfull_rr += 1
        return self.Pfull[:, 512 * i:512 * (i + 1)]

    def local_wo(self, t, inst, wbf):
        TE, VE, SE = self.TE, self.VE, self.SE
        l = inst % 2
        otv = self.ve.v
        bsum, bsq = self.pb[6], self.pb[7]
        self.bank_acq(bsum)
        self.bank_acq(bsq)
        TE.wait_ge(self.ve.h, otv)
        VE.wait_ge(self.pe.h, max(self.free.values(), default=0))
        base = 4 * (2 * inst + 1)
        for m in range(NCH):
            b = self.pb[m % 4]
            self.bank_acq(b)
            si, slot, wt = self.ring_use(("wo", l, m), wbf)
            TE.wait_ge(self.wload.h, wt)
            for k in range(4):
                mm = TE.matmul(b.ap[:, :], slot[0:128, 128 * k:128 * (k + 1)],
                               self.AR[:, self.OTL + 512 * k:self.OTL + 512 * (k + 1)],
                               start=(k == 0), stop=(k == 3))
            b.pe_tag = self.pe.inc(mm)
            self.ring_done(si)
            self.bank_rd(b, VE)
            sc = self.S[:, 512 * m:512 * (m + 1)]
            tv = self.ve.inc(VE.tensor_tensor(sc, sc, b.ap[:, :], ALU.add))
            b.cons.append((self.ve, tv))
            dst = self.AR[:, self.HBF + 512 * m:self.HBF + 512 * (m + 1)]
            if m < 4:
                tagm = self.ve.inc(VE.tensor_copy(dst, sc))
            else:
                cc = base + (m - 4)
                VE.tensor_copy(dst[:, 1:512], sc[:, 0:511])
                tagm = self.ve.inc(VE.tensor_copy(dst[:, 0:1], self.CARR[:, cc:cc + 1]))
                self.ve.inc(VE.tensor_copy(self.CARR[:, cc:cc + 1], sc[:, 511:512]))
            sl = m % 2
            if self.sq_cons[sl]:
                SE.wait_ge(self.pe.h, self.sq_cons[sl])
            SE.wait_ge(self.ve.h, tagm)
            tse = self.se.inc(SE.activation(self.sqb[:, 512 * sl:512 * (sl + 1)],
                                            dst, AF.Square))
            TE.wait_ge(self.ve.h, tagm)
            mm1 = TE.matmul(bsum.ap[0:1, :], self.onesbcol[:, :], dst,
                            start=(m == 0), stop=(m == NCH - 1))
            if m == NCH - 1:
                bsum.pe_tag = self.pe.inc(mm1)
            TE.wait_ge(self.se.h, tse)
            mm2 = TE.matmul(bsq.ap[0:1, :], self.onesbcol[:, :],
                            self.sqb[:, 512 * sl:512 * (sl + 1)],
                            start=(m == 0), stop=(m == NCH - 1))
            self.sq_cons[sl] = self.pe.inc(mm2)
            if m == NCH - 1:
                bsq.pe_tag = self.pe.v
        self.free["OTL"] = self.pe.v
        return self.ve.v

    def local_ffn(self, t, inst, wbf):
        TE, VE, SE, GE = self.TE, self.VE, self.SE, self.GE
        l = inst % 2
        hbg = lambda c: self.AR[:, self.HBF + 512 * c:self.HBF + 512 * (c + 1)]
        h2_tags = self.ln_finish(hbg, hbg, l)
        hv = self.ve.v
        for j in range(NJ):
            wj = JW[j]
            bA = self.pb[2 * (j % 2)]
            bG = self.pb[1 + 2 * (j % 2)]
            self.bank_acq(bA)
            self.bank_acq(bG)
            sa, slota, wta = self.ring_use(("w1a", l, j), wbf)
            sg, slotg, wtg = self.ring_use(("w1g", l, j), wbf)
            TE.wait_ge(self.wload.h, wtg)
            for k in range(NCH):
                if j == 0:
                    TE.wait_ge(self.ve.h, h2_tags[k])
                mma = TE.matmul(bA.ap[0:wj, :], slota[0:128, wj * k:wj * (k + 1)],
                                hbg(k), start=(k == 0), stop=(k == NCH - 1))
            bA.pe_tag = self.pe.inc(mma)
            for k in range(NCH):
                mmg = TE.matmul(bG.ap[0:wj, :], slotg[0:128, wj * k:wj * (k + 1)],
                                hbg(k), start=(k == 0), stop=(k == NCH - 1))
            bG.pe_tag = self.pe.inc(mmg)
            self.ring_done(sa)
            self.ring_done(sg)
            gsl = j % 2
            if self.gel_cons[gsl]:
                SE.wait_ge(self.ve.h, self.gel_cons[gsl])
            self.bank_rd(bG, SE)
            ts = self.se.inc(SE.activation(self.gelsb[0:wj, 512 * gsl:512 * gsl + 512],
                                           bG.ap[0:wj, :], AF.Gelu,
                                           bias=self.b1g[0:wj, 22 * l + j:22 * l + j + 1]))
            bG.cons.append((self.se, ts))
            self.bank_rd(bA, VE)
            VE.wait_ge(self.se.h, ts)
            tv = self.ve.inc(VE.scalar_tensor_tensor(
                self.AR[0:wj, self.AGOF + 512 * j:self.AGOF + 512 * (j + 1)],
                bA.ap[0:wj, :], self.b1a[0:wj, 22 * l + j:22 * l + j + 1],
                self.gelsb[0:wj, 512 * gsl:512 * gsl + 512], ALU.add, ALU.mult))
            bA.cons.append((self.ve, tv))
            self.gel_cons[gsl] = tv
        agv = self.ve.v
        for m in range(NCH):
            self.bank_acq(self.pb[m])
        TE.wait_ge(self.ve.h, agv)
        for j in range(NJ):
            kr = JW[j]
            si, slot, wt = self.ring_use(("w2", l, j), wbf)
            TE.wait_ge(self.wload.h, wt)
            for m in range(NCH):
                mm = TE.matmul(self.pb[m].ap[:, :], slot[0:kr, 128 * m:128 * (m + 1)],
                               self.AR[0:kr, self.AGOF + 512 * j:self.AGOF + 512 * (j + 1)],
                               start=(j == 0), stop=(j == NJ - 1))
                if j == NJ - 1:
                    self.pb[m].pe_tag = self.pe.inc(mm)
            if j < NJ - 1:
                self.pe.inc(mm)
            self.ring_done(si)
        self.free["AG"] = self.pe.v
        for m in range(NCH):
            b = self.pb[m]
            self.bank_rd(b, VE)
            sc = self.S[:, 512 * m:512 * (m + 1)]
            tv = self.ve.inc(VE.scalar_tensor_tensor(sc, b.ap[:, :],
                                                     self.b2c[:, 8 * l + m:8 * l + m + 1],
                                                     sc, ALU.add, ALU.add))
            b.cons.append((self.ve, tv))
        return self.ve.v

    @staticmethod
    def _bq_pos(h):
        return 64 * ((h // 4) % 2), (h % 4) + 4 * (h // 8)

    def phaseB(self, t, wbf):
        TE, VE, SE, GE = self.TE, self.VE, self.SE, self.GE
        g0 = max(self.free.values(), default=0)
        VE.wait_ge(self.pe.h, g0)
        xg = lambda c: self.X[:, 512 * c:512 * (c + 1)]
        yg = lambda c: self.AR[:, self.HBF + 512 * c:self.HBF + 512 * (c + 1)]
        y_tags = self.layernorm(xg, yg, None, out_f32=False, xbf_off=self.QKBF,
                                src_tags=self.x_tags)
        yv = self.ve.v
        hb = lambda k: self.AR[:, self.HBF + 512 * k:self.HBF + 512 * (k + 1)]
        for m in range(NCH):
            b = self.pb[m % 4]
            self.bank_acq(b)
            si, slot, wt = self.ring_use(("Bq", m), wbf)
            TE.wait_ge(self.wload.h, wt)
            for k in range(NCH):
                if m == 0:
                    TE.wait_ge(self.ve.h, y_tags[k])
                mm = TE.matmul(b.ap[:, :], slot[0:128, 128 * k:128 * (k + 1)], hb(k),
                               start=(k == 0), stop=(k == NCH - 1))
            b.pe_tag = self.pe.inc(mm)
            self.bank_rd(b, VE)
            for hh in range(2):
                h = 2 * m + hh
                rh, cs = self._bq_pos(h)
                dst = self.AR[rh:rh + 64, self.QKBF + 512 * cs:self.QKBF + 512 * (cs + 1)]
                tv = self.ve.inc(VE.tensor_scalar(dst, b.ap[64 * hh:64 * hh + 64, :],
                                                  0.125, None, ALU.mult))
            b.cons.append((self.ve, tv))
            self.ring_done(si)
        for m in range(2):
            b = self.pb[4 + m]
            self.bank_acq(b)
            si, slot, wt = self.ring_use(("Bk", m), wbf)
            TE.wait_ge(self.wload.h, wt)
            TE.wait_ge(self.ve.h, yv)
            for k in range(NCH):
                mm = TE.matmul(b.ap[:, :], slot[0:128, 128 * k:128 * (k + 1)], hb(k),
                               start=(k == 0), stop=(k == NCH - 1))
            b.pe_tag = self.pe.inc(mm)
            self.bank_rd(b, VE)
            tv = self.ve.inc(VE.tensor_copy(self.KB[:, 4096 * m + 512 * t:4096 * m + 512 * (t + 1)],
                                            b.ap[:, :]))
            b.cons.append((self.ve, tv))
            self.ring_done(si)
        for sub in range(4):
            self.bank_acq(self.pb[4 + sub])
        for k in range(NCH):
            si, slot, wt = self.ring_use(("Bv", k), wbf)
            TE.wait_ge(self.wload.h, wt)
            TE.wait_ge(self.ve.h, yv)
            for sub in range(4):
                mm = TE.matmul(self.pb[4 + sub].ap[:, 0:256],
                               hb(k)[:, 128 * sub:128 * (sub + 1)], slot[0:128, 0:256],
                               start=(k == 0), stop=(k == NCH - 1))
                if k == NCH - 1:
                    self.pb[4 + sub].pe_tag = self.pe.inc(mm)
            self.ring_done(si)
        for sub in range(4):
            b = self.pb[4 + sub]
            self.bank_rd(b, VE)
            ci = 4 * t + sub
            dst4 = self.VB[:, 260 * ci:260 * (ci + 1)].rearrange("p (g d) -> p g d", d=65)[:, :, 0:64]
            src4 = b.ap[:, 0:256].rearrange("p (g d) -> p g d", d=64)
            tv = self.ve.inc(VE.tensor_copy(dst4, src4))
            b.cons.append((self.ve, tv))
        qkv_ve = self.ve.v
        TEw = TE
        nK = 4 * (t + 1)
        self.norm_gp = self.gp.v
        pend = None
        for h in range(HQ):
            g = h // 4
            rh, cs = self._bq_pos(h)
            qs = self.AR[rh:rh + 64, self.QKBF + 512 * cs:self.QKBF + 512 * (cs + 1)]
            bO = self.pb[4 + (h % 2)]
            self.bank_acq(bO)
            plist = []
            TE.wait_ge(self.ve.h, qkv_ve)
            K = 0
            while K < 4 * t:
                pi = (K // 2) % 2
                bSa, bSb = self.pb[2 * pi], self.pb[2 * pi + 1]
                self.bank_acq(bSa)
                self.bank_acq(bSb)
                for j, bS in ((0, bSa), (1, bSb)):
                    lhs = self.KB[rh:rh + 64,
                                  4096 * (g // 2) + 128 * (K + j):4096 * (g // 2) + 128 * (K + j + 1)]
                    mm = TE.matmul(bS.ap[:, :], lhs, qs, start=True, stop=True)
                bSa.pe_tag = bSb.pe_tag = self.pe.inc(mm)
                pairap = self.PBIG[:, 1024 * pi:1024 * (pi + 1)]
                pslot = self.pfull_rr % 2
                self.pfull_rr += 1
                pbuf = self.Pfull[:, 1024 * pslot:1024 * (pslot + 1)]
                k0 = pbuf[:, 0:512].offset
                k1 = pbuf[:, 512:1024].offset
                mx = max(self.ptag.get(k0, 0), self.ptag.get(k1, 0))
                if mx:
                    SE.wait_ge(self.pe.h, mx)
                SE.wait_ge(self.pe.h, bSa.pe_tag)
                te = self.se.inc(SE.activation(pbuf, pairap, AF.Exp))
                bSa.cons.append((self.se, te))
                bSb.cons.append((self.se, te))
                TE.wait_ge(self.se.h, te)
                for j in range(2):
                    vsl = self.VB[:, 260 * (K + j) + 65 * g:260 * (K + j) + 65 * g + 65]
                    mmv = TE.matmul(bO.ap[0:65, :], vsl, pbuf[:, 512 * j:512 * (j + 1)],
                                    start=(K + j == 0), stop=(K + j == nK - 1))
                plist.append((k0, k1))
                K += 2
            for kci in range(4):
                K = 4 * t + kci
                bS = self.pb[K % 2]
                self.bank_acq(bS)
                lhs = self.KB[rh:rh + 64, 4096 * (g // 2) + 128 * K:4096 * (g // 2) + 128 * (K + 1)]
                lo = 128 * kci
                mm = TE.matmul(bS.ap[:, lo:512], lhs, qs[:, lo:512], start=True, stop=True)
                bS.pe_tag = self.pe.inc(mm)
                if kci > 0:
                    pbuf = self.Pcur[:, 512 * (kci - 1):512 * kci]
                else:
                    pslot = self.pfull_rr % 2
                    self.pfull_rr += 1
                    pbuf = self.Pfull[:, 1024 * pslot:1024 * pslot + 512]
                key = pbuf.offset
                mx = self.ptag.get(key, 0)
                if mx:
                    SE.wait_ge(self.pe.h, mx)
                self.bank_rd(bS, SE)
                te = self.se.inc(SE.activation(pbuf[:, lo:512], bS.ap[:, lo:512], AF.Exp))
                bS.cons.append((self.se, te))
                tg2 = 0
                if kci > 0:
                    blk = 128 * kci
                    GE.wait_ge(self.se.h, te)
                    tg2 = self.gp.inc(GE.tensor_tensor(pbuf[:, blk:blk + 128],
                                                       pbuf[:, blk:blk + 128],
                                                       self.tric[:, :], ALU.mult))
                else:
                    blk = 0
                    GE.wait_ge(self.se.h, te)
                    tg2 = self.gp.inc(GE.tensor_tensor(pbuf[:, 0:128],
                                                       pbuf[:, 0:128],
                                                       self.tric[:, :], ALU.mult))
                TE.wait_ge(self.gp.h, tg2)
                vsl = self.VB[:, 260 * K + 65 * g:260 * K + 65 * g + 65]
                mmv = TE.matmul(bO.ap[0:65, :], vsl, pbuf[:, :],
                                start=(K == 0), stop=(K == nK - 1))
                if K == nK - 1:
                    bO.pe_tag = self.pe.inc(mmv)
                plist.append((key, None))
            for (k0, k1) in plist:
                self.ptag[k0] = bO.pe_tag
                if k1 is not None:
                    self.ptag[k1] = bO.pe_tag
            if pend is not None:
                pbO, ph = pend
                pdst = self.AR[64 * (ph % 2):64 * (ph % 2) + 64,
                               self.QAUG + 512 * (ph // 2):self.QAUG + 512 * (ph // 2) + 512]
                self._attn_normalize(pbO, ph, pdst, qkv_ve, bb_base=6)
            pend = (bO, h)
        if pend is not None:
            pbO, ph = pend
            pdst = self.AR[64 * (ph % 2):64 * (ph % 2) + 64,
                           self.QAUG + 512 * (ph // 2):self.QAUG + 512 * (ph // 2) + 512]
            self._attn_normalize(pbO, ph, pdst, qkv_ve, bb_base=6)
        otv = self.ve.v
        self.x_tags = []
        for m in range(NCH):
            b = self.pb[4 + (m % 4)]
            self.bank_acq(b)
            si, slot, wt = self.ring_use(("Bo", m), wbf)
            TE.wait_ge(self.wload.h, wt)
            TE.wait_ge(self.ve.h, otv)
            for k in range(NCH):
                mm = TE.matmul(b.ap[:, :], slot[0:128, 128 * k:128 * (k + 1)],
                               self.AR[:, self.QAUG + 512 * k:self.QAUG + 512 * (k + 1)],
                               start=(k == 0), stop=(k == NCH - 1))
            b.pe_tag = self.pe.inc(mm)
            self.ring_done(si)
            self.bank_rd(b, VE)
            xc = self.X[:, 512 * m:512 * (m + 1)]
            tv = self.ve.inc(VE.tensor_tensor(xc, xc, b.ap[:, :], ALU.add))
            b.cons.append((self.ve, tv))
            self.x_tags.append(tv)
        self.free["OTB"] = self.pe.v
        return self.ve.v

    def ffn_phase(self, t, phase, wbf):
        VE = self.VE
        lim = _STAGES.index(STAGE_LIMIT)
        g0 = max(self.free.values(), default=0)
        VE.wait_ge(self.pe.h, g0)
        xg = lambda c: self.X[:, 512 * c:512 * (c + 1)]
        sg = lambda c: self.S[:, 512 * c:512 * (c + 1)]
        s_tags = self.layernorm(xg, sg, None, out_f32=True, xbf_off=self.QKBF,
                                src_tags=self.x_tags)
        if phase == 0:
            self.tap("A_S0", self.S[:, :])
        if lim < _STAGES.index("shift"):
            return self.ve.v
        for li in range(2):
            inst = 2 * phase + li
            g0 = max(self.free.values(), default=0)
            VE.wait_ge(self.pe.h, g0)
            h_tags = self.shift_cast(inst, 0, sg, self.HBF)
            if phase == 0 and li == 0:
                self.tap("A_h0", self.AR[:, self.HBF:self.HBF + 4096])
            if lim < _STAGES.index("qkvv"):
                return self.ve.v
            self.local_qkv(t, inst, wbf, v_only=(lim == _STAGES.index("qkvv")),
                           h_tags=h_tags)
            if lim < _STAGES.index("qkproc"):
                return self.ve.v
            k_tags = self.local_qk_process(t, inst, 1, self.io)
            self.local_norm_rows(t, inst)
            q_tags = self.local_qk_process(t, inst, 0, self.io)
            if lim < _STAGES.index("norm"):
                return self.ve.v
            if phase == 0 and li == 0:
                self.tap("A_qaug", self.AR[0:65, self.QAUG:self.QAUG + 4096])
                self.tap("A_kaug", self.AR[0:65, self.KAUG:self.KAUG + 4096])
                self.tap("A_vaug", self.AR[:, self.VAUGC:self.VAUGC + 2080])
            if lim < _STAGES.index("attn"):
                return self.ve.v
            self.local_attn(t, inst, q_tags, k_tags)
            if phase == 0 and li == 0:
                self.tap("A_oT", self.AR[:, self.OTL:self.OTL + 2048])
                self.tap("A_Pf", self.Pfull[:, :])
                self.tap("A_Pc", self.Pcur[:, :])
                self.tap("A_dbg1", self.scr1[:, :])
                self.tap("A_dbg2", self.statf[:, :])
            if lim < _STAGES.index("wo"):
                return self.ve.v
            self.local_wo(t, inst, wbf)
            if phase == 0 and li == 0:
                self.tap("A_S1", self.S[:, :])
            if lim < _STAGES.index("ffn"):
                return self.ve.v
            self.local_ffn(t, inst, wbf)
            if phase == 0 and li == 0:
                self.tap("A_AG", self.AR[:, self.AGOF:self.AGOF + 11264])
                self.tap("A_S2", self.S[:, :])
        self.x_tags = []
        for c in range(NCH):
            xc = self.X[:, 512 * c:512 * (c + 1)]
            self.x_tags.append(self.ve.inc(
                VE.scalar_tensor_tensor(xc, self.S[:, 512 * c:512 * (c + 1)],
                                        0.5, xc, ALU.mult, ALU.add)))
        if phase == 0:
            self.tap("A_X", self.X[:, :])
        return self.ve.v

    def emit_tile(self, t, io):
        self.io = io
        lim = _STAGES.index(STAGE_LIMIT)
        VE, SE, GE, SY = self.VE, self.SE, self.GE, self.SY
        if t > 0:
            SE.wait_ge(self.osem.h, self.osem.v)
        for c in range(NCH):
            self.xload.inc(SE.dma_start(self.X[:, 512 * c:512 * (c + 1)],
                                        io["x_in"][128 * c:128 * (c + 1), 512 * t:512 * (t + 1)]), 16)
        VE.wait_ge(self.xload.h, self.xload.v)
        self.TE.wait_ge(self.xload.h, self.xload.v)
        SE.wait_ge(self.xload.h, self.xload.v)
        self.x_tags = None
        if lim >= _STAGES.index("ln"):
            self.ffn_phase(t, 0, io["wbf"])
        if lim >= _STAGES.index("B"):
            self.phaseB(t, io["wbf"])
            if t == _TAP_AT_TILE:
                self.tap("B_X", self.X[:, :])
        if lim >= _STAGES.index("full"):
            self.ffn_phase(t, 1, io["wbf"])
        vfin = self.ve.v
        GE.wait_ge(self.ve.h, vfin)
        for c in range(NCH):
            self.osem.inc(GE.dma_start(io["out"][128 * c:128 * (c + 1), 512 * t:512 * (t + 1)],
                                       self.X[:, 512 * c:512 * (c + 1)]), 16)

    def finish(self):
        self.SY.wait_ge(self.osem.h, self.osem.v)
        self.GE.wait_ge(self.osem.h, self.osem.v)



def build_nc(n_tiles=None, taps=()):
    n_tiles = n_tiles or (DEV_NT or NT)
    nc = bass.Bass("TRN2", target_bir_lowering=False)
    io = {}
    io["x_in"] = nc.dram_tensor("x_in", [D, SEQ], F32, kind="ExternalInput")
    io["out"] = nc.dram_tensor("out", [D, SEQ], F32, kind="ExternalOutput")
    _wf_len = _WFLAT_LEN[0]
    io["wflat"] = nc.dram_tensor("wflat", [_wf_len], F32, kind="ExternalInput")
    io["wbf"] = nc.dram_tensor("wbf", [_wf_len], BF16)
    for n, s in (("t1q", [128, SEQ]), ("t2q", [128, SEQ]), ("t1k", [128, SEQ]),
                 ("t2k", [128, SEQ])):
        io[n] = nc.dram_tensor(n, s, F32, kind="ExternalInput")
    for n, s in (("rmat", [128, 128]), ("tric", [128, 128]), ("trip", [128, 128]),
                 ("selq", [128, 32]), ("selbc", [8, 512])):
        io[n] = nc.dram_tensor(n, s, BF16, kind="ExternalInput")
    for n, s in (("qs4", [128, 8]), ("b1a", [128, 44]), ("b1g", [128, 44]),
                 ("b2c", [128, 16]), ("onesrow", [1, 512]),
                 ("row_ones128", [1, 128]), ("row_neg128", [1, 128]),
                 ("row_ones64", [1, 64]), ("col_ones128", [128, 1]),
                 ("col_ones64", [64, 1]), ("gbrow_g", [1, 2048]), ("gbrow_b", [1, 2048])):
        io[n] = nc.dram_tensor(n, s, F32, kind="ExternalInput")
    tap_handles = {}
    for name in taps:
        shp = TAPS_DEF[name]
        tap_handles[name] = nc.dram_tensor("tap_" + name, list(shp), F32, kind="ExternalOutput")

    offs, marks = _OFFS[0], _MARKS[0]
    ctx = contextlib.ExitStack()
    mk = _MK(nc, ctx, offs, marks, n_tiles, tap_handles)
    with ctx, nc.Block() as block:
        @block.sync
        def _(sy):
            mk.emit_init(io)
            for t in range(n_tiles):
                mk.emit_tile(t, io)
            mk.finish()
    return nc


_WFLAT_LEN = [None]
_OFFS = [None]
_MARKS = [None]
_BUILD = {}


def _get_host_arrays(inputs):
    wflat, offs, marks = _flatten_weights(inputs)
    _WFLAT_LEN[0] = wflat.size
    _OFFS[0] = offs
    _MARKS[0] = marks
    tabs = _host_tables()
    sm = _host_smalls(inputs)
    gb = sm.pop("gbrows")
    consts = dict(
        wflat=wflat,
        t1q=tabs["t1q"], t2q=tabs["t2q"], t1k=tabs["t1k"], t2k=tabs["t2k"],
        rmat=np.asarray(tabs["rmat"]), tric=np.asarray(tabs["tric"]),
        trip=np.asarray(tabs["trip"]), selq=np.asarray(tabs["selq"]),
        selbc=np.asarray(tabs["selbc"]),
        qs4=sm["qs4"], b1a=sm["b1a"], b1g=sm["b1g"], b2c=sm["b2c"],
        onesrow=sm["onesrow"],
        row_ones128=np.ones((1, 128), np.float32),
        row_neg128=-np.ones((1, 128), np.float32),
        row_ones64=np.ones((1, 64), np.float32),
        col_ones128=np.ones((128, 1), np.float32),
        col_ones64=np.ones((64, 1), np.float32),
        gbrow_g=gb[0:1, :], gbrow_b=gb[1:2, :],
    )
    return consts


def kernel(**inputs):
    inputs = {k: np.asarray(v) for k, v in inputs.items()}
    consts = _get_host_arrays(inputs)
    key = ("full", DEV_NT)
    if key not in _BUILD:
        _BUILD[key] = build_nc()
    nc = _BUILD[key]
    x = inputs["L0_attn_hidden_states"]
    in_maps = []
    for b in range(8):
        m = dict(consts)
        m["x_in"] = np.ascontiguousarray(x[b].T.astype(np.float32))
        in_maps.append(m)
    res = run_bass_kernel_spmd(nc, in_maps, core_ids=list(range(8)))
    outs = [np.asarray(r["out"]).T for r in res.results]
    return np.stack(outs, axis=0).astype(np.float32)
